# version 02
# baseline (speedup 1.0000x reference)
import numpy as np
import ml_dtypes
from contextlib import ExitStack

import concourse.bass as bass
import concourse.mybir as mybir
from concourse.bass_utils import run_bass_kernel_spmd

F32 = mybir.dt.float32
BF16 = mybir.dt.bfloat16
AF = mybir.ActivationFunctionType
ALU = mybir.AluOpType
AX = mybir.AxisListType

NCORES = 8
TC = 1024
D = 2048
NCH = 16
BFNP = ml_dtypes.bfloat16
ENGINES = ("sync", "gpsimd", "vector", "scalar", "tensor")


class Step:
    __slots__ = ("idx", "engine", "fn", "deps", "kind", "label", "_ndma")

    def __init__(self, idx, engine, fn, deps, kind, label):
        self.idx, self.engine, self.fn = idx, engine, fn
        self.deps, self.kind, self.label = deps, kind, label
        self._ndma = 0


class P:

    def __init__(self):
        self.nc = bass.Bass("TRN2", target_bir_lowering=False, debug=True)
        self.es = ExitStack()
        self.steps = []
        self._names = set()

    def din(self, name, shape, dtype=F32):
        return self.nc.dram_tensor(name, list(shape), dtype, kind="ExternalInput").ap()

    def dout(self, name, shape, dtype=F32):
        return self.nc.dram_tensor(name, list(shape), dtype, kind="ExternalOutput").ap()

    def dint(self, name, shape, dtype=F32):
        return self.nc.dram_tensor(name, list(shape), dtype)

    def sb(self, name, shape, dtype=F32):
        assert name not in self._names, name
        self._names.add(name)
        return self.es.enter_context(self.nc.sbuf_tensor(name, list(shape), dtype))

    def ps(self, name, shape, dtype=F32):
        assert name not in self._names, name
        self._names.add(name)
        return self.es.enter_context(self.nc.psum_tensor(name, list(shape), dtype))

    def cstep(self, engine, fn, deps="all", label=""):
        s = Step(len(self.steps), engine, fn, deps, "c", label)
        self.steps.append(s)
        return s

    def dstep(self, engine, fn, deps="all", label="", n=None):
        s = Step(len(self.steps), engine, fn, deps, "d", label)
        s._ndma = 1 if n is None else n
        self.steps.append(s)
        return s

    def xstep(self, engine, fn, deps="all", label="", n=1):
        s = Step(len(self.steps), engine, fn, deps, "x", label)
        s._ndma = n
        self.steps.append(s)
        return s

    def build(self):
        used = {(st.engine, st.kind) for st in self.steps if st.kind != "x"}
        chans = [(e, k) for e in ENGINES for k in ("c", "d")
                 if (e, k) in used]
        for st in self.steps:
            if st.kind == "x":
                chans.append((st.engine, f"x{st.idx}"))
        sems = {ch: self.es.enter_context(
            self.nc.semaphore(f"sem_{ch[0]}_{ch[1]}")) for ch in chans}
        blk = self.es.enter_context(self.nc.Block())

        counts = {ch: 0 for ch in chans}
        after = []
        waits = []
        for st in self.steps:
            mych = (st.engine, st.kind if st.kind != "x" else f"x{st.idx}")
            if st.deps == "all":
                tg = {ch: counts[ch] for ch in chans if counts[ch] > 0}
            else:
                tg = {}
                for d in st.deps:
                    for ch in chans:
                        v = after[d.idx][ch]
                        if v > tg.get(ch, 0):
                            tg[ch] = v
                for ch in chans:
                    if ch[0] == st.engine and counts.get(ch, 0) > tg.get(ch, 0):
                        tg[ch] = counts[ch]
            waits.append(tg)
            if st.kind == "c":
                counts[mych] += 1
            elif st.kind == "x":
                counts[mych] += 1
            else:
                assert st._ndma is not None, f"step {st.idx} {st.label}: dstep needs n="
                counts[mych] += 16 * st._ndma
            after.append(dict(counts))

        by_eng = {e: [st for st in self.steps if st.engine == e] for e in ENGINES}
        for e in ENGINES:
            if not by_eng[e] and e != "sync":
                continue

            def section(eng, e=e):
                for st in by_eng[e]:
                    for ch, v in waits[st.idx].items():
                        eng.wait_ge(sems[ch], v)
                    if st.kind == "c":
                        last = st.fn(eng)
                        assert last is not None, (
                            f"step {st.idx} {st.label}: fn must return last instr")
                        last.then_inc(sems[(e, "c")], 1)
                    elif st.kind == "x":
                        instrs = st.fn(eng)
                        assert len(instrs) == 1, "one collective per xstep"
                        instrs[0].then_inc(sems[(e, f"x{st.idx}")])
                    else:
                        instrs = st.fn(eng)
                        assert len(instrs) == st._ndma, (
                            f"step {st.idx} {st.label}: declared n={st._ndma} "
                            f"!= emitted {len(instrs)}")
                        for ins in instrs:
                            ins.then_inc(sems[(e, "d")], 16)
                if e == "sync":
                    for ch in chans:
                        if counts[ch] > 0:
                            eng.wait_ge(sems[ch], counts[ch])
            getattr(blk, e)(section)

        return self.nc

    def close(self):
        self.es.close()


def chunkc(ap, c, width=TC):
    return ap[:, c * width:(c + 1) * width]




def _prep_host(inputs):
    g = {}
    pc = [dict() for _ in range(NCORES)]

    x = np.ascontiguousarray(np.asarray(inputs["L0_attn_x"], np.float32)[0].T)
    for i in range(NCORES):
        pc[i]["x"] = np.ascontiguousarray(x[:, i * TC:(i + 1) * TC])

    A = lambda k: np.ascontiguousarray(np.asarray(inputs[k], np.float32))
    g["igw"] = A("L0_attn_input_gate_weight")
    g["rgw"] = A("L0_attn_recurrent_gate_weight")
    g["igb"] = np.ascontiguousarray(A("L0_attn_input_gate_bias").T)
    g["rgb"] = np.ascontiguousarray(A("L0_attn_recurrent_gate_bias").T)
    g["rp"] = np.ascontiguousarray(A("L0_attn_recurrent_param").reshape(16, 128).T)

    g["ln1"] = np.ascontiguousarray(A("L0_ffn_ln1_w").reshape(16, 128).T)
    g["ln2"] = np.ascontiguousarray(A("L0_ffn_ln2_w").reshape(16, 128).T)
    g["qnw"] = np.ascontiguousarray(A("L0_ffn_qnorm_w").reshape(16, 128).T)
    g["knw"] = np.ascontiguousarray(A("L0_ffn_knorm_w").reshape(4, 128).T)
    g["wq"] = A("L0_ffn_wq")
    g["wk"] = A("L0_ffn_wk")
    g["wv"] = A("L0_ffn_wv")
    g["wo"] = A("L0_ffn_wo")
    g["rw0"] = A("L0_ffn_router_w")
    g["rb0t"] = np.ascontiguousarray(
        np.tile(A("L0_ffn_router_bias").reshape(1, 8), (128, 8)))
    g["wg0"] = A("L0_ffn_wg")
    g["wu0"] = A("L0_ffn_wu")
    g["wd0"] = A("L0_ffn_wd")

    g["WuT"] = np.ascontiguousarray(A("L1_attn_Wu").T)
    g["WvT"] = np.ascontiguousarray(A("L1_attn_Wv").T)
    g["Wg1T"] = np.ascontiguousarray(A("L1_attn_Wg1").T)
    g["Wg2T"] = np.ascontiguousarray(A("L1_attn_Wg2").T)
    g["bu"] = np.ascontiguousarray(A("L1_attn_bu").reshape(16, 128).T)
    g["bv"] = np.ascontiguousarray(A("L1_attn_bv").reshape(16, 128).T)
    g["bg1"] = np.ascontiguousarray(A("L1_attn_bg1").reshape(4, 128).T)
    g["bg2"] = np.ascontiguousarray(A("L1_attn_bg2").reshape(16, 128).T)
    g["cw"] = np.ascontiguousarray(
        A("L1_attn_conv_w")[:, 0, :].reshape(16, 128, 3).transpose(2, 1, 0))
    g["cb"] = np.ascontiguousarray(A("L1_attn_conv_b").reshape(16, 128).T)

    g["Wr1"] = A("L1_ffn_Wr")
    g["wg1"] = A("L1_ffn_Wg")
    g["wu1"] = A("L1_ffn_Wu")
    g["wd1"] = A("L1_ffn_Wd")
    g["Sg"] = A("L1_ffn_Sg")
    g["Su"] = A("L1_ffn_Su")
    g["Sd"] = A("L1_ffn_Sd")
    g["Wsg"] = np.ascontiguousarray(
        A("L1_ffn_Wsg").reshape(16, 128).T)

    swapm = np.zeros((64, 64), np.float32)
    for m in range(64):
        swapm[(m + 32) % 64, m] = 1.0
    g["swapm"] = swapm.astype(BFNP)

    theta = 1e6
    inv = 1.0 / theta ** (np.arange(0, 64, 2, dtype=np.float32) / 64)
    for i in range(NCORES):
        pos = (i * TC + np.arange(TC)).astype(np.float32)
        ang = pos[:, None] * inv[None, :]
        emb = np.concatenate([ang, ang], 1)
        pc[i]["cosr"] = np.ascontiguousarray(np.cos(emb).T.astype(np.float32))
        sint = np.sin(emb).T.astype(np.float32)
        sinsgn = sint.copy()
        sinsgn[0:32] = -sint[0:32]
        pc[i]["sinsgn"] = np.ascontiguousarray(sinsgn)
        pc[i]["qpos"] = pos.reshape(1, TC).copy()
        sel = np.zeros((128, 8), np.float32)
        sel[:, i] = 1.0
        pc[i]["sel"] = sel
        selp = np.zeros((128, 8), np.float32)
        if i > 0:
            selp[:, i - 1] = 1.0
        pc[i]["selp"] = selp
        seln = np.zeros((128, 8), np.float32)
        if i < NCORES - 1:
            seln[:, i + 1] = 1.0
        pc[i]["seln"] = seln
        fm = np.zeros((128, 1), np.float32)
        if i == 0:
            fm[:, 0] = 1.0
        pc[i]["first_mask"] = fm
    return g, pc




def feat_store(p, eng, dram_ap, sbuf, dtype_elems=TC):
    return eng.dma_start(
        out=dram_ap.rearrange("(c p) t -> p c t", p=128),
        in_=sbuf[:, :].rearrange("p (c t) -> p c t", t=dtype_elems))


def feat_load(p, eng, sbuf, dram_ap, dtype_elems=TC):
    return eng.dma_start(
        out=sbuf[:, :].rearrange("p (c t) -> p c t", t=dtype_elems),
        in_=dram_ap.rearrange("(c p) t -> p c t", p=128))


def emit_ln(p, X, XNB, SQB, XB, MB, RB, PSQ, PSS, PSB, ONEBF, ONE1,
            eps=1e-5, gain=None, out_f32_inplace=True, nch=NCH, width=TC):
    dim = 128 * nch
    for th in range(2):
        def sq(s, th=th):
            last = None
            for c in range(nch):
                last = s.activation(SQB[:, c * 512:(c + 1) * 512],
                                    X[:, c * width + th * 512: c * width + th * 512 + 512],
                                    AF.Square)
            return last
        p.cstep("scalar", sq, label="ln_sq")
        if gain is None:
            def cp(v, th=th):
                last = None
                for c in range(nch):
                    last = v.tensor_copy(XB[:, c * 512:(c + 1) * 512],
                                         X[:, c * width + th * 512: c * width + th * 512 + 512])
                return last
            p.cstep("vector", cp, label="ln_cp")

        def mm(t, th=th):
            last = None
            for c in range(nch):
                last = t.matmul(PSQ[:], ONEBF[:, 0:1], SQB[:, c * 512:(c + 1) * 512],
                                start=(c == 0), stop=(c == nch - 1))
            if gain is None:
                for c in range(nch):
                    last = t.matmul(PSS[:], ONEBF[:, 0:1], XB[:, c * 512:(c + 1) * 512],
                                    start=(c == 0), stop=(c == nch - 1))
            return last
        p.cstep("tensor", mm, label="ln_mm")

        if gain is None:
            def st1(v, th=th):
                return v.tensor_scalar(MB[0:1, 0:512], PSS[:], 1.0 / dim, None, ALU.mult)
            p.cstep("vector", st1, label="ln_mean")

            def st2(s, th=th):
                return s.activation(RB[0:1, 0:512], MB[0:1, 0:512], AF.Square)
            p.cstep("scalar", st2, label="ln_m2")

            def st3(v, th=th):
                return v.scalar_tensor_tensor(RB[0:1, 0:512], PSQ[:], 1.0 / dim,
                                              RB[0:1, 0:512], ALU.mult, ALU.subtract)
            p.cstep("vector", st3, label="ln_var")
            p.cstep("vector", lambda v: v.tensor_scalar(
                RB[0:1, 0:512], RB[0:1, 0:512], eps, None, ALU.add), label="ln_eps")
        else:
            def st3(v, th=th):
                return v.tensor_scalar(RB[0:1, 0:512], PSQ[:], 1.0 / dim, eps,
                                       ALU.mult, ALU.add)
            p.cstep("vector", st3, label="ln_var")

        def st4(s, th=th):
            return s.activation(RB[0:1, 0:512], RB[0:1, 0:512], AF.Sqrt)
        p.cstep("scalar", st4, label="ln_sqrt")

        def st5(v, th=th):
            return v.reciprocal(RB[0:1, 0:512], RB[0:1, 0:512])
        p.cstep("vector", st5, label="ln_recip")

        def bc1(t, th=th):
            return t.matmul(PSB[:], ONE1[0:1, :], RB[0:1, 0:512], start=True, stop=True)
        p.cstep("tensor", bc1, label="ln_bc_r")

        def cpr(v, th=th):
            return v.tensor_copy(RB[:, 0:512], PSB[:])
        p.cstep("vector", cpr, label="ln_cp_r")

        if gain is None:
            def bc2(t, th=th):
                return t.matmul(PSB[:], ONE1[0:1, :], MB[0:1, 0:512], start=True, stop=True)
            p.cstep("tensor", bc2, label="ln_bc_m")

            def cpm(v, th=th):
                return v.tensor_copy(MB[:, 0:512], PSB[:])
            p.cstep("vector", cpm, label="ln_cp_m")

        def segf(c, th):
            return X[:, c * width + th * 512: c * width + th * 512 + 512]

        def segb(c, th):
            return XNB[:, c * width + th * 512: c * width + th * 512 + 512]
        if gain is None:
            def n1(v, th=th):
                last = None
                for c in range(nch):
                    last = v.scalar_tensor_tensor(segf(c, th), segf(c, th), 1.0,
                                                  MB[:, 0:512], ALU.mult, ALU.subtract)
                return last
            p.cstep("vector", n1, label="ln_n1")

            def n2(v, th=th):
                last = None
                for c in range(nch):
                    last = v.tensor_tensor(segf(c, th), segf(c, th), RB[:, 0:512], ALU.mult)
                return last
            p.cstep("vector", n2, label="ln_n2")

            def n3(v, th=th):
                last = None
                for c in range(nch):
                    last = v.tensor_copy(segb(c, th), segf(c, th))
                return last
            p.cstep("vector", n3, label="ln_n3")
        else:
            def n1(v, th=th):
                last = None
                for c in range(nch):
                    last = v.scalar_tensor_tensor(segb(c, th), segf(c, th),
                                                  gain[:, c:c + 1], RB[:, 0:512],
                                                  ALU.mult, ALU.mult)
                return last
            p.cstep("vector", n1, label="ln_rms")


def emit_softplus8(p, TAB, SCR):
    p.cstep("scalar", lambda s: (s.activation(TAB[:, 48:64], TAB[:, 32:48], AF.Abs),
                                 s.activation(SCR[:, 32:48], TAB[:, 32:48], AF.Relu))[-1])
    p.cstep("scalar", lambda s: s.activation(SCR[:, 0:16], TAB[:, 48:64], AF.Exp, scale=-1.0))
    p.cstep("scalar", lambda s: s.activation(SCR[:, 16:32], SCR[:, 0:16], AF.Ln, bias=1.0))
    p.cstep("vector", lambda v: v.scalar_tensor_tensor(
        TAB[:, 48:64], SCR[:, 16:32], 1.0, SCR[:, 32:48], ALU.mult, ALU.add))
    p.cstep("vector", lambda v: v.tensor_scalar(
        TAB[:, 48:64], TAB[:, 48:64], -8.0, None, ALU.mult))


def emit_scan_branch(p, X, XNB, ALOC, YLOC, SCR, ZEROS, TAB, SEL, FMASK,
                     WGT, AGG, AGALL, CAR, SST, PG0, PG1,
                     igw_ap, rgw_ap, agg_in, agg_out):
    s_gx, s_ga = 0, 1024
    s_a, s_m, s_b = 2048, 3072, 1024
    for h in range(16):
        p.dstep("gpsimd", lambda g, h=h: [
            g.dma_start(out=WGT[:, :], in_=igw_ap[h])], label=f"gw{h}")

        def mm1(t, h=h):
            t.matmul(PG0[:], WGT[:, :], XNB[:, h * TC:h * TC + 512],
                     start=True, stop=True)
            return t.matmul(PG1[:], WGT[:, :], XNB[:, h * TC + 512:h * TC + 1024],
                            start=True, stop=True)
        p.cstep("tensor", mm1)

        def ev1(s, h=h):
            s.activation(SCR[:, s_gx:s_gx + 512], PG0[:], AF.Sigmoid,
                         bias=TAB[:, h:h + 1])
            return s.activation(SCR[:, s_gx + 512:s_gx + 1024], PG1[:], AF.Sigmoid,
                                bias=TAB[:, h:h + 1])
        p.cstep("scalar", ev1)

        p.dstep("gpsimd", lambda g, h=h: [
            g.dma_start(out=WGT[:, :], in_=rgw_ap[h])], label=f"rw{h}")

        def mm2(t, h=h):
            t.matmul(PG0[:], WGT[:, :], XNB[:, h * TC:h * TC + 512],
                     start=True, stop=True)
            return t.matmul(PG1[:], WGT[:, :], XNB[:, h * TC + 512:h * TC + 1024],
                            start=True, stop=True)
        p.cstep("tensor", mm2)

        def ev2(s, h=h):
            s.activation(SCR[:, s_ga:s_ga + 512], PG0[:], AF.Sigmoid,
                         bias=TAB[:, 16 + h:17 + h])
            return s.activation(SCR[:, s_ga + 512:s_ga + 1024], PG1[:], AF.Sigmoid,
                                bias=TAB[:, 16 + h:17 + h])
        p.cstep("scalar", ev2)

        def e1(v, h=h):
            return v.tensor_scalar(SCR[:, s_a:s_a + 1024], SCR[:, s_ga:s_ga + 1024],
                                   TAB[:, 48 + h:49 + h], None, ALU.mult)
        p.cstep("vector", e1)

        def e2(s, h=h):
            return s.activation(SCR[:, s_a:s_a + 1024], SCR[:, s_a:s_a + 1024], AF.Exp)
        p.cstep("scalar", e2)

        def e3(s, h=h):
            return s.activation(SCR[:, s_m:s_m + 1024], SCR[:, s_a:s_a + 1024], AF.Square)
        p.cstep("scalar", e3)

        def e4(v, h=h):
            return v.tensor_scalar(SCR[:, s_m:s_m + 1024], SCR[:, s_m:s_m + 1024],
                                   -1.0, 1.0, ALU.mult, ALU.add)
        p.cstep("vector", e4)

        def e5(s, h=h):
            return s.activation(SCR[:, s_m:s_m + 1024], SCR[:, s_m:s_m + 1024], AF.Sqrt)
        p.cstep("scalar", e5)

        p.cstep("vector", lambda v, h=h: v.tensor_scalar(
            SCR[:, s_b:s_b + 1], FMASK[:, 0:1], -1.0, 1.0, ALU.mult, ALU.add))
        p.cstep("vector", lambda v, h=h: v.scalar_tensor_tensor(
            SCR[:, s_m:s_m + 1], SCR[:, s_m:s_m + 1], SCR[:, s_b:s_b + 1],
            FMASK[:, 0:1], ALU.mult, ALU.add))
        p.cstep("vector", lambda v, h=h: v.tensor_tensor(
            SCR[:, s_b:s_b + 1024], X[:, h * TC:(h + 1) * TC],
            SCR[:, s_gx:s_gx + 1024], ALU.mult))
        p.cstep("vector", lambda v, h=h: v.tensor_tensor(
            SCR[:, s_b:s_b + 1024], SCR[:, s_b:s_b + 1024],
            SCR[:, s_m:s_m + 1024], ALU.mult))

        def e7(v, h=h):
            v.tensor_tensor_scan(ALOC[:, h * TC:(h + 1) * TC],
                                 SCR[:, s_a:s_a + 1024], ZEROS[:, 0:1024],
                                 1.0, ALU.mult, ALU.add)
            return v.tensor_tensor_scan(
                YLOC[:, h * TC:(h + 1) * TC], SCR[:, s_a:s_a + 1024],
                SCR[:, s_b:s_b + 1024], 0.0, ALU.mult, ALU.add)
        p.cstep("vector", e7)

        def e8(v, h=h):
            v.tensor_copy(AGG[:, h:h + 1], ALOC[:, h * TC + 1023:h * TC + 1024])
            return v.tensor_copy(AGG[:, 16 + h:17 + h],
                                 YLOC[:, h * TC + 1023:h * TC + 1024])
        p.cstep("vector", e8)

    p.dstep("gpsimd", lambda g: [g.dma_start(out=agg_in[:, :], in_=AGG[:, :])],
            label="agg_out_dma")
    import os
    if os.environ.get("NO_CC"):
        p.dstep("gpsimd", lambda g: [g.dma_start(
            out=agg_out[0:128, :], in_=agg_in[:, :])], label="agg_fake")
    else:
        p.xstep("gpsimd", lambda g: [g.collective_compute(
            "AllGather", ALU.bypass, replica_groups=[list(range(NCORES))],
            ins=[agg_in[:, :]], outs=[agg_out[:, :]])], label="agg_ag")
    p.dstep("gpsimd", lambda g: [g.dma_start(
        out=AGALL[:, :].rearrange("p (r c) -> p r c", c=32),
        in_=agg_out.ap().rearrange("(r p) c -> p r c", p=128))],
        label="agg_in_dma")

    def cm(v):
        v.memset(CAR[:, :], 0.0)
        return v.memset(SST[:, :], 0.0)
    p.cstep("vector", cm)
    for j in range(NCORES):
        p.cstep("vector", lambda v, j=j: v.scalar_tensor_tensor(
            CAR[:, :], SST[:, :], SEL[:, j:j + 1], CAR[:, :], ALU.mult, ALU.add))
        p.cstep("vector", lambda v, j=j: v.tensor_tensor(
            SST[:, :], SST[:, :], AGALL[:, j * 32:j * 32 + 16], ALU.mult))
        p.cstep("vector", lambda v, j=j: v.tensor_tensor(
            SST[:, :], SST[:, :], AGALL[:, j * 32 + 16:j * 32 + 32], ALU.add))

    def corr(v):
        last = None
        for h in range(16):
            last = v.scalar_tensor_tensor(
                YLOC[:, h * TC:(h + 1) * TC], ALOC[:, h * TC:(h + 1) * TC],
                CAR[:, h:h + 1], YLOC[:, h * TC:(h + 1) * TC], ALU.mult, ALU.add)
        return last
    p.cstep("vector", corr, label="carry_corr")


def build_launch_A(dbg=False):
    p = P()
    x_in = p.din("x", [D, TC])
    igw_ap = p.din("igw", [16, 128, 128])
    rgw_ap = p.din("rgw", [16, 128, 128])
    igb_ap = p.din("igb", [128, 16])
    rgb_ap = p.din("rgb", [128, 16])
    rp_ap = p.din("rp", [128, 16])
    sel_ap = p.din("sel", [128, 8])
    fm_ap = p.din("first_mask", [128, 1])
    agg_in = p.dint("agg_in", [128, 32])
    agg_out = p.dint("agg_out", [128 * NCORES, 32])
    xn_out = p.dout("xn", [D, TC])
    s0_out = p.dout("s0", [D, TC], BF16)

    X = p.sb("X", [128, NCH * TC])
    XNB = p.sb("XNB", [128, NCH * TC], BF16)
    ALOC = p.sb("ALOC", [128, NCH * TC], BF16)
    YLOC = p.sb("YLOC", [128, NCH * TC], BF16)
    SCR = p.sb("SCR", [128, 5 * 1024])
    ZEROS = p.sb("ZEROS", [128, 1024])
    MB = p.sb("MB", [128, 512])
    RB = p.sb("RB", [128, 512])
    TAB = p.sb("TAB", [128, 64])
    SEL = p.sb("SEL", [128, 8])
    FMASK = p.sb("FMASK", [128, 1])
    ONEBF = p.sb("ONEBF", [128, 1], BF16)
    ONE1 = p.sb("ONE1", [1, 128])
    WGT = p.sb("WGT", [128, 128], BF16)
    AGG = p.sb("AGG", [128, 32])
    AGALL = p.sb("AGALL", [128, 256])
    CAR = p.sb("CAR", [128, 16])
    SST = p.sb("SST", [128, 16])
    PSQ = p.ps("PSQ", [1, 512])
    PSS = p.ps("PSS", [1, 512])
    PSB = p.ps("PSB", [128, 512])
    PG0 = p.ps("PG0", [128, 512])
    PG1 = p.ps("PG1", [128, 512])

    p.dstep("sync", lambda e: [
        feat_load(p, e, X, x_in),
        e.dma_start(out=TAB[:, 0:16], in_=igb_ap),
        e.dma_start(out=TAB[:, 16:32], in_=rgb_ap),
        e.dma_start(out=TAB[:, 32:48], in_=rp_ap),
        e.dma_start(out=SEL[:, :], in_=sel_ap),
        e.dma_start(out=FMASK[:, :], in_=fm_ap),
    ], label="load", n=6)

    def init(v):
        v.memset(ZEROS[:, :], 0.0)
        v.memset(ONEBF[:, :], 1.0)
        return v.memset(ONE1[:, :], 1.0)
    p.cstep("vector", init, label="init")

    import os as _os
    emit_softplus8(p, TAB, SCR)
    if not _os.environ.get("NO_LN"):
        emit_ln(p, X, XNB, ALOC[:, 0:NCH * 512], YLOC[:, 0:NCH * 512],
                MB, RB, PSQ, PSS, PSB, ONEBF, ONE1)
    if _os.environ.get("NO_GATES"):
        def fake(v):
            return v.tensor_copy(AGG[:, 0:32], AGALL[:, 0:32])
        p.cstep("vector", fake)
        p.dstep("gpsimd", lambda g: [g.dma_start(out=agg_in[:, :], in_=AGG[:, :])])
        p.xstep("gpsimd", lambda g: [g.collective_compute(
            "AllGather", ALU.bypass, replica_groups=[list(range(NCORES))],
            ins=[agg_in.ap().opt()], outs=[agg_out.ap().opt()])], label="agg_ag")
        p.dstep("gpsimd", lambda g: [g.dma_start(
            out=AGALL[:, :].rearrange("p (r c) -> p r c", c=32),
            in_=agg_out.ap().rearrange("(r p) c -> p r c", p=128))])
    else:
        emit_scan_branch(p, X, XNB, ALOC, YLOC, SCR, ZEROS, TAB, SEL, FMASK,
                         WGT, AGG, AGALL, CAR, SST, PG0, PG1,
                         igw_ap, rgw_ap, agg_in, agg_out)

    import os as _os2
    if _os2.environ.get("NO_BF16STORE"):
        p.dstep("sync", lambda e: [feat_store(p, e, xn_out, X)], label="store")
        p.dstep("sync", lambda e: [e.dma_start(out=s0_out.rearrange(
            "(c p) t -> p c t", p=128)[:, 0:1, :],
            in_=YLOC[:, 0:TC].rearrange("p (c t) -> p c t", t=TC))], label="store2")
    else:
        p.dstep("sync", lambda e: [
            feat_store(p, e, xn_out, X),
            feat_store(p, e, s0_out, YLOC),
        ], label="store", n=2)

    nc = p.build()
    return p, nc




def emit_proj(p, w_ap, rhs_bf, Din, Dout, WSLAB, PS, evac_fn, rhs_col=0,
              t_width=TC, og_width=512):
    nkt = Din // 128
    nog = Dout // og_width
    ocpg = og_width // 128
    nth = t_width // 512
    assert ocpg * nth <= 8
    for og in range(nog):
        for kt in range(nkt):
            p.dstep("gpsimd", lambda g, og=og, kt=kt: [g.dma_start(
                out=WSLAB[:, :],
                in_=w_ap[kt * 128:(kt + 1) * 128,
                         og * og_width:(og + 1) * og_width])],
                label=f"pw{og}_{kt}")

            def mm(t, og=og, kt=kt):
                last = None
                for j in range(ocpg):
                    for th in range(nth):
                        last = t.matmul(
                            PS[:, (j * nth + th) * 512:(j * nth + th) * 512 + 512],
                            WSLAB[:, j * 128:(j + 1) * 128],
                            rhs_bf[:, rhs_col + kt * t_width + th * 512:
                                   rhs_col + kt * t_width + th * 512 + 512],
                            start=(kt == 0), stop=(kt == nkt - 1))
                return last
            p.cstep("tensor", mm, label=f"pm{og}")
        banks = [(og * ocpg + j, th,
                  PS[:, (j * nth + th) * 512:(j * nth + th) * 512 + 512])
                 for j in range(ocpg) for th in range(nth)]
        evac_fn(og, banks)


def emit_rownorm_scale(p, QB, gaintab, nch_q, PS, SQB, RROW, RBC, ONEBF, ONE1,
                       eps, tag, extra_scale=1.0):
    dim = nch_q * 128
    for th in range(2):
        def sq(s, th=th):
            last = None
            for c in range(nch_q):
                last = s.activation(SQB[:, c * 512:(c + 1) * 512],
                                    QB[:, c * TC + th * 512:c * TC + th * 512 + 512],
                                    AF.Square)
            return last
        p.cstep("scalar", sq, label=f"{tag}sq")

        def mmq(t, th=th):
            last = None
            for c in range(nch_q):
                last = t.matmul(PS[0:1, 3584:4096], ONEBF[:, 0:1],
                                SQB[:, c * 512:(c + 1) * 512],
                                start=(c == 0), stop=(c == nch_q - 1))
            return last
        p.cstep("tensor", mmq, label=f"{tag}mm")
        p.cstep("vector", lambda v: v.tensor_scalar(
            RROW[0:1, 0:512], PS[0:1, 3584:4096], 1.0 / dim, eps, ALU.mult, ALU.add))
        p.cstep("scalar", lambda s: s.activation(
            RROW[0:1, 0:512], RROW[0:1, 0:512], AF.Sqrt))
        p.cstep("vector", lambda v: v.reciprocal(RROW[0:1, 0:512], RROW[0:1, 0:512]))
        if extra_scale != 1.0:
            p.cstep("vector", lambda v: v.tensor_scalar(
                RROW[0:1, 0:512], RROW[0:1, 0:512], extra_scale, None, ALU.mult))
        p.cstep("tensor", lambda t: t.matmul(
            PS[:, 3584:4096], ONE1[0:1, :], RROW[0:1, 0:512], start=True, stop=True))
        p.cstep("vector", lambda v: v.tensor_copy(RBC[:, 0:512], PS[:, 3584:4096]))

        def scl(v, th=th):
            last = None
            for c in range(nch_q):
                seg = QB[:, c * TC + th * 512:c * TC + th * 512 + 512]
                last = v.scalar_tensor_tensor(seg, seg, gaintab[:, c:c + 1],
                                              RBC[:, 0:512], ALU.mult, ALU.mult)
            return last
        p.cstep("vector", scl, label=f"{tag}scl")


def emit_rope(p, QB, nch_q, COSR, SINSGN, SWAPM, PS, SCR):
    for h in range(nch_q):
        def mmswap(t, h=h):
            t.matmul(PS[0:64, 0:512], SWAPM[:, :],
                     QB[0:64, h * TC:h * TC + 512], start=True, stop=True)
            return t.matmul(PS[0:64, 512:1024], SWAPM[:, :],
                            QB[0:64, h * TC + 512:h * TC + 1024],
                            start=True, stop=True)
        p.cstep("tensor", mmswap)
        p.cstep("vector", lambda v, h=h: v.tensor_copy(
            SCR[0:64, 0:1024], PS[0:64, 0:1024]))
        p.cstep("vector", lambda v, h=h: v.tensor_tensor(
            SCR[0:64, 1024:2048], QB[0:64, h * TC:(h) * TC + 1024],
            COSR[0:64, :], ALU.mult))
        p.cstep("vector", lambda v, h=h: v.tensor_tensor(
            SCR[0:64, 0:1024], SCR[0:64, 0:1024], SINSGN[0:64, :], ALU.mult))
        p.cstep("vector", lambda v, h=h: v.tensor_tensor(
            QB[0:64, h * TC:h * TC + 1024], SCR[0:64, 1024:2048],
            SCR[0:64, 0:1024], ALU.add))


def build_launch_B1(g):
    p = P()
    xn_in = p.din("xn", [D, TC])
    wq_ap = p.din("wq", [D, D])
    wk_ap = p.din("wk", [D, 512])
    wv_ap = p.din("wv", [D, 512])
    ln1_ap = p.din("ln1", [128, 16])
    qnw_ap = p.din("qnw", [128, 16])
    knw_ap = p.din("knw", [128, 4])
    cos_ap = p.din("cosr", [64, TC])
    sin_ap = p.din("sinsgn", [64, TC])
    swap_ap = p.din("swapm", [64, 64], BF16)
    kag_in = p.dint("kag_in", [512, TC], BF16)
    kag_out = p.dint("kag_out", [512 * NCORES, TC], BF16)
    vag_in = p.dint("vag_in", [TC, 512], BF16)
    vag_out = p.dint("vag_out", [TC * NCORES, 512], BF16)
    q_out = p.dout("q", [D, TC], BF16)
    kf_out = p.dout("kfull", [512 * NCORES, TC], BF16)
    vf_out = p.dout("vfull", [TC * NCORES, 512], BF16)

    XN = p.sb("XN", [128, NCH * TC])
    HB = p.sb("HB", [128, NCH * TC], BF16)
    QB = p.sb("QB", [128, NCH * TC], BF16)
    KL = p.sb("KL", [128, 4 * TC], BF16)
    VT = p.sb("VT", [128, 8 * 512], BF16)
    SQB = p.sb("SQB", [128, NCH * 512], BF16)
    SCR = p.sb("SCR", [128, 3072])
    WSLAB = p.sb("WSLAB", [128, 512], BF16)
    TAB = p.sb("TAB", [128, 40])
    COSR = p.sb("COSR", [64, TC])
    SINSGN = p.sb("SINSGN", [64, TC])
    SWAPM = p.sb("SWAPM", [64, 64], BF16)
    RROW = p.sb("RROW", [1, 512])
    RBC = p.sb("RBC", [128, 512])
    ONEBF = p.sb("ONEBF", [128, 1], BF16)
    ONE1 = p.sb("ONE1", [1, 128])
    QNWS = p.sb("QNWS", [128, 16])
    PS = p.ps("PS", [128, 4096])

    p.dstep("sync", lambda e: [
        feat_load(p, e, XN, xn_in),
        e.dma_start(out=TAB[:, 0:16], in_=ln1_ap),
        e.dma_start(out=QNWS[:, 0:16], in_=qnw_ap),
        e.dma_start(out=TAB[:, 32:36], in_=knw_ap),
        e.dma_start(out=COSR[:, :], in_=cos_ap),
        e.dma_start(out=SINSGN[:, :], in_=sin_ap),
        e.dma_start(out=SWAPM[:, :], in_=swap_ap),
    ], label="load", n=7)

    def init(v):
        v.memset(ONEBF[:, :], 1.0)
        return v.memset(ONE1[:, :], 1.0)
    p.cstep("vector", init)
    p.cstep("vector", lambda v: v.tensor_scalar(
        QNWS[:, :], QNWS[:, :], float(1.0 / np.sqrt(128.0)), None, ALU.mult))

    emit_ln(p, XN, HB, SQB[:, 0:NCH * 512], None, None, RBC,
            PS[0:1, 3584:4096], None, PS[:, 3584:4096], ONEBF, ONE1,
            eps=1e-6, gain=TAB[:, 0:16])

    def ev_copy(dst, col_of):
        def f(og, banks):
            def e(s, banks=banks):
                last = None
                for (oc, th, pv) in banks:
                    last = s.copy(dst[:, col_of(oc, th):col_of(oc, th) + 512], pv)
                return last
            p.cstep("scalar", e, label="pev")
        return f

    emit_proj(p, wq_ap, HB, D, D, WSLAB, PS,
              ev_copy(QB, lambda oc, th: oc * TC + th * 512))
    emit_proj(p, wk_ap, HB, D, 512, WSLAB, PS,
              ev_copy(KL, lambda oc, th: oc * TC + th * 512))

    emit_rownorm_scale(p, QB, QNWS[:, 0:16], 16, PS, SQB[:, 0:NCH * 512],
                       RROW, RBC, ONEBF, ONE1, 1e-6, "qn")
    emit_rownorm_scale(p, KL, TAB[:, 32:36], 4, PS, SQB[:, 0:4 * 512],
                       RROW, RBC, ONEBF, ONE1, 1e-6, "kn")
    emit_rope(p, QB, 16, COSR, SINSGN, SWAPM, PS, SCR)
    emit_rope(p, KL, 4, COSR, SINSGN, SWAPM, PS, SCR)

    for kt in range(NCH):
        p.dstep("gpsimd", lambda gg, kt=kt: [gg.dma_start(
            out=WSLAB[:, :], in_=wv_ap[kt * 128:(kt + 1) * 128, :])],
            label=f"vw{kt}")

        def mmv(t, kt=kt):
            last = None
            for tt in range(8):
                last = t.matmul(PS[:, tt * 512:(tt + 1) * 512],
                                HB[:, kt * TC + tt * 128:kt * TC + tt * 128 + 128],
                                WSLAB[:, :], start=(kt == 0), stop=(kt == NCH - 1))
            return last
        p.cstep("tensor", mmv, label="vmm")

    def vev(s):
        last = None
        for tt in range(8):
            last = s.copy(VT[:, tt * 512:(tt + 1) * 512], PS[:, tt * 512:(tt + 1) * 512])
        return last
    p.cstep("scalar", vev)

    p.dstep("sync", lambda e: [
        e.dma_start(out=kag_in.ap().rearrange("(c p) t -> p c t", p=128),
                    in_=KL[:, :].rearrange("p (c t) -> p c t", t=TC)),
        e.dma_start(out=vag_in.ap().rearrange("(tt p) o -> p tt o", p=128),
                    in_=VT[:, :].rearrange("p (tt o) -> p tt o", o=512)),
    ], label="kv_out", n=2)
    p.xstep("gpsimd", lambda gg: [gg.collective_compute(
        "AllGather", ALU.bypass, replica_groups=[list(range(NCORES))],
        ins=[kag_in.ap().opt()], outs=[kag_out.ap().opt()])], label="ag_k")
    p.xstep("gpsimd", lambda gg: [gg.collective_compute(
        "AllGather", ALU.bypass, replica_groups=[list(range(NCORES))],
        ins=[vag_in.ap().opt()], outs=[vag_out.ap().opt()])], label="ag_v")
    p.dstep("sync", lambda e: [
        feat_store(p, e, q_out, QB),
        e.dma_start(out=kf_out, in_=kag_out[:, :]),
        e.dma_start(out=vf_out, in_=vag_out[:, :]),
    ], label="store", n=3)

    nc = p.build()
    return p, nc


def build_launch_B2():
    p = P()
    q_in = p.din("q", [D, TC], BF16)
    kf_in = p.din("kfull", [512 * NCORES, TC], BF16)
    vf_in = p.din("vfull", [TC * NCORES, 512], BF16)
    qpos_in = p.din("qpos", [1, TC])
    o_out = p.dout("o", [D, TC], BF16)

    QB = p.sb("QB", [128, NCH * TC], BF16)
    KF2 = p.sb("KF2", [128, 2 * 8192], BF16)
    VF2 = p.sb("VF2", [128, 64 * 256], BF16)
    OB = p.sb("OB", [128, NCH * TC], BF16)
    PT = p.sb("PT", [128, 2048], BF16)
    MT = p.sb("MT", [128, 2048], BF16)
    QROW = p.sb("QROW", [1, TC])
    QPOSB = p.sb("QPOSB", [128, TC])
    KIDXC = p.sb("KIDXC", [128, 64])
    ZROW = p.sb("ZROW", [1, 512])
    ZB = p.sb("ZB", [128, 512])
    ONEBF = p.sb("ONEBF", [128, 1], BF16)
    ONE1 = p.sb("ONE1", [1, 128])
    NEG30 = p.sb("NEG30", [128, 1])
    PS = p.ps("PS", [128, 4096])

    SLAB = PS[:, 0:2048]
    OP = PS[:, 2048:2560]
    ZP = PS[0:1, 2560:3072]
    PBC = PS[:, 3072:3584]

    p.dstep("sync", lambda e: [
        feat_load(p, e, QB, q_in),
        e.dma_start(out=QROW[:, :], in_=qpos_in),
    ], label="load", n=2)

    def init(v):
        v.memset(ONEBF[:, :], 1.0)
        v.memset(NEG30[:, :], -30.0)
        return v.memset(ONE1[:, :], 1.0)
    p.cstep("vector", init)
    p.cstep("gpsimd", lambda gg: gg.iota(
        KIDXC[:, :], pattern=[[128, 64]], base=0, channel_multiplier=1,
        allow_small_or_imprecise_dtypes=True))
    for th in range(2):
        p.cstep("tensor", lambda t, th=th: t.matmul(
            PBC, ONE1[0:1, :], QROW[0:1, th * 512:(th + 1) * 512],
            start=True, stop=True))
        p.cstep("vector", lambda v, th=th: v.tensor_copy(
            QPOSB[:, th * 512:(th + 1) * 512], PBC))

    for grp in range(2):
        def ldk(e, grp=grp):
            ins = []
            for chl in range(2):
                ch = 2 * grp + chl
                for r in range(NCORES):
                    ins.append(e.dma_start(
                        out=KF2[:, chl * 8192 + r * TC: chl * 8192 + (r + 1) * TC],
                        in_=kf_in[r * 512 + ch * 128: r * 512 + (ch + 1) * 128, :]))
            return ins
        p.dstep("sync", ldk, label=f"ldk{grp}", n=16)
        p.dstep("sync", lambda e, grp=grp: [e.dma_start(
            out=VF2[:, :].rearrange("p (kt o) -> p kt o", o=256),
            in_=vf_in[:, grp * 256:(grp + 1) * 256]
            .rearrange("(kt p) o -> p kt o", p=128))], label=f"ldv{grp}")

        for h8 in range(8):
            h = grp * 8 + h8
            chl = h8 // 4
            for qtg in range(2):
                last_kt = 59 + 4 * qtg
                nkt = last_kt + 1
                qc0 = qtg * 512
                for ktg in range((nkt + 3) // 4):
                    kts = [kt for kt in range(ktg * 4, min(ktg * 4 + 4, nkt))]

                    def mm(t, h=h, chl=chl, qtg=qtg, kts=kts, ktg=ktg):
                        last = None
                        for j, kt in enumerate(kts):
                            last = t.matmul(
                                PS[:, j * 512:(j + 1) * 512],
                                KF2[:, chl * 8192 + kt * 128: chl * 8192 + kt * 128 + 128],
                                QB[:, h * TC + qtg * 512: h * TC + qtg * 512 + 512],
                                start=True, stop=True)
                        return last
                    p.cstep("tensor", mm, label=f"sc{h}_{qtg}_{ktg}")

                    def expv(s, n=len(kts)):
                        return s.activation(PT[:, 0:n * 512], SLAB[:, 0:n * 512],
                                            AF.Exp, bias=NEG30[:, 0:1])
                    p.cstep("scalar", expv)

                    mkts = [kt for kt in kts if kt >= 4 * qtg]
                    if mkts:
                        def mbuild(v, kts=kts, mkts=mkts, qc0=qc0):
                            last = None
                            for j, kt in enumerate(kts):
                                if kt in mkts:
                                    last = v.tensor_scalar(
                                        MT[:, j * 512:(j + 1) * 512],
                                        QPOSB[:, qc0:qc0 + 512],
                                        KIDXC[:, kt:kt + 1], None, ALU.is_ge)
                            return last
                        p.cstep("vector", mbuild)

                        def mapply(v, kts=kts, mkts=mkts):
                            last = None
                            for j, kt in enumerate(kts):
                                if kt in mkts:
                                    last = v.tensor_tensor(
                                        PT[:, j * 512:(j + 1) * 512],
                                        PT[:, j * 512:(j + 1) * 512],
                                        MT[:, j * 512:(j + 1) * 512], ALU.mult)
                            return last
                        p.cstep("vector", mapply)

                    def zav(t, kts=kts, chl=chl, last_kt=last_kt):
                        last = None
                        for j, kt in enumerate(kts):
                            last = t.matmul(ZP, ONEBF[:, 0:1],
                                            PT[:, j * 512:(j + 1) * 512],
                                            start=(kt == 0), stop=(kt == last_kt))
                        for j, kt in enumerate(kts):
                            last = t.matmul(
                                OP, VF2[:, kt * 256 + chl * 128: kt * 256 + chl * 128 + 128],
                                PT[:, j * 512:(j + 1) * 512],
                                start=(kt == 0), stop=(kt == last_kt))
                        return last
                    p.cstep("tensor", zav, label=f"zav{h}_{qtg}_{ktg}")

                p.cstep("vector", lambda v: v.reciprocal(ZROW[0:1, :], ZP))
                p.cstep("tensor", lambda t: t.matmul(
                    PBC, ONE1[0:1, :], ZROW[0:1, :], start=True, stop=True))
                p.cstep("vector", lambda v: v.tensor_copy(ZB[:, :], PBC))
                p.cstep("vector", lambda v, h=h, qc0=qc0: v.tensor_tensor(
                    OB[:, h * TC + qc0: h * TC + qc0 + 512], OP, ZB[:, :], ALU.mult))

    p.dstep("sync", lambda e: [feat_store(p, e, o_out, OB)], label="store")
    nc = p.build()
    return p, nc


def build_launch_B3():
    p = P()
    o_in = p.din("o", [D, TC], BF16)
    xn_in = p.din("xn", [D, TC])
    wo_ap = p.din("wo", [D, D])
    x1_out = p.dout("x1", [D, TC])

    XN = p.sb("XN", [128, NCH * TC])
    OB = p.sb("OB", [128, NCH * TC], BF16)
    WSLAB = p.sb("WSLAB", [128, 512], BF16)
    PS = p.ps("PS", [128, 4096])

    p.dstep("sync", lambda e: [feat_load(p, e, XN, xn_in),
                               feat_load(p, e, OB, o_in)], label="load", n=2)

    def ev(og, banks):
        def f(v, banks=banks):
            last = None
            for (oc, th, pv) in banks:
                seg = XN[:, oc * TC + th * 512: oc * TC + th * 512 + 512]
                last = v.tensor_tensor(seg, seg, pv, ALU.add)
            return last
        p.cstep("vector", f, label="ev_add")
    emit_proj(p, wo_ap, OB, D, D, WSLAB, PS, ev)
    p.dstep("sync", lambda e: [feat_store(p, e, x1_out, XN)], label="store")
    nc = p.build()
    return p, nc




def emit_router(p, TB, rw_slab_ap, RWB, COMB, CT, RLOG, RSIG, RB0B, IDF, OHALL,
                PS, M1, kind, bias_tiled_ap=None):
    p.dstep("gpsimd", lambda g: [g.dma_start(
        out=RWB[:, :].rearrange("p (k e) -> p k e", e=8),
        in_=rw_slab_ap.rearrange("(k p) e -> p k e", p=128))], label="rw_dma")
    if bias_tiled_ap is not None:
        p.dstep("sync", lambda e: [e.dma_start(out=RB0B[:, :], in_=bias_tiled_ap)],
                label="rb_dma")

    def mmr(t):
        last = None
        for tt in range(8):
            for kt in range(NCH):
                last = t.matmul(PS[:, tt * 8: tt * 8 + 8],
                                TB[:, kt * TC + tt * 128: kt * TC + tt * 128 + 128],
                                RWB[:, kt * 8:(kt + 1) * 8],
                                start=(kt == 0), stop=(kt == NCH - 1))
        return last
    p.cstep("tensor", mmr, label="r_mm")
    p.cstep("vector", lambda v: v.tensor_copy(RLOG[:, :], PS[:, 0:64]))

    if kind == "l0":
        p.cstep("scalar", lambda s: s.activation(RSIG[:, :], RLOG[:, :], AF.Sigmoid))
        p.cstep("vector", lambda v: v.tensor_tensor(RLOG[:, :], RSIG[:, :],
                                                    RB0B[:, :], ALU.add))
        SELV = RLOG
        W = RSIG
    else:
        p.cstep("vector", lambda v: v.tensor_reduce(
            M1[:, 0:8], RLOG[:, :].rearrange("p (t e) -> p t e", e=8),
            AX.X, ALU.max))
        def sub(v):
            last = None
            for tt in range(8):
                last = v.tensor_scalar(RLOG[:, tt * 8:(tt + 1) * 8],
                                       RLOG[:, tt * 8:(tt + 1) * 8],
                                       M1[:, tt:tt + 1], None, ALU.subtract)
            return last
        p.cstep("vector", sub)
        p.cstep("scalar", lambda s: s.activation(RSIG[:, :], RLOG[:, :], AF.Exp))
        p.cstep("vector", lambda v: v.tensor_reduce(
            M1[:, 8:16], RSIG[:, :].rearrange("p (t e) -> p t e", e=8),
            AX.X, ALU.add))
        p.cstep("vector", lambda v: v.reciprocal(M1[:, 8:16], M1[:, 8:16]))
        def nrm(v):
            last = None
            for tt in range(8):
                last = v.tensor_scalar(RSIG[:, tt * 8:(tt + 1) * 8],
                                       RSIG[:, tt * 8:(tt + 1) * 8],
                                       M1[:, 8 + tt:9 + tt], None, ALU.mult)
            return last
        p.cstep("vector", nrm)
        SELV = RSIG
        W = RSIG

    p.cstep("vector", lambda v: v.tensor_reduce(
        M1[:, 0:8], SELV[:, :].rearrange("p (t e) -> p t e", e=8), AX.X, ALU.max))
    for tt in range(8):
        sl = slice(tt * 8, (tt + 1) * 8)
        p.cstep("vector", lambda v, tt=tt, sl=sl: v.tensor_scalar(
            COMB[:, sl], SELV[:, sl], M1[:, tt:tt + 1], None, ALU.is_ge))
        p.cstep("vector", lambda v, tt=tt, sl=sl: v.scalar_tensor_tensor(
            M1[:, 16 + tt * 8:16 + tt * 8 + 8]
            if False else RLOG[:, sl], COMB[:, sl], -40000.0, SELV[:, sl],
            ALU.mult, ALU.add))
    p.cstep("vector", lambda v: v.tensor_reduce(
        M1[:, 8:16], RLOG[:, :].rearrange("p (t e) -> p t e", e=8), AX.X, ALU.max))
    for tt in range(8):
        sl = slice(tt * 8, (tt + 1) * 8)
        p.cstep("vector", lambda v, tt=tt, sl=sl: v.tensor_scalar(
            RLOG[:, sl], RLOG[:, sl], M1[:, 8 + tt:9 + tt], None, ALU.is_ge))
        p.cstep("vector", lambda v, tt=tt, sl=sl: v.tensor_tensor(
            COMB[:, sl], COMB[:, sl], RLOG[:, sl], ALU.add))
        p.cstep("vector", lambda v, tt=tt, sl=sl: v.tensor_tensor(
            COMB[:, sl], COMB[:, sl], W[:, sl], ALU.mult))
    if kind == "l0":
        p.cstep("vector", lambda v: v.tensor_reduce(
            M1[:, 0:8], COMB[:, :].rearrange("p (t e) -> p t e", e=8),
            AX.X, ALU.add))
        p.cstep("vector", lambda v: v.reciprocal(M1[:, 0:8], M1[:, 0:8]))
        def nm(v):
            last = None
            for tt in range(8):
                last = v.tensor_scalar(COMB[:, tt * 8:(tt + 1) * 8],
                                       COMB[:, tt * 8:(tt + 1) * 8],
                                       M1[:, tt:tt + 1], None, ALU.mult)
            return last
        p.cstep("vector", nm)

    for tt in range(8):
        p.cstep("tensor", lambda t, tt=tt: t.transpose(
            PS[0:8, tt * 128:(tt + 1) * 128], COMB[:, tt * 8:(tt + 1) * 8],
            IDF[:, :]))
    p.cstep("vector", lambda v: v.tensor_copy(CT[0:8, :], PS[0:8, 0:1024]))


def emit_proj_group(p, w_ap, rhs_bf, ocs, Din, WSLAB, PS, tag=""):
    nkt = Din // 128
    now = len(ocs) * 128
    o0 = ocs[0] * 128
    for kt in range(nkt):
        p.dstep("gpsimd", lambda g, kt=kt: [g.dma_start(
            out=WSLAB[:, 0:now],
            in_=w_ap[kt * 128:(kt + 1) * 128, o0:o0 + now])], label=f"pg{tag}")

        def mm(t, kt=kt):
            last = None
            for j in range(len(ocs)):
                for th in range(2):
                    last = t.matmul(
                        PS[:, (j * 2 + th) * 512:(j * 2 + th) * 512 + 512],
                        WSLAB[:, j * 128:(j + 1) * 128],
                        rhs_bf[:, kt * TC + th * 512:kt * TC + th * 512 + 512],
                        start=(kt == 0), stop=(kt == nkt - 1))
            return last
        p.cstep("tensor", mm, label=f"pgm{tag}")
    return [(oc, th, PS[:, (j * 2 + th) * 512:(j * 2 + th) * 512 + 512])
            for j, oc in enumerate(ocs) for th in range(2)]


def _groups(nch):
    ocs, out = list(range(nch)), []
    while ocs:
        out.append(ocs[:4])
        ocs = ocs[4:]
    return out


def emit_moe_experts(p, TB, wg_ap, wu_ap, Ich, CT, OHALL, CB, GSIL, UCB,
                     WSLAB, PS, acts, rowbase_of, scale2=None, tag="m"):
    for e in range(8):
        for th in range(2):
            p.cstep("tensor", lambda t, e=e, th=th: t.matmul(
                PS[:, 3584:4096], OHALL[0:8, e * 128:(e + 1) * 128],
                CT[0:8, th * 512:(th + 1) * 512], start=True, stop=True))
            p.cstep("vector", lambda v, th=th: v.tensor_copy(
                CB[:, th * 512:(th + 1) * 512], PS[:, 3584:4096]))
        for gi, ocs in enumerate(_groups(Ich)):
            banks = emit_proj_group(p, wg_ap[e], TB, ocs, D, WSLAB, PS,
                                    tag=f"{tag}g{e}_{gi}")

            def evg(s, banks=banks):
                last = None
                for bi, (oc, th, pv) in enumerate(banks):
                    last = s.activation(GSIL[:, bi * 512:(bi + 1) * 512], pv,
                                        AF.Silu)
                return last
            p.cstep("scalar", evg)
            banks = emit_proj_group(p, wu_ap[e], TB, ocs, D, WSLAB, PS,
                                    tag=f"{tag}u{e}_{gi}")

            def evu(v, banks=banks):
                last = None
                for bi, (oc, th, pv) in enumerate(banks):
                    last = v.tensor_tensor(UCB[:, bi * 512:(bi + 1) * 512], pv,
                                           CB[:, th * 512:(th + 1) * 512],
                                           ALU.mult)
                return last
            p.cstep("vector", evu)

            def eva(v, banks=banks):
                last = None
                for bi in range(len(banks)):
                    last = v.tensor_tensor(UCB[:, bi * 512:(bi + 1) * 512],
                                           UCB[:, bi * 512:(bi + 1) * 512],
                                           GSIL[:, bi * 512:(bi + 1) * 512],
                                           ALU.mult)
                return last
            p.cstep("vector", eva)

            def st(ee, ocs=ocs, e=e):
                rows0 = (rowbase_of(e) + ocs[0]) * 128
                return [ee.dma_start(
                    out=acts.ap()[rows0:rows0 + len(ocs) * 128, :]
                    .rearrange("(c p) t -> p c t", p=128),
                    in_=UCB[:, 0:len(ocs) * TC]
                    .rearrange("p (c t) -> p c t", t=TC))]
            p.dstep("sync", st, label=f"{tag}act{e}")


def emit_shared_expert(p, TB, sg_ap, su_ap, SGB, GSIL, UCB, WSLAB, PS,
                       acts, rowbase, tag="s"):
    for gi, ocs in enumerate(_groups(44)):
        banks = emit_proj_group(p, sg_ap, TB, ocs, D, WSLAB, PS, tag=f"{tag}g{gi}")

        def evg(s, banks=banks):
            last = None
            for bi, (oc, th, pv) in enumerate(banks):
                last = s.activation(GSIL[:, bi * 512:(bi + 1) * 512], pv, AF.Silu)
            return last
        p.cstep("scalar", evg)
        banks = emit_proj_group(p, su_ap, TB, ocs, D, WSLAB, PS, tag=f"{tag}u{gi}")

        def evu(v, banks=banks):
            last = None
            for bi, (oc, th, pv) in enumerate(banks):
                last = v.tensor_tensor(UCB[:, bi * 512:(bi + 1) * 512], pv,
                                       SGB[:, th * 512:(th + 1) * 512], ALU.mult)
            return last
        p.cstep("vector", evu)

        def eva(v, banks=banks):
            last = None
            for bi in range(len(banks)):
                last = v.tensor_tensor(UCB[:, bi * 512:(bi + 1) * 512],
                                       UCB[:, bi * 512:(bi + 1) * 512],
                                       GSIL[:, bi * 512:(bi + 1) * 512], ALU.mult)
            return last
        p.cstep("vector", eva)

        def st(ee, ocs=ocs):
            rows0 = (rowbase + ocs[0]) * 128
            return [ee.dma_start(
                out=acts.ap()[rows0:rows0 + len(ocs) * 128, :]
                .rearrange("(c p) t -> p c t", p=128),
                in_=UCB[:, 0:len(ocs) * TC].rearrange("p (c t) -> p c t", t=TC))]
        p.dstep("sync", st, label=f"{tag}act{gi}")


def emit_moe_down(p, sources, acts, ACT2, WSLAB, PS, down_evac, tag="d"):
    total = sum(nkt for (_, nkt, _) in sources)
    for og2 in range(4):
        seen = 0
        for (wd_ap, nkt, rowbase) in sources:
            for kt in range(nkt):
                p.dstep("gpsimd", lambda g, wd_ap=wd_ap, kt=kt, og2=og2: [
                    g.dma_start(out=WSLAB[:, :],
                                in_=wd_ap[kt * 128:(kt + 1) * 128,
                                          og2 * 512:(og2 + 1) * 512])],
                    label=f"{tag}w")
                p.dstep("sync", lambda e, rowbase=rowbase, kt=kt: [
                    e.dma_start(out=ACT2[:, :],
                                in_=acts.ap()[(rowbase + kt) * 128:
                                              (rowbase + kt + 1) * 128, :])],
                    label=f"{tag}a")
                first = seen == 0
                last_ = seen == total - 1

                def mm(t, first=first, last_=last_):
                    r = None
                    for j in range(4):
                        for th in range(2):
                            r = t.matmul(
                                PS[:, (j * 2 + th) * 512:(j * 2 + th) * 512 + 512],
                                WSLAB[:, j * 128:(j + 1) * 128],
                                ACT2[:, th * 512:th * 512 + 512],
                                start=first, stop=last_)
                    return r
                p.cstep("tensor", mm, label=f"{tag}m")
                seen += 1
        banks = [(og2 * 4 + j, th,
                  PS[:, (j * 2 + th) * 512:(j * 2 + th) * 512 + 512])
                 for j in range(4) for th in range(2)]
        down_evac(og2, banks)


def emit_idf_ohall(p, IDF, OHALL, IDI):
    p.cstep("gpsimd", lambda g: g.iota(
        IDI[:, :], pattern=[[1, 128]], base=0, channel_multiplier=-1,
        allow_small_or_imprecise_dtypes=True))
    p.cstep("vector", lambda v: v.tensor_scalar(
        IDF[:, :], IDI[:, :], 0.0, None, ALU.is_equal))
    p.cstep("gpsimd", lambda g: g.iota(
        OHALL[0:8, :].rearrange("p (e m) -> p e m", m=128),
        pattern=[[-1, 8], [0, 128]], base=0, channel_multiplier=1,
        allow_small_or_imprecise_dtypes=True))
    p.cstep("vector", lambda v: v.tensor_scalar(
        OHALL[0:8, :], OHALL[0:8, :], 0.0, None, ALU.is_equal))


def build_launch_C():
    p = P()
    x1_in = p.din("x1", [D, TC])
    ln2_ap = p.din("ln2", [128, 16])
    rw_ap = p.din("rw0", [D, 8])
    rbt_ap = p.din("rb0t", [128, 64])
    wg_ap = p.din("wg0", [8, D, 1024])
    wu_ap = p.din("wu0", [8, D, 1024])
    wd_ap = p.din("wd0", [8, 1024, D])
    acts = p.dint("acts", [64 * 128, TC], BF16)
    g0_out = p.dout("g0", [D, TC], BF16)

    X1 = p.sb("X1", [128, NCH * TC])
    TB = p.sb("TB", [128, NCH * TC], BF16)
    SQB = p.sb("SQB", [128, NCH * 512], BF16)
    GSIL = p.sb("GSIL", [128, 4 * TC], BF16)
    UCB = p.sb("UCB", [128, 4 * TC], BF16)
    ACT2 = p.sb("ACT2", [128, TC], BF16)
    CB = p.sb("CB", [128, TC])
    WSLAB = p.sb("WSLAB", [128, 512], BF16)
    RWB = p.sb("RWB", [128, NCH * 8], BF16)
    COMB = p.sb("COMB", [128, 64])
    CT = p.sb("CT", [8, TC])
    RLOG = p.sb("RLOG", [128, 64])
    RSIG = p.sb("RSIG", [128, 64])
    RB0B = p.sb("RB0B", [128, 64])
    M1 = p.sb("M1", [128, 24])
    IDF = p.sb("IDF", [128, 128])
    IDI = p.sb("IDI", [128, 128], mybir.dt.int32)
    OHALL = p.sb("OHALL", [8, 8 * 128])
    TAB = p.sb("TAB", [128, 16])
    RROW = p.sb("RROW", [1, 512])
    RBC = p.sb("RBC", [128, 512])
    ONEBF = p.sb("ONEBF", [128, 1], BF16)
    ONE1 = p.sb("ONE1", [1, 128])
    PS = p.ps("PS", [128, 4096])

    p.dstep("sync", lambda e: [
        feat_load(p, e, X1, x1_in),
        e.dma_start(out=TAB[:, 0:16], in_=ln2_ap),
    ], label="load", n=2)

    def init(v):
        v.memset(ONEBF[:, :], 1.0)
        return v.memset(ONE1[:, :], 1.0)
    p.cstep("vector", init)
    emit_idf_ohall(p, IDF, OHALL, IDI)

    emit_ln(p, X1, TB, SQB[:, 0:NCH * 512], None, None, RBC,
            PS[0:1, 3584:4096], None, PS[:, 3584:4096], ONEBF, ONE1,
            eps=1e-6, gain=TAB[:, 0:16])

    emit_router(p, TB, rw_ap, RWB, COMB, CT, RLOG, RSIG, RB0B, IDF, OHALL,
                PS, M1, "l0", bias_tiled_ap=rbt_ap)
    emit_moe_experts(p, TB, wg_ap, wu_ap, 8, CT, OHALL, CB, GSIL, UCB,
                     WSLAB, PS, acts, lambda e: e * 8, tag="c")

    def devac(og2, banks):
        def f1(v, banks=banks, og2=og2):
            last = None
            for (oc, th, pv) in banks:
                seg = X1[:, oc * TC + th * 512: oc * TC + th * 512 + 512]
                last = v.tensor_tensor(seg, seg, pv, ALU.add)
            return last
        p.cstep("vector", f1)

        def f2(s, banks=banks, og2=og2):
            last = None
            for (oc, th, pv) in banks:
                seg = X1[:, oc * TC + th * 512: oc * TC + th * 512 + 512]
                last = s.activation(TB[:, oc * TC + th * 512:
                                       oc * TC + th * 512 + 512], seg, AF.Silu)
            return last
        p.cstep("scalar", f2)
    emit_moe_down(p, [(wd_ap[e], 8, e * 8) for e in range(8)], acts, ACT2,
                  WSLAB, PS, devac, tag="cd")
    p.dstep("sync", lambda e: [feat_store(p, e, g0_out, TB)], label="store")
    nc = p.build()
    return p, nc


def build_launch_D():
    p = P()
    x_in = p.din("x", [D, TC])
    s0_in = p.din("s0", [D, TC], BF16)
    g0_in = p.din("g0", [D, TC], BF16)
    cw_ap = p.din("cw", [3, 128, 16])
    cb_ap = p.din("cb", [128, 16])
    selp_ap = p.din("selp", [128, 8])
    seln_ap = p.din("seln", [128, 8])
    hal_in = p.dint("hal_in", [128, 32])
    hal_out = p.dint("hal_out", [128 * NCORES, 32])
    xl1_out = p.dout("xl1", [D, TC])
    xn1_out = p.dout("xn1", [D, TC])
    xn1b_out = p.dout("xn1b", [D, TC], BF16)
    tout_out = p.dout("tout", [D, TC], BF16)

    X = p.sb("X", [128, NCH * TC])
    XN1 = p.sb("XN1", [128, NCH * TC])
    S0 = p.sb("S0", [128, NCH * TC], BF16)
    G0 = p.sb("G0", [128, NCH * TC], BF16)
    SCRD = p.sb("SCRD", [128, TC])
    MB = p.sb("MB", [128, 512])
    RB = p.sb("RB", [128, 512])
    TABD = p.sb("TABD", [128, 64])
    SELP = p.sb("SELP", [128, 8])
    SELN = p.sb("SELN", [128, 8])
    HAL = p.sb("HAL", [128, 32])
    HALL = p.sb("HALL", [128, 256])
    LCOL = p.sb("LCOL", [128, 16])
    RCOL = p.sb("RCOL", [128, 16])
    ONEBF = p.sb("ONEBF", [128, 1], BF16)
    ONE1 = p.sb("ONE1", [1, 128])
    PS = p.ps("PS", [128, 4096])

    p.dstep("sync", lambda e: [
        feat_load(p, e, X, x_in),
        feat_load(p, e, S0, s0_in),
        feat_load(p, e, G0, g0_in),
        e.dma_start(out=TABD[:, 0:48].rearrange("p (k c) -> p k c", c=16),
                    in_=cw_ap[:, :, :].rearrange("k p c -> p k c")),
        e.dma_start(out=TABD[:, 48:64], in_=cb_ap),
        e.dma_start(out=SELP[:, :], in_=selp_ap),
        e.dma_start(out=SELN[:, :], in_=seln_ap),
    ], label="load", n=7)

    def init(v):
        v.memset(ONEBF[:, :], 1.0)
        return v.memset(ONE1[:, :], 1.0)
    p.cstep("vector", init)

    for c in range(NCH):
        p.cstep("vector", lambda v, c=c: v.tensor_tensor(
            SCRD[:, :], S0[:, c * TC:(c + 1) * TC], G0[:, c * TC:(c + 1) * TC],
            ALU.mult))
        p.cstep("vector", lambda v, c=c: v.tensor_tensor(
            X[:, c * TC:(c + 1) * TC], X[:, c * TC:(c + 1) * TC], SCRD[:, :],
            ALU.add))

    def cpx(v):
        last = None
        for c in range(NCH):
            last = v.tensor_copy(XN1[:, c * TC:(c + 1) * TC],
                                 X[:, c * TC:(c + 1) * TC])
        return last
    p.cstep("vector", cpx)

    emit_ln(p, XN1, G0, S0[:, 0:NCH * 512], S0[:, NCH * 512:NCH * 1024],
            MB, RB, PS[0:1, 3584:4096], PS[0:1, 3072:3584], PS[:, 2048:2560],
            ONEBF, ONE1, eps=1e-5)

    def pack(v):
        last = None
        for c in range(NCH):
            v.tensor_copy(HAL[:, c:c + 1], XN1[:, c * TC:c * TC + 1])
            last = v.tensor_copy(HAL[:, 16 + c:17 + c],
                                 XN1[:, c * TC + TC - 1:c * TC + TC])
        return last
    p.cstep("vector", pack)
    p.dstep("gpsimd", lambda g: [g.dma_start(out=hal_in[:, :], in_=HAL[:, :])])
    p.xstep("gpsimd", lambda g: [g.collective_compute(
        "AllGather", ALU.bypass, replica_groups=[list(range(NCORES))],
        ins=[hal_in.ap().opt()], outs=[hal_out.ap().opt()])], label="ag_hal")
    p.dstep("gpsimd", lambda g: [g.dma_start(
        out=HALL[:, :].rearrange("p (r c) -> p r c", c=32),
        in_=hal_out.ap().rearrange("(r p) c -> p r c", p=128))])

    def zc(v):
        v.memset(LCOL[:, :], 0.0)
        return v.memset(RCOL[:, :], 0.0)
    p.cstep("vector", zc)
    for j in range(NCORES):
        p.cstep("vector", lambda v, j=j: v.scalar_tensor_tensor(
            LCOL[:, :], HALL[:, j * 32 + 16:j * 32 + 32], SELP[:, j:j + 1],
            LCOL[:, :], ALU.mult, ALU.add))
        p.cstep("vector", lambda v, j=j: v.scalar_tensor_tensor(
            RCOL[:, :], HALL[:, j * 32:j * 32 + 16], SELN[:, j:j + 1],
            RCOL[:, :], ALU.mult, ALU.add))

    for c in range(NCH):
        xc = XN1[:, c * TC:(c + 1) * TC]
        p.cstep("vector", lambda v, c=c, xc=xc: v.tensor_scalar(
            SCRD[:, :], xc, TABD[:, 16 + c:17 + c], TABD[:, 48 + c:49 + c],
            ALU.mult, ALU.add))
        p.cstep("vector", lambda v, c=c, xc=xc: v.scalar_tensor_tensor(
            SCRD[:, 1:TC], xc[:, 0:TC - 1], TABD[:, c:c + 1], SCRD[:, 1:TC],
            ALU.mult, ALU.add))
        p.cstep("vector", lambda v, c=c: v.scalar_tensor_tensor(
            SCRD[:, 0:1], LCOL[:, c:c + 1], TABD[:, c:c + 1], SCRD[:, 0:1],
            ALU.mult, ALU.add))
        p.cstep("vector", lambda v, c=c, xc=xc: v.scalar_tensor_tensor(
            SCRD[:, 0:TC - 1], xc[:, 1:TC], TABD[:, 32 + c:33 + c],
            SCRD[:, 0:TC - 1], ALU.mult, ALU.add))
        p.cstep("vector", lambda v, c=c: v.scalar_tensor_tensor(
            SCRD[:, TC - 1:TC], RCOL[:, c:c + 1], TABD[:, 32 + c:33 + c],
            SCRD[:, TC - 1:TC], ALU.mult, ALU.add))
        p.cstep("vector", lambda v, c=c: v.tensor_copy(
            S0[:, c * TC:(c + 1) * TC], SCRD[:, :]))

    p.dstep("sync", lambda e: [
        feat_store(p, e, xl1_out, X),
        feat_store(p, e, xn1_out, XN1),
        feat_store(p, e, xn1b_out, G0),
        feat_store(p, e, tout_out, S0),
    ], label="store", n=4)
    nc = p.build()
    return p, nc


def build_launch_E():
    p = P()
    xn1b_in = p.din("xn1b", [D, TC], BF16)
    tout_in = p.din("tout", [D, TC], BF16)
    WuT_ap = p.din("WuT", [D, D])
    WvT_ap = p.din("WvT", [D, D])
    Wg1T_ap = p.din("Wg1T", [D, 512])
    Wg2T_ap = p.din("Wg2T", [512, D])
    bu_ap = p.din("bu", [128, 16])
    bv_ap = p.din("bv", [128, 16])
    bg1_ap = p.din("bg1", [128, 4])
    bg2_ap = p.din("bg2", [128, 16])
    sel_ap = p.din("sel", [128, 8])
    agg_in = p.dint("agg_in", [128, 16])
    agg_out = p.dint("agg_out", [128 * NCORES, 16])
    s1_out = p.dout("s1", [D, TC], BF16)

    XN1B = p.sb("XN1B", [128, NCH * TC], BF16)
    S1B = p.sb("S1B", [128, NCH * TC], BF16)
    UB = p.sb("UB", [128, NCH * TC], BF16)
    GB = p.sb("GB", [128, NCH * TC], BF16)
    VB = p.sb("VB", [128, NCH * TC], BF16)
    H1B = p.sb("H1B", [128, 4 * TC], BF16)
    ZEROS = p.sb("ZEROS", [128, TC])
    TABE = p.sb("TABE", [128, 52])
    SEL = p.sb("SEL", [128, 8])
    AGG = p.sb("AGG", [128, 16])
    AGALL = p.sb("AGALL", [128, 128])
    CAR = p.sb("CAR", [128, 16])
    SST = p.sb("SST", [128, 16])
    WSLAB = p.sb("WSLAB", [128, 512], BF16)
    PS = p.ps("PS", [128, 4096])

    p.dstep("sync", lambda e: [
        feat_load(p, e, XN1B, xn1b_in),
        feat_load(p, e, S1B, tout_in),
        e.dma_start(out=TABE[:, 0:16], in_=bu_ap),
        e.dma_start(out=TABE[:, 16:32], in_=bv_ap),
        e.dma_start(out=TABE[:, 32:36], in_=bg1_ap),
        e.dma_start(out=TABE[:, 36:52], in_=bg2_ap),
        e.dma_start(out=SEL[:, :], in_=sel_ap),
    ], label="load", n=7)
    p.cstep("vector", lambda v: v.memset(ZEROS[:, :], 0.0))

    def ev_bias(dst, bias_col_of, act=None):
        def f(og, banks):
            def e(eng, banks=banks):
                last = None
                for (oc, th, pv) in banks:
                    seg = dst[:, oc * TC + th * 512: oc * TC + th * 512 + 512]
                    if act is None:
                        last = eng.tensor_scalar(seg, pv, bias_col_of(oc), None,
                                                 ALU.add)
                    else:
                        last = eng.activation(seg, pv, act, bias=bias_col_of(oc))
                return last
            p.cstep("vector" if act is None else "scalar", e, label="evb")
        return f

    emit_proj(p, WuT_ap, XN1B, D, D, WSLAB, PS,
              ev_bias(UB, lambda oc: TABE[:, oc:oc + 1]))
    emit_proj(p, Wg1T_ap, XN1B, D, 512, WSLAB, PS,
              ev_bias(H1B, lambda oc: TABE[:, 32 + oc:33 + oc], act=AF.Relu))
    emit_proj(p, Wg2T_ap, H1B, 512, D, WSLAB, PS,
              ev_bias(GB, lambda oc: TABE[:, 36 + oc:37 + oc], act=AF.Sigmoid))

    def ug(v):
        last = None
        for c in range(NCH):
            last = v.tensor_tensor(UB[:, c * TC:(c + 1) * TC],
                                   UB[:, c * TC:(c + 1) * TC],
                                   GB[:, c * TC:(c + 1) * TC], ALU.mult)
        return last
    p.cstep("vector", ug)

    def cum(v):
        last = None
        for c in range(NCH):
            last = v.tensor_tensor_scan(
                UB[:, c * TC:(c + 1) * TC], ZEROS[:, :],
                UB[:, c * TC:(c + 1) * TC], 0.0, ALU.add, ALU.add)
        return last
    p.cstep("vector", cum)

    def agcp(v):
        last = None
        for c in range(NCH):
            last = v.tensor_copy(AGG[:, c:c + 1],
                                 UB[:, c * TC + TC - 1:c * TC + TC])
        return last
    p.cstep("vector", agcp)
    p.dstep("gpsimd", lambda g: [g.dma_start(out=agg_in[:, :], in_=AGG[:, :])])
    p.xstep("gpsimd", lambda g: [g.collective_compute(
        "AllGather", ALU.bypass, replica_groups=[list(range(NCORES))],
        ins=[agg_in.ap().opt()], outs=[agg_out.ap().opt()])], label="ag_cum")
    p.dstep("gpsimd", lambda g: [g.dma_start(
        out=AGALL[:, :].rearrange("p (r c) -> p r c", c=16),
        in_=agg_out.ap().rearrange("(r p) c -> p r c", p=128))])

    def cm(v):
        v.memset(CAR[:, :], 0.0)
        return v.memset(SST[:, :], 0.0)
    p.cstep("vector", cm)
    for j in range(NCORES):
        p.cstep("vector", lambda v, j=j: v.scalar_tensor_tensor(
            CAR[:, :], SST[:, :], SEL[:, j:j + 1], CAR[:, :], ALU.mult, ALU.add))
        p.cstep("vector", lambda v, j=j: v.tensor_tensor(
            SST[:, :], SST[:, :], AGALL[:, j * 16:(j + 1) * 16], ALU.add))

    emit_proj(p, WvT_ap, XN1B, D, D, WSLAB, PS,
              ev_bias(VB, lambda oc: TABE[:, 16 + oc:17 + oc]))

    def vg(v):
        last = None
        for c in range(NCH):
            last = v.tensor_tensor(VB[:, c * TC:(c + 1) * TC],
                                   VB[:, c * TC:(c + 1) * TC],
                                   GB[:, c * TC:(c + 1) * TC], ALU.mult)
        return last
    p.cstep("vector", vg)

    def go(v):
        last = None
        for c in range(NCH):
            last = v.scalar_tensor_tensor(
                VB[:, c * TC:(c + 1) * TC], UB[:, c * TC:(c + 1) * TC],
                CAR[:, c:c + 1], VB[:, c * TC:(c + 1) * TC], ALU.add, ALU.mult)
        return last
    p.cstep("vector", go)

    def ev_add(og, banks):
        def f(v, banks=banks):
            last = None
            for (oc, th, pv) in banks:
                seg = S1B[:, oc * TC + th * 512: oc * TC + th * 512 + 512]
                last = v.tensor_tensor(seg, seg, pv, ALU.add)
            return last
        p.cstep("vector", f, label="uv_add")
    emit_proj(p, WuT_ap, VB, D, D, WSLAB, PS, ev_add)

    p.dstep("sync", lambda e: [feat_store(p, e, s1_out, S1B)], label="store")
    nc = p.build()
    return p, nc


def build_launch_F():
    p = P()
    xn1b_in = p.din("xn1b", [D, TC], BF16)
    xl1_in = p.din("xl1", [D, TC])
    s1_in = p.din("s1", [D, TC], BF16)
    wr_ap = p.din("Wr1", [D, 8])
    wg_ap = p.din("wg1", [8, D, 1408])
    wu_ap = p.din("wu1", [8, D, 1408])
    wd_ap = p.din("wd1", [8, 1408, D])
    sg_ap = p.din("Sg", [D, 5632])
    su_ap = p.din("Su", [D, 5632])
    sd_ap = p.din("Sd", [5632, D])
    wsg_ap = p.din("Wsg", [128, 16])
    acts = p.dint("acts1", [132 * 128, TC], BF16)
    out_out = p.dout("out", [D, TC])

    XN1B = p.sb("XN1B", [128, NCH * TC], BF16)
    XL1 = p.sb("XL1", [128, NCH * TC])
    S1 = p.sb("S1", [128, NCH * TC], BF16)
    GSIL = p.sb("GSIL", [128, 4 * TC], BF16)
    UCB = p.sb("UCB", [128, 4 * TC], BF16)
    ACT2 = p.sb("ACT2", [128, TC], BF16)
    CB = p.sb("CB", [128, TC])
    SGB = p.sb("SGB", [128, TC])
    WSGB = p.sb("WSGB", [128, 16], BF16)
    SROW = p.sb("SROW", [1, TC])
    WSLAB = p.sb("WSLAB", [128, 512], BF16)
    RWB = p.sb("RWB", [128, NCH * 8], BF16)
    COMB = p.sb("COMB", [128, 64])
    CT = p.sb("CT", [8, TC])
    RLOG = p.sb("RLOG", [128, 64])
    RSIG = p.sb("RSIG", [128, 64])
    RB0B = p.sb("RB0B", [128, 64])
    M1 = p.sb("M1", [128, 24])
    IDF = p.sb("IDF", [128, 128])
    IDI = p.sb("IDI", [128, 128], mybir.dt.int32)
    OHALL = p.sb("OHALL", [8, 8 * 128])
    ONEBF = p.sb("ONEBF", [128, 1], BF16)
    ONE1 = p.sb("ONE1", [1, 128])
    PS = p.ps("PS", [128, 4096])

    p.dstep("sync", lambda e: [
        feat_load(p, e, XN1B, xn1b_in),
        feat_load(p, e, XL1, xl1_in),
        feat_load(p, e, S1, s1_in),
    ], label="load", n=3)
    p.dstep("gpsimd", lambda g: [g.dma_start(out=WSGB[:, :], in_=wsg_ap)],
            label="wsg_dma")

    def init(v):
        v.memset(ONEBF[:, :], 1.0)
        return v.memset(ONE1[:, :], 1.0)
    p.cstep("vector", init)
    emit_idf_ohall(p, IDF, OHALL, IDI)
    emit_router(p, XN1B, wr_ap, RWB, COMB, CT, RLOG, RSIG, RB0B, IDF, OHALL,
                PS, M1, "l1")

    def mmsg(t):
        last = None
        for th in range(2):
            for kt in range(NCH):
                last = t.matmul(PS[0:1, 3584 + th * 256:3584 + th * 256 + 512]
                                if False else PS[0:1, (7 - th) * 512:(7 - th) * 512 + 512],
                                WSGB[:, kt:kt + 1],
                                XN1B[:, kt * TC + th * 512:kt * TC + th * 512 + 512],
                                start=(kt == 0), stop=(kt == NCH - 1))
        return last
    p.cstep("tensor", mmsg)
    p.cstep("scalar", lambda s: s.activation(SROW[0:1, 0:512], PS[0:1, 3584:4096],
                                             AF.Sigmoid))
    p.cstep("scalar", lambda s: s.activation(SROW[0:1, 512:1024], PS[0:1, 3072:3584],
                                             AF.Sigmoid))
    for th in range(2):
        p.cstep("tensor", lambda t, th=th: t.matmul(
            PS[:, 2048:2560], ONE1[0:1, :], SROW[0:1, th * 512:(th + 1) * 512],
            start=True, stop=True))
        p.cstep("vector", lambda v, th=th: v.tensor_copy(
            SGB[:, th * 512:(th + 1) * 512], PS[:, 2048:2560]))

    emit_moe_experts(p, XN1B, wg_ap, wu_ap, 11, CT, OHALL, CB, GSIL, UCB,
                     WSLAB, PS, acts, lambda e: e * 11, tag="f")
    emit_shared_expert(p, XN1B, sg_ap, su_ap, SGB, GSIL, UCB, WSLAB, PS,
                       acts, 88, tag="fs")

    def devac(og2, banks):
        def f1(s, banks=banks):
            last = None
            for bi, (oc, th, pv) in enumerate(banks):
                last = s.activation(GSIL[:, bi * 512:(bi + 1) * 512], pv, AF.Silu)
            return last
        p.cstep("scalar", f1)

        def f2(v, banks=banks):
            last = None
            for bi, (oc, th, pv) in enumerate(banks):
                last = v.tensor_tensor(
                    UCB[:, bi * 512:(bi + 1) * 512],
                    GSIL[:, bi * 512:(bi + 1) * 512],
                    S1[:, oc * TC + th * 512: oc * TC + th * 512 + 512], ALU.mult)
            return last
        p.cstep("vector", f2)

        def f3(v, banks=banks):
            last = None
            for bi, (oc, th, pv) in enumerate(banks):
                seg = XL1[:, oc * TC + th * 512: oc * TC + th * 512 + 512]
                last = v.tensor_tensor(seg, seg, UCB[:, bi * 512:(bi + 1) * 512],
                                       ALU.add)
            return last
        p.cstep("vector", f3)
    emit_moe_down(p, [(wd_ap[e], 11, e * 11) for e in range(8)]
                  + [(sd_ap, 44, 88)], acts, ACT2, WSLAB, PS, devac, tag="fd")

    p.dstep("sync", lambda e: [feat_store(p, e, out_out, XL1)], label="store")
    nc = p.build()
    return p, nc




def _run(nc, in_maps):
    res = run_bass_kernel_spmd(nc, in_maps, core_ids=list(range(NCORES)))
    return res.results


def kernel_multi(**inputs):
    g, pc = _prep_host(inputs)
    cores = range(NCORES)

    pA, ncA = build_launch_A()
    rA = _run(ncA, [dict(x=pc[i]["x"], igw=g["igw"], rgw=g["rgw"], igb=g["igb"],
                         rgb=g["rgb"], rp=g["rp"], sel=pc[i]["sel"],
                         first_mask=pc[i]["first_mask"]) for i in cores])
    pA.close()

    pB1, ncB1 = build_launch_B1(g)
    rB1 = _run(ncB1, [dict(xn=rA[i]["xn"], wq=g["wq"], wk=g["wk"], wv=g["wv"],
                           ln1=g["ln1"], qnw=g["qnw"], knw=g["knw"],
                           cosr=pc[i]["cosr"], sinsgn=pc[i]["sinsgn"],
                           swapm=g["swapm"]) for i in cores])
    pB1.close()

    pB2, ncB2 = build_launch_B2()
    rB2 = _run(ncB2, [dict(q=rB1[i]["q"], kfull=rB1[0]["kfull"],
                           vfull=rB1[0]["vfull"], qpos=pc[i]["qpos"])
                      for i in cores])
    pB2.close()

    pB3, ncB3 = build_launch_B3()
    rB3 = _run(ncB3, [dict(o=rB2[i]["o"], xn=rA[i]["xn"], wo=g["wo"])
                      for i in cores])
    pB3.close()

    pC, ncC = build_launch_C()
    rC = _run(ncC, [dict(x1=rB3[i]["x1"], ln2=g["ln2"], rw0=g["rw0"],
                         rb0t=g["rb0t"], wg0=g["wg0"], wu0=g["wu0"],
                         wd0=g["wd0"]) for i in cores])
    pC.close()

    pD, ncD = build_launch_D()
    rD = _run(ncD, [dict(x=pc[i]["x"], s0=rA[i]["s0"], g0=rC[i]["g0"],
                         cw=g["cw"], cb=g["cb"], selp=pc[i]["selp"],
                         seln=pc[i]["seln"]) for i in cores])
    pD.close()

    pE, ncE = build_launch_E()
    rE = _run(ncE, [dict(xn1b=rD[i]["xn1b"], tout=rD[i]["tout"], WuT=g["WuT"],
                         WvT=g["WvT"], Wg1T=g["Wg1T"], Wg2T=g["Wg2T"],
                         bu=g["bu"], bv=g["bv"], bg1=g["bg1"], bg2=g["bg2"],
                         sel=pc[i]["sel"]) for i in cores])
    pE.close()

    pF, ncF = build_launch_F()
    rF = _run(ncF, [dict(xn1b=rD[i]["xn1b"], xl1=rD[i]["xl1"], s1=rE[i]["s1"],
                         Wr1=g["Wr1"], wg1=g["wg1"], wu1=g["wu1"],
                         wd1=g["wd1"], Sg=g["Sg"], Su=g["Su"], Sd=g["Sd"],
                         Wsg=g["Wsg"]) for i in cores])
    pF.close()

    out = np.concatenate([np.asarray(rF[i]["out"], np.float32) for i in cores],
                         axis=1)
    return np.ascontiguousarray(out.T)[None]




def build_fused():
    p = P()
    x_in = p.din("x", [D, TC])
    out_out = p.dout("out", [D, TC])
    igw_ap = p.din("igw", [16, 128, 128])
    rgw_ap = p.din("rgw", [16, 128, 128])
    igb_ap = p.din("igb", [128, 16])
    rgb_ap = p.din("rgb", [128, 16])
    rp_ap = p.din("rp", [128, 16])
    sel_ap = p.din("sel", [128, 8])
    selp_ap = p.din("selp", [128, 8])
    seln_ap = p.din("seln", [128, 8])
    fm_ap = p.din("first_mask", [128, 1])
    wq_ap = p.din("wq", [D, D])
    wk_ap = p.din("wk", [D, 512])
    wv_ap = p.din("wv", [D, 512])
    wo_ap = p.din("wo", [D, D])
    ln1_ap = p.din("ln1", [128, 16])
    ln2_ap = p.din("ln2", [128, 16])
    qnw_ap = p.din("qnw", [128, 16])
    knw_ap = p.din("knw", [128, 4])
    cos_ap = p.din("cosr", [64, TC])
    sin_ap = p.din("sinsgn", [64, TC])
    swap_ap = p.din("swapm", [64, 64], BF16)
    qpos_ap = p.din("qpos", [1, TC])
    rw0_ap = p.din("rw0", [D, 8])
    rbt_ap = p.din("rb0t", [128, 64])
    wg0_ap = p.din("wg0", [8, D, 1024])
    wu0_ap = p.din("wu0", [8, D, 1024])
    wd0_ap = p.din("wd0", [8, 1024, D])
    cw_ap = p.din("cw", [3, 128, 16])
    cb_ap = p.din("cb", [128, 16])
    WuT_ap = p.din("WuT", [D, D])
    WvT_ap = p.din("WvT", [D, D])
    Wg1T_ap = p.din("Wg1T", [D, 512])
    Wg2T_ap = p.din("Wg2T", [512, D])
    bu_ap = p.din("bu", [128, 16])
    bv_ap = p.din("bv", [128, 16])
    bg1_ap = p.din("bg1", [128, 4])
    bg2_ap = p.din("bg2", [128, 16])
    wr1_ap = p.din("Wr1", [D, 8])
    wg1_ap = p.din("wg1", [8, D, 1408])
    wu1_ap = p.din("wu1", [8, D, 1408])
    wd1_ap = p.din("wd1", [8, 1408, D])
    sg_ap = p.din("Sg", [D, 5632])
    su_ap = p.din("Su", [D, 5632])
    sd_ap = p.din("Sd", [5632, D])
    wsg_ap = p.din("Wsg", [128, 16])

    agg_in = p.dint("agg_in", [128, 32])
    agg_out = p.dint("agg_out", [128 * NCORES, 32])
    kag_in = p.dint("kag_in", [512, TC], BF16)
    kag_out = p.dint("kag_out", [512 * NCORES, TC], BF16)
    vag_in = p.dint("vag_in", [TC, 512], BF16)
    vag_out = p.dint("vag_out", [TC * NCORES, 512], BF16)
    s0_d = p.dint("s0_d", [D, TC], BF16)
    o_d = p.dint("o_d", [D, TC], BF16)
    g0_d = p.dint("g0_d", [D, TC], BF16)
    xl1_d = p.dint("xl1_d", [D, TC])
    tout_d = p.dint("tout_d", [D, TC], BF16)
    s1_d = p.dint("s1_d", [D, TC], BF16)
    hal_in = p.dint("hal_in", [128, 32])
    hal_out = p.dint("hal_out", [128 * NCORES, 32])
    acts0 = p.dint("acts0", [64 * 128, TC], BF16)
    acts1 = p.dint("acts1", [132 * 128, TC], BF16)

    F32A = p.sb("F32A", [128, NCH * TC])
    BF1 = p.sb("BF1", [128, NCH * TC], BF16)
    BF2 = p.sb("BF2", [128, NCH * TC], BF16)
    BF3 = p.sb("BF3", [128, NCH * TC], BF16)
    SMF = p.sb("SMF", [128, 5120])
    SMB = p.sb("SMB", [128, 9216], BF16)
    WSLAB = p.sb("WSLAB", [128, 512], BF16)
    IDI = p.sb("IDI", [128, 128], mybir.dt.int32)
    MB = p.sb("MB", [128, 512])
    RB = p.sb("RB", [128, 512])
    TAB = p.sb("TAB", [128, 64])
    TAB2 = p.sb("TAB2", [128, 64])
    ONE1 = p.sb("ONE1", [1, 128])
    ONEBF = p.sb("ONEBF", [128, 1], BF16)
    NEG30 = p.sb("NEG30", [128, 1])
    SEL = p.sb("SEL", [128, 8])
    SELP = p.sb("SELP", [128, 8])
    SELN = p.sb("SELN", [128, 8])
    FMASK = p.sb("FMASK", [128, 1])
    AGG = p.sb("AGG", [128, 32])
    AGALL = p.sb("AGALL", [128, 256])
    CAR = p.sb("CAR", [128, 16])
    SST = p.sb("SST", [128, 16])
    QNWS = p.sb("QNWS", [128, 16])
    PS = p.ps("PS", [128, 4096])

    X, XNB, ALOC, YLOC = F32A, BF1, BF2, BF3
    SCR = SMF[:, 0:4096]
    ZEROS = SMF[:, 4096:5120]
    p.dstep("sync", lambda e: [
        feat_load(p, e, X, x_in),
        e.dma_start(out=TAB[:, 0:16], in_=igb_ap),
        e.dma_start(out=TAB[:, 16:32], in_=rgb_ap),
        e.dma_start(out=TAB[:, 32:48], in_=rp_ap),
        e.dma_start(out=SEL[:, :], in_=sel_ap),
        e.dma_start(out=SELP[:, :], in_=selp_ap),
        e.dma_start(out=SELN[:, :], in_=seln_ap),
        e.dma_start(out=FMASK[:, :], in_=fm_ap),
        e.dma_start(out=QNWS[:, 0:16], in_=qnw_ap),
    ], label="loadA", n=9)

    def init(v):
        v.memset(ZEROS[:, :], 0.0)
        v.memset(ONEBF[:, :], 1.0)
        v.memset(NEG30[:, :], -30.0)
        return v.memset(ONE1[:, :], 1.0)
    p.cstep("vector", init)
    emit_softplus8(p, TAB, SCR)
    emit_ln(p, X, XNB, ALOC[:, 0:NCH * 512], YLOC[:, 0:NCH * 512],
            MB, RB, PS[0:1, 3584:4096], PS[0:1, 3072:3584], PS[:, 2048:2560],
            ONEBF, ONE1)
    emit_scan_branch(p, X, XNB, ALOC, YLOC, SCR, ZEROS, TAB, SEL, FMASK,
                     WSLAB[:, 0:128], AGG, AGALL, CAR, SST,
                     PS[:, 0:512], PS[:, 512:1024],
                     igw_ap, rgw_ap, agg_in, agg_out)
    p.dstep("sync", lambda e: [feat_store(p, e, s0_d.ap(), YLOC)], label="s0st")

    XN, HB, QB = F32A, BF1, BF2
    SQB = BF3[:, 0:NCH * 512]
    KL = SMB[:, 0:4096]
    VT = SMB[:, 4096:8192]
    SWAPM = SMB[0:64, 8192:8256]
    COSR = SMF[0:64, 0:1024]
    SINSGN = SMF[0:64, 1024:2048]
    RROW = SMF[0:1, 4096:4608]
    p.dstep("sync", lambda e: [
        e.dma_start(out=TAB[:, 0:16], in_=ln1_ap),
        e.dma_start(out=TAB[:, 32:36], in_=knw_ap),
        e.dma_start(out=COSR, in_=cos_ap),
        e.dma_start(out=SINSGN, in_=sin_ap),
        e.dma_start(out=SWAPM, in_=swap_ap),
    ], label="loadB1", n=5)
    p.cstep("vector", lambda v: v.tensor_scalar(
        QNWS[:, :], QNWS[:, :], float(1.0 / np.sqrt(128.0)), None, ALU.mult))
    emit_ln(p, XN, HB, SQB, None, None, RB,
            PS[0:1, 3584:4096], None, PS[:, 3584:4096], ONEBF, ONE1,
            eps=1e-6, gain=TAB[:, 0:16])

    def ev_copy(dst, col_of):
        def f(og, banks):
            def e(s, banks=banks):
                last = None
                for (oc, th, pv) in banks:
                    last = s.copy(dst[:, col_of(oc, th):col_of(oc, th) + 512], pv)
                return last
            p.cstep("scalar", e, label="pev")
        return f
    emit_proj(p, wq_ap, HB, D, D, WSLAB, PS,
              ev_copy(QB, lambda oc, th: oc * TC + th * 512))
    emit_proj(p, wk_ap, HB, D, 512, WSLAB, PS,
              ev_copy(KL, lambda oc, th: oc * TC + th * 512))
    emit_rownorm_scale(p, QB, QNWS[:, 0:16], 16, PS, SQB, RROW, RB,
                       ONEBF, ONE1, 1e-6, "qn")
    emit_rownorm_scale(p, KL, TAB[:, 32:36], 4, PS, SQB[:, 0:4 * 512], RROW,
                       RB, ONEBF, ONE1, 1e-6, "kn")
    emit_rope(p, QB, 16, COSR, SINSGN, SWAPM, PS, SMF[:, 2048:4096])
    emit_rope(p, KL, 4, COSR, SINSGN, SWAPM, PS, SMF[:, 2048:4096])
    for kt in range(NCH):
        p.dstep("gpsimd", lambda gg, kt=kt: [gg.dma_start(
            out=WSLAB[:, :], in_=wv_ap[kt * 128:(kt + 1) * 128, :])],
            label=f"vw{kt}")

        def mmv(t, kt=kt):
            last = None
            for tt in range(8):
                last = t.matmul(PS[:, tt * 512:(tt + 1) * 512],
                                HB[:, kt * TC + tt * 128:kt * TC + tt * 128 + 128],
                                WSLAB[:, :], start=(kt == 0), stop=(kt == NCH - 1))
            return last
        p.cstep("tensor", mmv, label="vmm")

    def vev(s):
        last = None
        for tt in range(8):
            last = s.copy(VT[:, tt * 512:(tt + 1) * 512],
                          PS[:, tt * 512:(tt + 1) * 512])
        return last
    p.cstep("scalar", vev)
    p.dstep("sync", lambda e: [
        e.dma_start(out=kag_in.ap().rearrange("(c p) t -> p c t", p=128),
                    in_=KL[:, :].rearrange("p (c t) -> p c t", t=TC)),
        e.dma_start(out=vag_in.ap().rearrange("(tt p) o -> p tt o", p=128),
                    in_=VT[:, :].rearrange("p (tt o) -> p tt o", o=512)),
    ], label="kv_out", n=2)
    p.xstep("gpsimd", lambda gg: [gg.collective_compute(
        "AllGather", ALU.bypass, replica_groups=[list(range(NCORES))],
        ins=[kag_in.ap().opt()], outs=[kag_out.ap().opt()])], label="ag_k")
    p.xstep("gpsimd", lambda gg: [gg.collective_compute(
        "AllGather", ALU.bypass, replica_groups=[list(range(NCORES))],
        ins=[vag_in.ap().opt()], outs=[vag_out.ap().opt()])], label="ag_v")

    KF2, VF2 = BF1, BF3
    PT = SMB[:, 0:2048]
    MT = SMB[:, 2048:4096]
    QPOSB = SMF[:, 0:1024]
    ZB = SMF[:, 1024:1536]
    KIDXC = SMF[:, 1536:1600]
    QROW = SMF[0:1, 1600:2624]
    ZROW = SMF[0:1, 2624:3136]
    SLAB = PS[:, 0:2048]
    OP = PS[:, 2048:2560]
    ZP = PS[0:1, 2560:3072]
    PBC = PS[:, 3072:3584]
    p.dstep("sync", lambda e: [e.dma_start(out=QROW, in_=qpos_ap)],
            label="loadqp")
    p.cstep("gpsimd", lambda gg: gg.iota(
        KIDXC, pattern=[[128, 64]], base=0, channel_multiplier=1,
        allow_small_or_imprecise_dtypes=True))
    for th in range(2):
        p.cstep("tensor", lambda t, th=th: t.matmul(
            PBC, ONE1[0:1, :], QROW[0:1, th * 512:(th + 1) * 512],
            start=True, stop=True))
        p.cstep("vector", lambda v, th=th: v.tensor_copy(
            QPOSB[:, th * 512:(th + 1) * 512], PBC))
    for grp in range(2):
        def ldk(e, grp=grp):
            ins = []
            for chl in range(2):
                ch = 2 * grp + chl
                for r in range(NCORES):
                    ins.append(e.dma_start(
                        out=KF2[:, chl * 8192 + r * TC: chl * 8192 + (r + 1) * TC],
                        in_=kag_out.ap()[r * 512 + ch * 128: r * 512 + (ch + 1) * 128, :]))
            return ins
        p.dstep("sync", ldk, label=f"ldk{grp}", n=16)
        p.dstep("sync", lambda e, grp=grp: [e.dma_start(
            out=VF2[:, 0:64 * 256].rearrange("p (kt o) -> p kt o", o=256),
            in_=vag_out.ap()[:, grp * 256:(grp + 1) * 256]
            .rearrange("(kt p) o -> p kt o", p=128))], label=f"ldv{grp}")
        for h8 in range(8):
            h = grp * 8 + h8
            chl = h8 // 4
            for qtg in range(2):
                last_kt = 59 + 4 * qtg
                nkt = last_kt + 1
                qc0 = qtg * 512
                for ktg in range((nkt + 3) // 4):
                    kts = [kt for kt in range(ktg * 4, min(ktg * 4 + 4, nkt))]

                    def mm(t, h=h, chl=chl, qtg=qtg, kts=kts):
                        last = None
                        for j, kt in enumerate(kts):
                            last = t.matmul(
                                PS[:, j * 512:(j + 1) * 512],
                                KF2[:, chl * 8192 + kt * 128: chl * 8192 + kt * 128 + 128],
                                QB[:, h * TC + qtg * 512: h * TC + qtg * 512 + 512],
                                start=True, stop=True)
                        return last
                    p.cstep("tensor", mm, label="sc")
                    p.cstep("scalar", lambda s, n=len(kts): s.activation(
                        PT[:, 0:n * 512], SLAB[:, 0:n * 512], AF.Exp,
                        bias=NEG30[:, 0:1]))
                    mkts = [kt for kt in kts if kt >= 4 * qtg]
                    if mkts:
                        def mbuild(v, kts=kts, mkts=mkts, qc0=qc0):
                            last = None
                            for j, kt in enumerate(kts):
                                if kt in mkts:
                                    last = v.tensor_scalar(
                                        MT[:, j * 512:(j + 1) * 512],
                                        QPOSB[:, qc0:qc0 + 512],
                                        KIDXC[:, kt:kt + 1], None, ALU.is_ge)
                            return last
                        p.cstep("vector", mbuild)

                        def mapply(v, kts=kts, mkts=mkts):
                            last = None
                            for j, kt in enumerate(kts):
                                if kt in mkts:
                                    last = v.tensor_tensor(
                                        PT[:, j * 512:(j + 1) * 512],
                                        PT[:, j * 512:(j + 1) * 512],
                                        MT[:, j * 512:(j + 1) * 512], ALU.mult)
                            return last
                        p.cstep("vector", mapply)

                    def zav(t, kts=kts, chl=chl, last_kt=last_kt):
                        last = None
                        for j, kt in enumerate(kts):
                            last = t.matmul(ZP, ONEBF[:, 0:1],
                                            PT[:, j * 512:(j + 1) * 512],
                                            start=(kt == 0), stop=(kt == last_kt))
                        for j, kt in enumerate(kts):
                            last = t.matmul(
                                OP, VF2[:, kt * 256 + chl * 128: kt * 256 + chl * 128 + 128],
                                PT[:, j * 512:(j + 1) * 512],
                                start=(kt == 0), stop=(kt == last_kt))
                        return last
                    p.cstep("tensor", zav, label="zav")
                p.cstep("vector", lambda v: v.reciprocal(ZROW, ZP))
                p.cstep("tensor", lambda t: t.matmul(PBC, ONE1[0:1, :], ZROW,
                                                     start=True, stop=True))
                p.cstep("vector", lambda v: v.tensor_copy(ZB, PBC))
                p.cstep("vector", lambda v: v.tensor_tensor(
                    PT[:, 0:512], OP, ZB, ALU.mult))
                p.dstep("sync", lambda e, h=h, qc0=qc0: [e.dma_start(
                    out=o_d.ap()[h * 128:(h + 1) * 128, qc0:qc0 + 512],
                    in_=PT[:, 0:512])], label="ost")

    OB = BF1
    p.dstep("sync", lambda e: [feat_load(p, e, OB, o_d.ap())], label="ldo")

    def ev_add_xn(og, banks):
        def f(v, banks=banks):
            last = None
            for (oc, th, pv) in banks:
                seg = XN[:, oc * TC + th * 512: oc * TC + th * 512 + 512]
                last = v.tensor_tensor(seg, seg, pv, ALU.add)
            return last
        p.cstep("vector", f, label="ev_add")
    emit_proj(p, wo_ap, OB, D, D, WSLAB, PS, ev_add_xn)

    X1, TB = F32A, BF1
    GSIL = BF2
    UCB = BF3
    RWB = SMB[:, 0:128]
    ACT2 = SMB[:, 128:1152]
    COMB = SMF[:, 0:64]
    RLOG = SMF[:, 64:128]
    RSIG = SMF[:, 128:192]
    RB0B = SMF[:, 192:256]
    M1 = SMF[:, 256:280]
    IDF = SMF[:, 280:408]
    OHALL = SMF[0:8, 408:1432]
    CT = SMF[0:8, 1432:2456]
    CB = SMF[:, 2456:3480]
    p.dstep("sync", lambda e: [e.dma_start(out=TAB[:, 0:16], in_=ln2_ap)],
            label="loadC")
    emit_idf_ohall(p, IDF, OHALL, IDI)
    emit_ln(p, X1, TB, UCB[:, 0:NCH * 512], None, None, RB,
            PS[0:1, 3584:4096], None, PS[:, 3584:4096], ONEBF, ONE1,
            eps=1e-6, gain=TAB[:, 0:16])
    emit_router(p, TB, rw0_ap, RWB, COMB, CT, RLOG, RSIG, RB0B, IDF, OHALL,
                PS, M1, "l0", bias_tiled_ap=rbt_ap)
    emit_moe_experts(p, TB, wg0_ap, wu0_ap, 8, CT, OHALL, CB,
                     GSIL[:, 0:4096], UCB[:, 0:4096], WSLAB, PS, acts0,
                     lambda e: e * 8, tag="c")

    def devacC(og2, banks):
        def f1(v, banks=banks):
            last = None
            for (oc, th, pv) in banks:
                seg = X1[:, oc * TC + th * 512: oc * TC + th * 512 + 512]
                last = v.tensor_tensor(seg, seg, pv, ALU.add)
            return last
        p.cstep("vector", f1)

        def f2(s, banks=banks):
            last = None
            for (oc, th, pv) in banks:
                seg = X1[:, oc * TC + th * 512: oc * TC + th * 512 + 512]
                last = s.activation(TB[:, oc * TC + th * 512:
                                       oc * TC + th * 512 + 512], seg, AF.Silu)
            return last
        p.cstep("scalar", f2)
    emit_moe_down(p, [(wd0_ap[e], 8, e * 8) for e in range(8)], acts0, ACT2,
                  WSLAB, PS, devacC, tag="cd")
    p.dstep("sync", lambda e: [feat_store(p, e, g0_d.ap(), TB)], label="g0st")

    X = F32A
    S0, G0, XN1B = BF1, BF2, BF3
    SCRD = SMF[:, 0:1024]
    HAL = SMF[:, 1024:1056]
    HALL = SMF[:, 1056:1312]
    LCOL = SMF[:, 1312:1328]
    RCOL = SMF[:, 1328:1344]
    TCV = SMB[:, 0:1024]
    p.dstep("sync", lambda e: [
        feat_load(p, e, X, x_in),
        feat_load(p, e, S0, s0_d.ap()),
        feat_load(p, e, G0, g0_d.ap()),
        e.dma_start(out=TAB[:, 0:48].rearrange("p (k c) -> p k c", c=16),
                    in_=cw_ap[:, :, :].rearrange("k p c -> p k c")),
        e.dma_start(out=TAB2[:, 0:16], in_=cb_ap),
    ], label="loadD", n=5)
    for c in range(NCH):
        p.cstep("vector", lambda v, c=c: v.tensor_tensor(
            SCRD[:, :], S0[:, c * TC:(c + 1) * TC], G0[:, c * TC:(c + 1) * TC],
            ALU.mult))
        p.cstep("vector", lambda v, c=c: v.tensor_tensor(
            X[:, c * TC:(c + 1) * TC], X[:, c * TC:(c + 1) * TC], SCRD[:, :],
            ALU.add))
    p.dstep("sync", lambda e: [feat_store(p, e, xl1_d.ap(), X)], label="xl1st")
    emit_ln(p, X, XN1B, S0[:, 0:NCH * 512], S0[:, NCH * 512:NCH * 1024],
            MB, RB, PS[0:1, 3584:4096], PS[0:1, 3072:3584], PS[:, 2048:2560],
            ONEBF, ONE1, eps=1e-5)

    def pack(v):
        last = None
        for c in range(NCH):
            v.tensor_copy(HAL[:, c:c + 1], X[:, c * TC:c * TC + 1])
            last = v.tensor_copy(HAL[:, 16 + c:17 + c],
                                 X[:, c * TC + TC - 1:c * TC + TC])
        return last
    p.cstep("vector", pack)
    p.dstep("gpsimd", lambda g: [g.dma_start(out=hal_in[:, :], in_=HAL)])
    p.xstep("gpsimd", lambda g: [g.collective_compute(
        "AllGather", ALU.bypass, replica_groups=[list(range(NCORES))],
        ins=[hal_in.ap().opt()], outs=[hal_out.ap().opt()])], label="ag_hal")
    p.dstep("gpsimd", lambda g: [g.dma_start(
        out=HALL.rearrange("p (r c) -> p r c", c=32),
        in_=hal_out.ap().rearrange("(r p) c -> p r c", p=128))])

    def zc(v):
        v.memset(LCOL[:, :], 0.0)
        return v.memset(RCOL[:, :], 0.0)
    p.cstep("vector", zc)
    for j in range(NCORES):
        p.cstep("vector", lambda v, j=j: v.scalar_tensor_tensor(
            LCOL[:, :], HALL[:, j * 32 + 16:j * 32 + 32], SELP[:, j:j + 1],
            LCOL[:, :], ALU.mult, ALU.add))
        p.cstep("vector", lambda v, j=j: v.scalar_tensor_tensor(
            RCOL[:, :], HALL[:, j * 32:j * 32 + 16], SELN[:, j:j + 1],
            RCOL[:, :], ALU.mult, ALU.add))
    for c in range(NCH):
        xc = X[:, c * TC:(c + 1) * TC]
        p.cstep("vector", lambda v, c=c, xc=xc: v.tensor_scalar(
            SCRD[:, :], xc, TAB[:, 16 + c:17 + c], TAB2[:, c:c + 1],
            ALU.mult, ALU.add))
        p.cstep("vector", lambda v, c=c, xc=xc: v.scalar_tensor_tensor(
            SCRD[:, 1:TC], xc[:, 0:TC - 1], TAB[:, c:c + 1], SCRD[:, 1:TC],
            ALU.mult, ALU.add))
        p.cstep("vector", lambda v, c=c: v.scalar_tensor_tensor(
            SCRD[:, 0:1], LCOL[:, c:c + 1], TAB[:, c:c + 1], SCRD[:, 0:1],
            ALU.mult, ALU.add))
        p.cstep("vector", lambda v, c=c, xc=xc: v.scalar_tensor_tensor(
            SCRD[:, 0:TC - 1], xc[:, 1:TC], TAB[:, 32 + c:33 + c],
            SCRD[:, 0:TC - 1], ALU.mult, ALU.add))
        p.cstep("vector", lambda v, c=c: v.scalar_tensor_tensor(
            SCRD[:, TC - 1:TC], RCOL[:, c:c + 1], TAB[:, 32 + c:33 + c],
            SCRD[:, TC - 1:TC], ALU.mult, ALU.add))
        p.cstep("vector", lambda v, c=c: v.tensor_copy(TCV, SCRD[:, :]))
        p.dstep("sync", lambda e, c=c: [e.dma_start(
            out=tout_d.ap()[c * 128:(c + 1) * 128, :], in_=TCV)],
            label=f"tst{c}")

    UB, GB = BF1, BF2
    H1B = SMB[:, 0:4096]
    TOUT2 = SMB[:, 4096:8192]
    ZEROSE = SMF[:, 0:1024]
    p.dstep("sync", lambda e: [
        e.dma_start(out=TAB[:, 0:16], in_=bu_ap),
        e.dma_start(out=TAB[:, 16:32], in_=bv_ap),
        e.dma_start(out=TAB[:, 32:36], in_=bg1_ap),
        e.dma_start(out=TAB[:, 36:52], in_=bg2_ap),
    ], label="loadE", n=4)
    p.cstep("vector", lambda v: v.memset(ZEROSE[:, :], 0.0))

    def ev_biasE(dst, bias_col_of, act=None):
        def f(og, banks):
            def e(eng, banks=banks):
                last = None
                for (oc, th, pv) in banks:
                    seg = dst[:, oc * TC + th * 512: oc * TC + th * 512 + 512]
                    if act is None:
                        last = eng.tensor_scalar(seg, pv, bias_col_of(oc), None,
                                                 ALU.add)
                    else:
                        last = eng.activation(seg, pv, act, bias=bias_col_of(oc))
                return last
            p.cstep("vector" if act is None else "scalar", e, label="evb")
        return f
    emit_proj(p, WuT_ap, XN1B, D, D, WSLAB, PS,
              ev_biasE(UB, lambda oc: TAB[:, oc:oc + 1]))
    emit_proj(p, Wg1T_ap, XN1B, D, 512, WSLAB, PS,
              ev_biasE(H1B, lambda oc: TAB[:, 32 + oc:33 + oc], act=AF.Relu))
    emit_proj(p, Wg2T_ap, H1B, 512, D, WSLAB, PS,
              ev_biasE(GB, lambda oc: TAB[:, 36 + oc:37 + oc], act=AF.Sigmoid))

    def ug(v):
        last = None
        for c in range(NCH):
            last = v.tensor_tensor(UB[:, c * TC:(c + 1) * TC],
                                   UB[:, c * TC:(c + 1) * TC],
                                   GB[:, c * TC:(c + 1) * TC], ALU.mult)
        return last
    p.cstep("vector", ug)

    def cum(v):
        last = None
        for c in range(NCH):
            last = v.tensor_tensor_scan(
                UB[:, c * TC:(c + 1) * TC], ZEROSE[:, :],
                UB[:, c * TC:(c + 1) * TC], 0.0, ALU.add, ALU.add)
        return last
    p.cstep("vector", cum)

    def agcp(v):
        last = None
        for c in range(NCH):
            last = v.tensor_copy(AGG[:, c:c + 1],
                                 UB[:, c * TC + TC - 1:c * TC + TC])
        return last
    p.cstep("vector", agcp)
    p.dstep("gpsimd", lambda g: [g.dma_start(out=agg_in.ap()[:, 0:16], in_=AGG[:, 0:16])])
    p.xstep("gpsimd", lambda g: [g.collective_compute(
        "AllGather", ALU.bypass, replica_groups=[list(range(NCORES))],
        ins=[agg_in.ap().opt()], outs=[agg_out.ap().opt()])], label="ag_cum")
    p.dstep("gpsimd", lambda g: [g.dma_start(
        out=AGALL[:, :].rearrange("p (r c) -> p r c", c=32),
        in_=agg_out.ap().rearrange("(r p) c -> p r c", p=128))])

    def cm(v):
        v.memset(CAR[:, :], 0.0)
        return v.memset(SST[:, :], 0.0)
    p.cstep("vector", cm)
    for j in range(NCORES):
        p.cstep("vector", lambda v, j=j: v.scalar_tensor_tensor(
            CAR[:, :], SST[:, :], SEL[:, j:j + 1], CAR[:, :], ALU.mult, ALU.add))
        p.cstep("vector", lambda v, j=j: v.tensor_tensor(
            SST[:, :], SST[:, :], AGALL[:, j * 32:j * 32 + 16], ALU.add))

    def ev_vg(og, banks):
        def f(v, banks=banks):
            last = None
            for (oc, th, pv) in banks:
                seg = GB[:, oc * TC + th * 512: oc * TC + th * 512 + 512]
                last = v.scalar_tensor_tensor(seg, pv, TAB[:, 16 + oc:17 + oc],
                                              seg, ALU.add, ALU.mult)
            return last
        p.cstep("vector", f, label="ev_vg")
    emit_proj(p, WvT_ap, XN1B, D, D, WSLAB, PS, ev_vg)

    def go(v):
        last = None
        for c in range(NCH):
            last = v.scalar_tensor_tensor(
                GB[:, c * TC:(c + 1) * TC], UB[:, c * TC:(c + 1) * TC],
                CAR[:, c:c + 1], GB[:, c * TC:(c + 1) * TC], ALU.add, ALU.mult)
        return last
    p.cstep("vector", go)

    def ev_uv(og, banks):
        ocs = sorted({oc for (oc, th, pv) in banks})
        p.dstep("sync", lambda e, ocs=ocs: [e.dma_start(
            out=TOUT2[:, 0:len(ocs) * TC].rearrange("p (c t) -> p c t", t=TC),
            in_=tout_d.ap()[ocs[0] * 128:(ocs[0] + len(ocs)) * 128, :]
            .rearrange("(c p) t -> p c t", p=128))], label="t2ld")

        def f(v, banks=banks, ocs=ocs):
            last = None
            for (oc, th, pv) in banks:
                lc = oc - ocs[0]
                seg = TOUT2[:, lc * TC + th * 512: lc * TC + th * 512 + 512]
                last = v.tensor_tensor(seg, seg, pv, ALU.add)
            return last
        p.cstep("vector", f, label="uv_add")
        p.dstep("sync", lambda e, ocs=ocs: [e.dma_start(
            out=s1_d.ap()[ocs[0] * 128:(ocs[0] + len(ocs)) * 128, :]
            .rearrange("(c p) t -> p c t", p=128),
            in_=TOUT2[:, 0:len(ocs) * TC].rearrange("p (c t) -> p c t", t=TC))],
            label="s1st")
    emit_proj(p, WuT_ap, GB, D, D, WSLAB, PS, ev_uv)

    XL1 = F32A
    S1 = BF1
    GSILF = BF2[:, 0:4096]
    UCBF = BF2[:, 4096:8192]
    SGB = SMF[:, 3480:4504]
    SROW = SMF[0:1, 2456:3480]
    WSGB = SMB[:, 1152:1168]
    p.dstep("sync", lambda e: [
        feat_load(p, e, XL1, xl1_d.ap()),
        feat_load(p, e, S1, s1_d.ap()),
    ], label="loadF", n=2)
    p.dstep("gpsimd", lambda g: [g.dma_start(out=WSGB, in_=wsg_ap)],
            label="wsg_dma")
    emit_idf_ohall(p, IDF, OHALL, IDI)
    emit_router(p, XN1B, wr1_ap, RWB, COMB, CT, RLOG, RSIG, RB0B, IDF, OHALL,
                PS, M1, "l1")

    def mmsg(t):
        last = None
        for th in range(2):
            for kt in range(NCH):
                last = t.matmul(PS[0:1, (7 - th) * 512:(7 - th) * 512 + 512],
                                WSGB[:, kt:kt + 1],
                                XN1B[:, kt * TC + th * 512:kt * TC + th * 512 + 512],
                                start=(kt == 0), stop=(kt == NCH - 1))
        return last
    p.cstep("tensor", mmsg)
    p.cstep("scalar", lambda s: s.activation(SROW[0:1, 0:512],
                                             PS[0:1, 3584:4096], AF.Sigmoid))
    p.cstep("scalar", lambda s: s.activation(SROW[0:1, 512:1024],
                                             PS[0:1, 3072:3584], AF.Sigmoid))
    for th in range(2):
        p.cstep("tensor", lambda t, th=th: t.matmul(
            PS[:, 2048:2560], ONE1[0:1, :],
            SROW[0:1, th * 512:(th + 1) * 512], start=True, stop=True))
        p.cstep("vector", lambda v, th=th: v.tensor_copy(
            SGB[:, th * 512:(th + 1) * 512], PS[:, 2048:2560]))
    emit_moe_experts(p, XN1B, wg1_ap, wu1_ap, 11, CT, OHALL, CB,
                     GSILF, UCBF, WSLAB, PS, acts1, lambda e: e * 11, tag="f")
    emit_shared_expert(p, XN1B, sg_ap, su_ap, SGB, GSILF, UCBF, WSLAB, PS,
                       acts1, 88, tag="fs")

    def devacF(og2, banks):
        def f1(s, banks=banks):
            last = None
            for bi, (oc, th, pv) in enumerate(banks):
                last = s.activation(GSILF[:, bi * 512:(bi + 1) * 512], pv, AF.Silu)
            return last
        p.cstep("scalar", f1)

        def f2(v, banks=banks):
            last = None
            for bi, (oc, th, pv) in enumerate(banks):
                last = v.tensor_tensor(
                    UCBF[:, bi * 512:(bi + 1) * 512],
                    GSILF[:, bi * 512:(bi + 1) * 512],
                    S1[:, oc * TC + th * 512: oc * TC + th * 512 + 512], ALU.mult)
            return last
        p.cstep("vector", f2)

        def f3(v, banks=banks):
            last = None
            for bi, (oc, th, pv) in enumerate(banks):
                seg = XL1[:, oc * TC + th * 512: oc * TC + th * 512 + 512]
                last = v.tensor_tensor(seg, seg, UCBF[:, bi * 512:(bi + 1) * 512],
                                       ALU.add)
            return last
        p.cstep("vector", f3)
    emit_moe_down(p, [(wd1_ap[e], 11, e * 11) for e in range(8)]
                  + [(sd_ap, 44, 88)], acts1, ACT2, WSLAB, PS, devacF, tag="fd")

    p.dstep("sync", lambda e: [feat_store(p, e, out_out, XL1)], label="outst")
    nc = p.build()
    return p, nc


def kernel(**inputs):
    g, pc = _prep_host(inputs)
    p, nc = build_fused()
    in_maps = []
    for i in range(NCORES):
        m = dict(x=pc[i]["x"], igw=g["igw"], rgw=g["rgw"], igb=g["igb"],
                 rgb=g["rgb"], rp=g["rp"], sel=pc[i]["sel"], selp=pc[i]["selp"],
                 seln=pc[i]["seln"], first_mask=pc[i]["first_mask"],
                 wq=g["wq"], wk=g["wk"], wv=g["wv"], wo=g["wo"],
                 ln1=g["ln1"], ln2=g["ln2"], qnw=g["qnw"], knw=g["knw"],
                 cosr=pc[i]["cosr"], sinsgn=pc[i]["sinsgn"], swapm=g["swapm"],
                 qpos=pc[i]["qpos"], rw0=g["rw0"], rb0t=g["rb0t"],
                 wg0=g["wg0"], wu0=g["wu0"], wd0=g["wd0"],
                 cw=g["cw"], cb=g["cb"], WuT=g["WuT"], WvT=g["WvT"],
                 Wg1T=g["Wg1T"], Wg2T=g["Wg2T"], bu=g["bu"], bv=g["bv"],
                 bg1=g["bg1"], bg2=g["bg2"], Wr1=g["Wr1"],
                 wg1=g["wg1"], wu1=g["wu1"], wd1=g["wd1"],
                 Sg=g["Sg"], Su=g["Su"], Sd=g["Sd"], Wsg=g["Wsg"])
        in_maps.append(m)
    res = run_bass_kernel_spmd(nc, in_maps, core_ids=list(range(NCORES)))
    p.close()
    out = np.concatenate([np.asarray(res.results[i]["out"], np.float32)
                          for i in range(NCORES)], axis=1)
    return np.ascontiguousarray(out.T)[None]


# version 03
# speedup vs baseline: 1.7313x; 1.7313x over previous
import numpy as np
import ml_dtypes
from contextlib import ExitStack

import concourse.bass as bass
import concourse.mybir as mybir
from concourse.bass_utils import run_bass_kernel_spmd

F32 = mybir.dt.float32
BF16 = mybir.dt.bfloat16
AF = mybir.ActivationFunctionType
ALU = mybir.AluOpType
AX = mybir.AxisListType

NCORES = 8
TC = 1024
D = 2048
NCH = 16
BFNP = ml_dtypes.bfloat16
ENGINES = ("sync", "gpsimd", "vector", "scalar", "tensor")


class Step:
    __slots__ = ("idx", "engine", "fn", "deps", "kind", "label", "_ndma")

    def __init__(self, idx, engine, fn, deps, kind, label):
        self.idx, self.engine, self.fn = idx, engine, fn
        self.deps, self.kind, self.label = deps, kind, label
        self._ndma = 0


class P:

    def __init__(self):
        self.nc = bass.Bass("TRN2", target_bir_lowering=False, debug=True)
        self.es = ExitStack()
        self.steps = []
        self._names = set()

    def din(self, name, shape, dtype=F32):
        return self.nc.dram_tensor(name, list(shape), dtype, kind="ExternalInput").ap()

    def dout(self, name, shape, dtype=F32):
        return self.nc.dram_tensor(name, list(shape), dtype, kind="ExternalOutput").ap()

    def dint(self, name, shape, dtype=F32):
        return self.nc.dram_tensor(name, list(shape), dtype)

    def sb(self, name, shape, dtype=F32):
        assert name not in self._names, name
        self._names.add(name)
        return self.es.enter_context(self.nc.sbuf_tensor(name, list(shape), dtype))

    def ps(self, name, shape, dtype=F32):
        assert name not in self._names, name
        self._names.add(name)
        return self.es.enter_context(self.nc.psum_tensor(name, list(shape), dtype))

    def cstep(self, engine, fn, deps="all", label=""):
        s = Step(len(self.steps), engine, fn, deps, "c", label)
        self.steps.append(s)
        return s

    def dstep(self, engine, fn, deps="all", label="", n=None):
        s = Step(len(self.steps), engine, fn, deps, "d", label)
        s._ndma = 1 if n is None else n
        self.steps.append(s)
        return s

    def xstep(self, engine, fn, deps="all", label="", n=1):
        s = Step(len(self.steps), engine, fn, deps, "x", label)
        s._ndma = n
        self.steps.append(s)
        return s

    def build(self):
        used = {(st.engine, st.kind) for st in self.steps if st.kind != "x"}
        chans = [(e, k) for e in ENGINES for k in ("c", "d")
                 if (e, k) in used]
        for st in self.steps:
            if st.kind == "x":
                chans.append((st.engine, f"x{st.idx}"))
        sems = {ch: self.es.enter_context(
            self.nc.semaphore(f"sem_{ch[0]}_{ch[1]}")) for ch in chans}
        blk = self.es.enter_context(self.nc.Block())

        counts = {ch: 0 for ch in chans}
        after = []
        waits = []
        for st in self.steps:
            mych = (st.engine, st.kind if st.kind != "x" else f"x{st.idx}")
            if st.deps == "all":
                tg = {ch: counts[ch] for ch in chans if counts[ch] > 0}
            else:
                tg = {}
                for d in st.deps:
                    for ch in chans:
                        v = after[d.idx][ch]
                        if v > tg.get(ch, 0):
                            tg[ch] = v
                for ch in chans:
                    if ch[0] == st.engine and counts.get(ch, 0) > tg.get(ch, 0):
                        tg[ch] = counts[ch]
            waits.append(tg)
            if st.kind == "c":
                counts[mych] += 1
            elif st.kind == "x":
                counts[mych] += 1
            else:
                assert st._ndma is not None, f"step {st.idx} {st.label}: dstep needs n="
                counts[mych] += 16 * st._ndma
            after.append(dict(counts))

        by_eng = {e: [st for st in self.steps if st.engine == e] for e in ENGINES}
        for e in ENGINES:
            if not by_eng[e] and e != "sync":
                continue

            def section(eng, e=e):
                for st in by_eng[e]:
                    for ch, v in waits[st.idx].items():
                        eng.wait_ge(sems[ch], v)
                    if st.kind == "c":
                        last = st.fn(eng)
                        assert last is not None, (
                            f"step {st.idx} {st.label}: fn must return last instr")
                        last.then_inc(sems[(e, "c")], 1)
                    elif st.kind == "x":
                        instrs = st.fn(eng)
                        assert len(instrs) == 1, "one collective per xstep"
                        instrs[0].then_inc(sems[(e, f"x{st.idx}")])
                    else:
                        instrs = st.fn(eng)
                        assert len(instrs) == st._ndma, (
                            f"step {st.idx} {st.label}: declared n={st._ndma} "
                            f"!= emitted {len(instrs)}")
                        for ins in instrs:
                            ins.then_inc(sems[(e, "d")], 16)
                if e == "sync":
                    for ch in chans:
                        if counts[ch] > 0:
                            eng.wait_ge(sems[ch], counts[ch])
            getattr(blk, e)(section)

        return self.nc

    def close(self):
        self.es.close()


def chunkc(ap, c, width=TC):
    return ap[:, c * width:(c + 1) * width]




def _prep_host(inputs):
    g = {}
    pc = [dict() for _ in range(NCORES)]

    x = np.ascontiguousarray(np.asarray(inputs["L0_attn_x"], np.float32)[0].T)
    for i in range(NCORES):
        pc[i]["x"] = np.ascontiguousarray(x[:, i * TC:(i + 1) * TC])

    A = lambda k: np.ascontiguousarray(np.asarray(inputs[k], np.float32))
    g["igw"] = A("L0_attn_input_gate_weight")
    g["rgw"] = A("L0_attn_recurrent_gate_weight")
    g["igb"] = np.ascontiguousarray(A("L0_attn_input_gate_bias").T)
    g["rgb"] = np.ascontiguousarray(A("L0_attn_recurrent_gate_bias").T)
    g["rp"] = np.ascontiguousarray(A("L0_attn_recurrent_param").reshape(16, 128).T)

    g["ln1"] = np.ascontiguousarray(A("L0_ffn_ln1_w").reshape(16, 128).T)
    g["ln2"] = np.ascontiguousarray(A("L0_ffn_ln2_w").reshape(16, 128).T)
    g["qnw"] = np.ascontiguousarray(A("L0_ffn_qnorm_w").reshape(16, 128).T)
    g["knw"] = np.ascontiguousarray(A("L0_ffn_knorm_w").reshape(4, 128).T)
    g["wq"] = A("L0_ffn_wq")
    g["wk"] = A("L0_ffn_wk")
    g["wv"] = A("L0_ffn_wv")
    g["wo"] = A("L0_ffn_wo")
    g["rw0"] = A("L0_ffn_router_w")
    g["rb0t"] = np.ascontiguousarray(
        np.tile(A("L0_ffn_router_bias").reshape(1, 8), (128, 8)))
    g["wg0"] = A("L0_ffn_wg")
    g["wu0"] = A("L0_ffn_wu")
    g["wd0"] = A("L0_ffn_wd")

    g["WuT"] = np.ascontiguousarray(A("L1_attn_Wu").T)
    g["WvT"] = np.ascontiguousarray(A("L1_attn_Wv").T)
    g["Wg1T"] = np.ascontiguousarray(A("L1_attn_Wg1").T)
    g["Wg2T"] = np.ascontiguousarray(A("L1_attn_Wg2").T)
    g["bu"] = np.ascontiguousarray(A("L1_attn_bu").reshape(16, 128).T)
    g["bv"] = np.ascontiguousarray(A("L1_attn_bv").reshape(16, 128).T)
    g["bg1"] = np.ascontiguousarray(A("L1_attn_bg1").reshape(4, 128).T)
    g["bg2"] = np.ascontiguousarray(A("L1_attn_bg2").reshape(16, 128).T)
    g["cw"] = np.ascontiguousarray(
        A("L1_attn_conv_w")[:, 0, :].reshape(16, 128, 3).transpose(2, 1, 0))
    g["cb"] = np.ascontiguousarray(A("L1_attn_conv_b").reshape(16, 128).T)

    g["Wr1"] = A("L1_ffn_Wr")
    g["wg1"] = A("L1_ffn_Wg")
    g["wu1"] = A("L1_ffn_Wu")
    g["wd1"] = A("L1_ffn_Wd")
    g["Sg"] = A("L1_ffn_Sg")
    g["Su"] = A("L1_ffn_Su")
    g["Sd"] = A("L1_ffn_Sd")
    g["Wsg"] = np.ascontiguousarray(
        A("L1_ffn_Wsg").reshape(16, 128).T)

    swapm = np.zeros((64, 64), np.float32)
    for m in range(64):
        swapm[(m + 32) % 64, m] = 1.0
    g["swapm"] = swapm.astype(BFNP)

    theta = 1e6
    inv = 1.0 / theta ** (np.arange(0, 64, 2, dtype=np.float32) / 64)
    for i in range(NCORES):
        pos = (i * TC + np.arange(TC)).astype(np.float32)
        ang = pos[:, None] * inv[None, :]
        emb = np.concatenate([ang, ang], 1)
        pc[i]["cosr"] = np.ascontiguousarray(np.cos(emb).T.astype(np.float32))
        sint = np.sin(emb).T.astype(np.float32)
        sinsgn = sint.copy()
        sinsgn[0:32] = -sint[0:32]
        pc[i]["sinsgn"] = np.ascontiguousarray(sinsgn)
        pc[i]["qpos"] = pos.reshape(1, TC).copy()
        sel = np.zeros((128, 8), np.float32)
        sel[:, i] = 1.0
        pc[i]["sel"] = sel
        selp = np.zeros((128, 8), np.float32)
        if i > 0:
            selp[:, i - 1] = 1.0
        pc[i]["selp"] = selp
        seln = np.zeros((128, 8), np.float32)
        if i < NCORES - 1:
            seln[:, i + 1] = 1.0
        pc[i]["seln"] = seln
        fm = np.zeros((128, 1), np.float32)
        if i == 0:
            fm[:, 0] = 1.0
        pc[i]["first_mask"] = fm
    return g, pc




def feat_store(p, eng, dram_ap, sbuf, dtype_elems=TC):
    return eng.dma_start(
        out=dram_ap.rearrange("(c p) t -> p c t", p=128),
        in_=sbuf[:, :].rearrange("p (c t) -> p c t", t=dtype_elems))


def feat_load(p, eng, sbuf, dram_ap, dtype_elems=TC):
    return eng.dma_start(
        out=sbuf[:, :].rearrange("p (c t) -> p c t", t=dtype_elems),
        in_=dram_ap.rearrange("(c p) t -> p c t", p=128))


def emit_ln(p, X, XNB, SQB, XB, MB, RB, PSQ, PSS, PSB, ONEBF, ONE1,
            eps=1e-5, gain=None, out_f32_inplace=True, nch=NCH, width=TC):
    dim = 128 * nch
    for th in range(2):
        def sq(s, th=th):
            last = None
            for c in range(nch):
                last = s.activation(SQB[:, c * 512:(c + 1) * 512],
                                    X[:, c * width + th * 512: c * width + th * 512 + 512],
                                    AF.Square)
            return last
        p.cstep("scalar", sq, label="ln_sq")
        if gain is None:
            def cp(v, th=th):
                last = None
                for c in range(nch):
                    last = v.tensor_copy(XB[:, c * 512:(c + 1) * 512],
                                         X[:, c * width + th * 512: c * width + th * 512 + 512])
                return last
            p.cstep("vector", cp, label="ln_cp")

        def mm(t, th=th):
            last = None
            for c in range(nch):
                last = t.matmul(PSQ[:], ONEBF[:, 0:1], SQB[:, c * 512:(c + 1) * 512],
                                start=(c == 0), stop=(c == nch - 1))
            if gain is None:
                for c in range(nch):
                    last = t.matmul(PSS[:], ONEBF[:, 0:1], XB[:, c * 512:(c + 1) * 512],
                                    start=(c == 0), stop=(c == nch - 1))
            return last
        p.cstep("tensor", mm, label="ln_mm")

        if gain is None:
            def st1(v, th=th):
                return v.tensor_scalar(MB[0:1, 0:512], PSS[:], 1.0 / dim, None, ALU.mult)
            p.cstep("vector", st1, label="ln_mean")

            def st2(s, th=th):
                return s.activation(RB[0:1, 0:512], MB[0:1, 0:512], AF.Square)
            p.cstep("scalar", st2, label="ln_m2")

            def st3(v, th=th):
                return v.scalar_tensor_tensor(RB[0:1, 0:512], PSQ[:], 1.0 / dim,
                                              RB[0:1, 0:512], ALU.mult, ALU.subtract)
            p.cstep("vector", st3, label="ln_var")
            p.cstep("vector", lambda v: v.tensor_scalar(
                RB[0:1, 0:512], RB[0:1, 0:512], eps, None, ALU.add), label="ln_eps")
        else:
            def st3(v, th=th):
                return v.tensor_scalar(RB[0:1, 0:512], PSQ[:], 1.0 / dim, eps,
                                       ALU.mult, ALU.add)
            p.cstep("vector", st3, label="ln_var")

        def st4(s, th=th):
            return s.activation(RB[0:1, 0:512], RB[0:1, 0:512], AF.Sqrt)
        p.cstep("scalar", st4, label="ln_sqrt")

        def st5(v, th=th):
            return v.reciprocal(RB[0:1, 0:512], RB[0:1, 0:512])
        p.cstep("vector", st5, label="ln_recip")

        def bc1(t, th=th):
            return t.matmul(PSB[:], ONE1[0:1, :], RB[0:1, 0:512], start=True, stop=True)
        p.cstep("tensor", bc1, label="ln_bc_r")

        def cpr(v, th=th):
            return v.tensor_copy(RB[:, 0:512], PSB[:])
        p.cstep("vector", cpr, label="ln_cp_r")

        if gain is None:
            def bc2(t, th=th):
                return t.matmul(PSB[:], ONE1[0:1, :], MB[0:1, 0:512], start=True, stop=True)
            p.cstep("tensor", bc2, label="ln_bc_m")

            def cpm(v, th=th):
                return v.tensor_copy(MB[:, 0:512], PSB[:])
            p.cstep("vector", cpm, label="ln_cp_m")

        def segf(c, th):
            return X[:, c * width + th * 512: c * width + th * 512 + 512]

        def segb(c, th):
            return XNB[:, c * width + th * 512: c * width + th * 512 + 512]
        if gain is None:
            def n1(v, th=th):
                last = None
                for c in range(nch):
                    last = v.scalar_tensor_tensor(segf(c, th), segf(c, th), 1.0,
                                                  MB[:, 0:512], ALU.mult, ALU.subtract)
                return last
            p.cstep("vector", n1, label="ln_n1")

            def n2(v, th=th):
                last = None
                for c in range(nch):
                    last = v.tensor_tensor(segf(c, th), segf(c, th), RB[:, 0:512], ALU.mult)
                return last
            p.cstep("vector", n2, label="ln_n2")

            def n3(v, th=th):
                last = None
                for c in range(nch):
                    last = v.tensor_copy(segb(c, th), segf(c, th))
                return last
            p.cstep("vector", n3, label="ln_n3")
        else:
            def n1(v, th=th):
                last = None
                for c in range(nch):
                    last = v.scalar_tensor_tensor(segb(c, th), segf(c, th),
                                                  gain[:, c:c + 1], RB[:, 0:512],
                                                  ALU.mult, ALU.mult)
                return last
            p.cstep("vector", n1, label="ln_rms")


def emit_softplus8(p, TAB, SCR):
    p.cstep("scalar", lambda s: (s.activation(TAB[:, 48:64], TAB[:, 32:48], AF.Abs),
                                 s.activation(SCR[:, 32:48], TAB[:, 32:48], AF.Relu))[-1])
    p.cstep("scalar", lambda s: s.activation(SCR[:, 0:16], TAB[:, 48:64], AF.Exp, scale=-1.0))
    p.cstep("scalar", lambda s: s.activation(SCR[:, 16:32], SCR[:, 0:16], AF.Ln, bias=1.0))
    p.cstep("vector", lambda v: v.scalar_tensor_tensor(
        TAB[:, 48:64], SCR[:, 16:32], 1.0, SCR[:, 32:48], ALU.mult, ALU.add))
    p.cstep("vector", lambda v: v.tensor_scalar(
        TAB[:, 48:64], TAB[:, 48:64], -8.0, None, ALU.mult))


def emit_scan_branch(p, X, XNB, ALOC, YLOC, SCR, ZEROS, TAB, SEL, FMASK,
                     WGT, AGG, AGALL, CAR, SST, PG0, PG1,
                     igw_ap, rgw_ap, agg_in, agg_out):
    s_gx, s_ga = 0, 1024
    s_a, s_m, s_b = 2048, 3072, 1024
    for h in range(16):
        p.dstep("gpsimd", lambda g, h=h: [
            g.dma_start(out=WGT[:, :], in_=igw_ap[h])], label=f"gw{h}")

        def mm1(t, h=h):
            t.matmul(PG0[:], WGT[:, :], XNB[:, h * TC:h * TC + 512],
                     start=True, stop=True)
            return t.matmul(PG1[:], WGT[:, :], XNB[:, h * TC + 512:h * TC + 1024],
                            start=True, stop=True)
        p.cstep("tensor", mm1)

        def ev1(s, h=h):
            s.activation(SCR[:, s_gx:s_gx + 512], PG0[:], AF.Sigmoid,
                         bias=TAB[:, h:h + 1])
            return s.activation(SCR[:, s_gx + 512:s_gx + 1024], PG1[:], AF.Sigmoid,
                                bias=TAB[:, h:h + 1])
        p.cstep("scalar", ev1)

        p.dstep("gpsimd", lambda g, h=h: [
            g.dma_start(out=WGT[:, :], in_=rgw_ap[h])], label=f"rw{h}")

        def mm2(t, h=h):
            t.matmul(PG0[:], WGT[:, :], XNB[:, h * TC:h * TC + 512],
                     start=True, stop=True)
            return t.matmul(PG1[:], WGT[:, :], XNB[:, h * TC + 512:h * TC + 1024],
                            start=True, stop=True)
        p.cstep("tensor", mm2)

        def ev2(s, h=h):
            s.activation(SCR[:, s_ga:s_ga + 512], PG0[:], AF.Sigmoid,
                         bias=TAB[:, 16 + h:17 + h])
            return s.activation(SCR[:, s_ga + 512:s_ga + 1024], PG1[:], AF.Sigmoid,
                                bias=TAB[:, 16 + h:17 + h])
        p.cstep("scalar", ev2)

        def e1(v, h=h):
            return v.tensor_scalar(SCR[:, s_a:s_a + 1024], SCR[:, s_ga:s_ga + 1024],
                                   TAB[:, 48 + h:49 + h], None, ALU.mult)
        p.cstep("vector", e1)

        def e2(s, h=h):
            return s.activation(SCR[:, s_a:s_a + 1024], SCR[:, s_a:s_a + 1024], AF.Exp)
        p.cstep("scalar", e2)

        def e3(s, h=h):
            return s.activation(SCR[:, s_m:s_m + 1024], SCR[:, s_a:s_a + 1024], AF.Square)
        p.cstep("scalar", e3)

        def e4(v, h=h):
            return v.tensor_scalar(SCR[:, s_m:s_m + 1024], SCR[:, s_m:s_m + 1024],
                                   -1.0, 1.0, ALU.mult, ALU.add)
        p.cstep("vector", e4)

        def e5(s, h=h):
            return s.activation(SCR[:, s_m:s_m + 1024], SCR[:, s_m:s_m + 1024], AF.Sqrt)
        p.cstep("scalar", e5)

        p.cstep("vector", lambda v, h=h: v.tensor_scalar(
            SCR[:, s_b:s_b + 1], FMASK[:, 0:1], -1.0, 1.0, ALU.mult, ALU.add))
        p.cstep("vector", lambda v, h=h: v.scalar_tensor_tensor(
            SCR[:, s_m:s_m + 1], SCR[:, s_m:s_m + 1], SCR[:, s_b:s_b + 1],
            FMASK[:, 0:1], ALU.mult, ALU.add))
        p.cstep("vector", lambda v, h=h: v.tensor_tensor(
            SCR[:, s_b:s_b + 1024], X[:, h * TC:(h + 1) * TC],
            SCR[:, s_gx:s_gx + 1024], ALU.mult))
        p.cstep("vector", lambda v, h=h: v.tensor_tensor(
            SCR[:, s_b:s_b + 1024], SCR[:, s_b:s_b + 1024],
            SCR[:, s_m:s_m + 1024], ALU.mult))

        def e7(v, h=h):
            v.tensor_tensor_scan(ALOC[:, h * TC:(h + 1) * TC],
                                 SCR[:, s_a:s_a + 1024], ZEROS[:, 0:1024],
                                 1.0, ALU.mult, ALU.add)
            return v.tensor_tensor_scan(
                YLOC[:, h * TC:(h + 1) * TC], SCR[:, s_a:s_a + 1024],
                SCR[:, s_b:s_b + 1024], 0.0, ALU.mult, ALU.add)
        p.cstep("vector", e7)

        def e8(v, h=h):
            v.tensor_copy(AGG[:, h:h + 1], ALOC[:, h * TC + 1023:h * TC + 1024])
            return v.tensor_copy(AGG[:, 16 + h:17 + h],
                                 YLOC[:, h * TC + 1023:h * TC + 1024])
        p.cstep("vector", e8)

    p.dstep("gpsimd", lambda g: [g.dma_start(out=agg_in[:, :], in_=AGG[:, :])],
            label="agg_out_dma")
    import os
    if os.environ.get("NO_CC"):
        p.dstep("gpsimd", lambda g: [g.dma_start(
            out=agg_out[0:128, :], in_=agg_in[:, :])], label="agg_fake")
    else:
        p.xstep("gpsimd", lambda g: [g.collective_compute(
            "AllGather", ALU.bypass, replica_groups=[list(range(NCORES))],
            ins=[agg_in[:, :]], outs=[agg_out[:, :]])], label="agg_ag")
    p.dstep("gpsimd", lambda g: [g.dma_start(
        out=AGALL[:, :].rearrange("p (r c) -> p r c", c=32),
        in_=agg_out.ap().rearrange("(r p) c -> p r c", p=128))],
        label="agg_in_dma")

    def cm(v):
        v.memset(CAR[:, :], 0.0)
        return v.memset(SST[:, :], 0.0)
    p.cstep("vector", cm)
    for j in range(NCORES):
        p.cstep("vector", lambda v, j=j: v.scalar_tensor_tensor(
            CAR[:, :], SST[:, :], SEL[:, j:j + 1], CAR[:, :], ALU.mult, ALU.add))
        p.cstep("vector", lambda v, j=j: v.tensor_tensor(
            SST[:, :], SST[:, :], AGALL[:, j * 32:j * 32 + 16], ALU.mult))
        p.cstep("vector", lambda v, j=j: v.tensor_tensor(
            SST[:, :], SST[:, :], AGALL[:, j * 32 + 16:j * 32 + 32], ALU.add))

    def corr(v):
        last = None
        for h in range(16):
            last = v.scalar_tensor_tensor(
                YLOC[:, h * TC:(h + 1) * TC], ALOC[:, h * TC:(h + 1) * TC],
                CAR[:, h:h + 1], YLOC[:, h * TC:(h + 1) * TC], ALU.mult, ALU.add)
        return last
    p.cstep("vector", corr, label="carry_corr")


def build_launch_A(dbg=False):
    p = P()
    x_in = p.din("x", [D, TC])
    igw_ap = p.din("igw", [16, 128, 128])
    rgw_ap = p.din("rgw", [16, 128, 128])
    igb_ap = p.din("igb", [128, 16])
    rgb_ap = p.din("rgb", [128, 16])
    rp_ap = p.din("rp", [128, 16])
    sel_ap = p.din("sel", [128, 8])
    fm_ap = p.din("first_mask", [128, 1])
    agg_in = p.dint("agg_in", [128, 32])
    agg_out = p.dint("agg_out", [128 * NCORES, 32])
    xn_out = p.dout("xn", [D, TC])
    s0_out = p.dout("s0", [D, TC], BF16)

    X = p.sb("X", [128, NCH * TC])
    XNB = p.sb("XNB", [128, NCH * TC], BF16)
    ALOC = p.sb("ALOC", [128, NCH * TC], BF16)
    YLOC = p.sb("YLOC", [128, NCH * TC], BF16)
    SCR = p.sb("SCR", [128, 5 * 1024])
    ZEROS = p.sb("ZEROS", [128, 1024])
    MB = p.sb("MB", [128, 512])
    RB = p.sb("RB", [128, 512])
    TAB = p.sb("TAB", [128, 64])
    SEL = p.sb("SEL", [128, 8])
    FMASK = p.sb("FMASK", [128, 1])
    ONEBF = p.sb("ONEBF", [128, 1], BF16)
    ONE1 = p.sb("ONE1", [1, 128])
    WGT = p.sb("WGT", [128, 128], BF16)
    AGG = p.sb("AGG", [128, 32])
    AGALL = p.sb("AGALL", [128, 256])
    CAR = p.sb("CAR", [128, 16])
    SST = p.sb("SST", [128, 16])
    PSQ = p.ps("PSQ", [1, 512])
    PSS = p.ps("PSS", [1, 512])
    PSB = p.ps("PSB", [128, 512])
    PG0 = p.ps("PG0", [128, 512])
    PG1 = p.ps("PG1", [128, 512])

    p.dstep("sync", lambda e: [
        feat_load(p, e, X, x_in),
        e.dma_start(out=TAB[:, 0:16], in_=igb_ap),
        e.dma_start(out=TAB[:, 16:32], in_=rgb_ap),
        e.dma_start(out=TAB[:, 32:48], in_=rp_ap),
        e.dma_start(out=SEL[:, :], in_=sel_ap),
        e.dma_start(out=FMASK[:, :], in_=fm_ap),
    ], label="load", n=6)

    def init(v):
        v.memset(ZEROS[:, :], 0.0)
        v.memset(ONEBF[:, :], 1.0)
        return v.memset(ONE1[:, :], 1.0)
    p.cstep("vector", init, label="init")

    import os as _os
    emit_softplus8(p, TAB, SCR)
    if not _os.environ.get("NO_LN"):
        emit_ln(p, X, XNB, ALOC[:, 0:NCH * 512], YLOC[:, 0:NCH * 512],
                MB, RB, PSQ, PSS, PSB, ONEBF, ONE1)
    if _os.environ.get("NO_GATES"):
        def fake(v):
            return v.tensor_copy(AGG[:, 0:32], AGALL[:, 0:32])
        p.cstep("vector", fake)
        p.dstep("gpsimd", lambda g: [g.dma_start(out=agg_in[:, :], in_=AGG[:, :])])
        p.xstep("gpsimd", lambda g: [g.collective_compute(
            "AllGather", ALU.bypass, replica_groups=[list(range(NCORES))],
            ins=[agg_in.ap().opt()], outs=[agg_out.ap().opt()])], label="agg_ag")
        p.dstep("gpsimd", lambda g: [g.dma_start(
            out=AGALL[:, :].rearrange("p (r c) -> p r c", c=32),
            in_=agg_out.ap().rearrange("(r p) c -> p r c", p=128))])
    else:
        emit_scan_branch(p, X, XNB, ALOC, YLOC, SCR, ZEROS, TAB, SEL, FMASK,
                         WGT, AGG, AGALL, CAR, SST, PG0, PG1,
                         igw_ap, rgw_ap, agg_in, agg_out)

    import os as _os2
    if _os2.environ.get("NO_BF16STORE"):
        p.dstep("sync", lambda e: [feat_store(p, e, xn_out, X)], label="store")
        p.dstep("sync", lambda e: [e.dma_start(out=s0_out.rearrange(
            "(c p) t -> p c t", p=128)[:, 0:1, :],
            in_=YLOC[:, 0:TC].rearrange("p (c t) -> p c t", t=TC))], label="store2")
    else:
        p.dstep("sync", lambda e: [
            feat_store(p, e, xn_out, X),
            feat_store(p, e, s0_out, YLOC),
        ], label="store", n=2)

    nc = p.build()
    return p, nc




def emit_proj(p, w_ap, rhs_bf, Din, Dout, WB, PS, evac_fn):
    prev_evac = None
    for og in range(Dout // 512):
        ocs = list(range(og * 4, og * 4 + 4))
        banks, last_mm = emit_proj_group(p, w_ap, rhs_bf, ocs, Din, WB, PS,
                                         tag=f"p{og}", prev_evac=prev_evac)
        prev_evac = evac_fn(og, banks, [last_mm])


def emit_rownorm_scale(p, QB, gaintab, nch_q, PS, SQB, RROW, RBC, ONEBF, ONE1,
                       eps, tag, extra_scale=1.0):
    dim = nch_q * 128
    for th in range(2):
        def sq(s, th=th):
            last = None
            for c in range(nch_q):
                last = s.activation(SQB[:, c * 512:(c + 1) * 512],
                                    QB[:, c * TC + th * 512:c * TC + th * 512 + 512],
                                    AF.Square)
            return last
        p.cstep("scalar", sq, label=f"{tag}sq")

        def mmq(t, th=th):
            last = None
            for c in range(nch_q):
                last = t.matmul(PS[0:1, 3584:4096], ONEBF[:, 0:1],
                                SQB[:, c * 512:(c + 1) * 512],
                                start=(c == 0), stop=(c == nch_q - 1))
            return last
        p.cstep("tensor", mmq, label=f"{tag}mm")
        p.cstep("vector", lambda v: v.tensor_scalar(
            RROW[0:1, 0:512], PS[0:1, 3584:4096], 1.0 / dim, eps, ALU.mult, ALU.add))
        p.cstep("scalar", lambda s: s.activation(
            RROW[0:1, 0:512], RROW[0:1, 0:512], AF.Sqrt))
        p.cstep("vector", lambda v: v.reciprocal(RROW[0:1, 0:512], RROW[0:1, 0:512]))
        if extra_scale != 1.0:
            p.cstep("vector", lambda v: v.tensor_scalar(
                RROW[0:1, 0:512], RROW[0:1, 0:512], extra_scale, None, ALU.mult))
        p.cstep("tensor", lambda t: t.matmul(
            PS[:, 3584:4096], ONE1[0:1, :], RROW[0:1, 0:512], start=True, stop=True))
        p.cstep("vector", lambda v: v.tensor_copy(RBC[:, 0:512], PS[:, 3584:4096]))

        def scl(v, th=th):
            last = None
            for c in range(nch_q):
                seg = QB[:, c * TC + th * 512:c * TC + th * 512 + 512]
                last = v.scalar_tensor_tensor(seg, seg, gaintab[:, c:c + 1],
                                              RBC[:, 0:512], ALU.mult, ALU.mult)
            return last
        p.cstep("vector", scl, label=f"{tag}scl")


def emit_rope(p, QB, nch_q, COSR, SINSGN, SWAPM, PS, SCR):
    for h in range(nch_q):
        def mmswap(t, h=h):
            t.matmul(PS[0:64, 0:512], SWAPM[:, :],
                     QB[0:64, h * TC:h * TC + 512], start=True, stop=True)
            return t.matmul(PS[0:64, 512:1024], SWAPM[:, :],
                            QB[0:64, h * TC + 512:h * TC + 1024],
                            start=True, stop=True)
        p.cstep("tensor", mmswap)
        p.cstep("vector", lambda v, h=h: v.tensor_copy(
            SCR[0:64, 0:1024], PS[0:64, 0:1024]))
        p.cstep("vector", lambda v, h=h: v.tensor_tensor(
            SCR[0:64, 1024:2048], QB[0:64, h * TC:(h) * TC + 1024],
            COSR[0:64, :], ALU.mult))
        p.cstep("vector", lambda v, h=h: v.tensor_tensor(
            SCR[0:64, 0:1024], SCR[0:64, 0:1024], SINSGN[0:64, :], ALU.mult))
        p.cstep("vector", lambda v, h=h: v.tensor_tensor(
            QB[0:64, h * TC:h * TC + 1024], SCR[0:64, 1024:2048],
            SCR[0:64, 0:1024], ALU.add))


def emit_router(p, TB, rw_slab_ap, RWB, COMB, CT, RLOG, RSIG, RB0B, IDF, OHALL,
                PS, M1, kind, bias_tiled_ap=None):
    p.dstep("gpsimd", lambda g: [g.dma_start(
        out=RWB[:, :].rearrange("p (k e) -> p k e", e=8),
        in_=rw_slab_ap.rearrange("(k p) e -> p k e", p=128))], label="rw_dma")
    if bias_tiled_ap is not None:
        p.dstep("sync", lambda e: [e.dma_start(out=RB0B[:, :], in_=bias_tiled_ap)],
                label="rb_dma")

    def mmr(t):
        last = None
        for tt in range(8):
            for kt in range(NCH):
                last = t.matmul(PS[:, tt * 8: tt * 8 + 8],
                                TB[:, kt * TC + tt * 128: kt * TC + tt * 128 + 128],
                                RWB[:, kt * 8:(kt + 1) * 8],
                                start=(kt == 0), stop=(kt == NCH - 1))
        return last
    p.cstep("tensor", mmr, label="r_mm")
    p.cstep("vector", lambda v: v.tensor_copy(RLOG[:, :], PS[:, 0:64]))

    if kind == "l0":
        p.cstep("scalar", lambda s: s.activation(RSIG[:, :], RLOG[:, :], AF.Sigmoid))
        p.cstep("vector", lambda v: v.tensor_tensor(RLOG[:, :], RSIG[:, :],
                                                    RB0B[:, :], ALU.add))
        SELV = RLOG
        W = RSIG
    else:
        p.cstep("vector", lambda v: v.tensor_reduce(
            M1[:, 0:8], RLOG[:, :].rearrange("p (t e) -> p t e", e=8),
            AX.X, ALU.max))
        def sub(v):
            last = None
            for tt in range(8):
                last = v.tensor_scalar(RLOG[:, tt * 8:(tt + 1) * 8],
                                       RLOG[:, tt * 8:(tt + 1) * 8],
                                       M1[:, tt:tt + 1], None, ALU.subtract)
            return last
        p.cstep("vector", sub)
        p.cstep("scalar", lambda s: s.activation(RSIG[:, :], RLOG[:, :], AF.Exp))
        p.cstep("vector", lambda v: v.tensor_reduce(
            M1[:, 8:16], RSIG[:, :].rearrange("p (t e) -> p t e", e=8),
            AX.X, ALU.add))
        p.cstep("vector", lambda v: v.reciprocal(M1[:, 8:16], M1[:, 8:16]))
        def nrm(v):
            last = None
            for tt in range(8):
                last = v.tensor_scalar(RSIG[:, tt * 8:(tt + 1) * 8],
                                       RSIG[:, tt * 8:(tt + 1) * 8],
                                       M1[:, 8 + tt:9 + tt], None, ALU.mult)
            return last
        p.cstep("vector", nrm)
        SELV = RSIG
        W = RSIG

    p.cstep("vector", lambda v: v.tensor_reduce(
        M1[:, 0:8], SELV[:, :].rearrange("p (t e) -> p t e", e=8), AX.X, ALU.max))
    for tt in range(8):
        sl = slice(tt * 8, (tt + 1) * 8)
        p.cstep("vector", lambda v, tt=tt, sl=sl: v.tensor_scalar(
            COMB[:, sl], SELV[:, sl], M1[:, tt:tt + 1], None, ALU.is_ge))
        p.cstep("vector", lambda v, tt=tt, sl=sl: v.scalar_tensor_tensor(
            M1[:, 16 + tt * 8:16 + tt * 8 + 8]
            if False else RLOG[:, sl], COMB[:, sl], -40000.0, SELV[:, sl],
            ALU.mult, ALU.add))
    p.cstep("vector", lambda v: v.tensor_reduce(
        M1[:, 8:16], RLOG[:, :].rearrange("p (t e) -> p t e", e=8), AX.X, ALU.max))
    for tt in range(8):
        sl = slice(tt * 8, (tt + 1) * 8)
        p.cstep("vector", lambda v, tt=tt, sl=sl: v.tensor_scalar(
            RLOG[:, sl], RLOG[:, sl], M1[:, 8 + tt:9 + tt], None, ALU.is_ge))
        p.cstep("vector", lambda v, tt=tt, sl=sl: v.tensor_tensor(
            COMB[:, sl], COMB[:, sl], RLOG[:, sl], ALU.add))
        p.cstep("vector", lambda v, tt=tt, sl=sl: v.tensor_tensor(
            COMB[:, sl], COMB[:, sl], W[:, sl], ALU.mult))
    if kind == "l0":
        p.cstep("vector", lambda v: v.tensor_reduce(
            M1[:, 0:8], COMB[:, :].rearrange("p (t e) -> p t e", e=8),
            AX.X, ALU.add))
        p.cstep("vector", lambda v: v.reciprocal(M1[:, 0:8], M1[:, 0:8]))
        def nm(v):
            last = None
            for tt in range(8):
                last = v.tensor_scalar(COMB[:, tt * 8:(tt + 1) * 8],
                                       COMB[:, tt * 8:(tt + 1) * 8],
                                       M1[:, tt:tt + 1], None, ALU.mult)
            return last
        p.cstep("vector", nm)

    for tt in range(8):
        p.cstep("tensor", lambda t, tt=tt: t.transpose(
            PS[0:8, tt * 128:(tt + 1) * 128], COMB[:, tt * 8:(tt + 1) * 8],
            IDF[:, :]))
    p.cstep("vector", lambda v: v.tensor_copy(CT[0:8, :], PS[0:8, 0:1024]))


def emit_proj_group(p, w_ap, rhs_bf, ocs, Din, WB, PS, tag="", prev_evac=None):
    nkt = Din // 128
    now = len(ocs) * 128
    o0 = ocs[0] * 128
    nb = (nkt + 3) // 4
    mm_steps = []
    for b in range(nb):
        kts = list(range(b * 4, min(b * 4 + 4, nkt)))
        deps = []
        if b >= 2:
            deps.append(mm_steps[b - 2])
        elif prev_evac is not None:
            deps.append(prev_evac)
        if not deps:
            deps = [p.steps[-1]]
        buf = WB[b % 2]
        p.dstep("gpsimd", lambda g, kts=kts, buf=buf: [g.dma_start(
            out=buf[:, 0:len(kts) * now].rearrange("p (c n) -> p c n", n=now),
            in_=w_ap[kts[0] * 128:(kts[-1] + 1) * 128, o0:o0 + now]
            .rearrange("(c p) n -> p c n", p=128))],
            deps=deps, label=f"pg{tag}")

        def mm(t, kts=kts, buf=buf, b=b):
            last = None
            for i, kt in enumerate(kts):
                for j in range(len(ocs)):
                    for th in range(2):
                        last = t.matmul(
                            PS[:, (j * 2 + th) * 512:(j * 2 + th) * 512 + 512],
                            buf[:, i * now + j * 128:i * now + (j + 1) * 128],
                            rhs_bf[:, kt * TC + th * 512:kt * TC + th * 512 + 512],
                            start=(kt == 0), stop=(kt == nkt - 1))
            return last
        mdeps = [p.steps[-1]]
        if b == 0 and prev_evac is not None:
            mdeps.append(prev_evac)
        mm_steps.append(p.cstep("tensor", mm, deps=mdeps, label=f"pgm{tag}"))
    banks = [(oc, th, PS[:, (j * 2 + th) * 512:(j * 2 + th) * 512 + 512])
             for j, oc in enumerate(ocs) for th in range(2)]
    return banks, mm_steps[-1]


def _groups(nch):
    ocs, out = list(range(nch)), []
    while ocs:
        out.append(ocs[:4])
        ocs = ocs[4:]
    return out


def emit_moe_experts(p, TB, wg_ap, wu_ap, Ich, CT, OHALL, CB, GSIL, UCB,
                     WB, PS, acts, rowbase_of, tag="m"):
    prev = None
    for e in range(8):
        for th in range(2):
            p.cstep("tensor", lambda t, e=e, th=th: t.matmul(
                PS[:, 3584:4096], OHALL[0:8, e * 128:(e + 1) * 128],
                CT[0:8, th * 512:(th + 1) * 512], start=True, stop=True))
            p.cstep("vector", lambda v, th=th: v.tensor_copy(
                CB[:, th * 512:(th + 1) * 512], PS[:, 3584:4096]))
        cb_anchor = p.steps[-1]
        for gi, ocs in enumerate(_groups(Ich)):
            banks, lg = emit_proj_group(p, wg_ap[e], TB, ocs, D, WB, PS,
                                        tag=f"{tag}g{e}_{gi}",
                                        prev_evac=prev or cb_anchor)

            def evg(s, banks=banks):
                last = None
                for bi, (oc, th, pv) in enumerate(banks):
                    last = s.activation(GSIL[:, bi * 512:(bi + 1) * 512], pv,
                                        AF.Silu)
                return last
            dg = [lg] + ([prev] if prev else [])
            sg_step = p.cstep("scalar", evg, deps=dg)
            banks, lu = emit_proj_group(p, wu_ap[e], TB, ocs, D, WB, PS,
                                        tag=f"{tag}u{e}_{gi}", prev_evac=sg_step)

            def evu(v, banks=banks):
                last = None
                for bi, (oc, th, pv) in enumerate(banks):
                    last = v.tensor_tensor(UCB[:, bi * 512:(bi + 1) * 512], pv,
                                           CB[:, th * 512:(th + 1) * 512],
                                           ALU.mult)
                return last
            du = [lu] + ([prev] if prev else [])
            evu_step = p.cstep("vector", evu, deps=du)

            def eva(v, banks=banks):
                last = None
                for bi in range(len(banks)):
                    last = v.tensor_tensor(UCB[:, bi * 512:(bi + 1) * 512],
                                           UCB[:, bi * 512:(bi + 1) * 512],
                                           GSIL[:, bi * 512:(bi + 1) * 512],
                                           ALU.mult)
                return last
            eva_step = p.cstep("vector", eva, deps=[evu_step, sg_step])

            def st(ee, ocs=ocs, e=e):
                rows0 = (rowbase_of(e) + ocs[0]) * 128
                return [ee.dma_start(
                    out=acts.ap()[rows0:rows0 + len(ocs) * 128, :]
                    .rearrange("(c p) t -> p c t", p=128),
                    in_=UCB[:, 0:len(ocs) * TC]
                    .rearrange("p (c t) -> p c t", t=TC))]
            prev = p.dstep("sync", st, deps=[eva_step], label=f"{tag}act{e}")


def emit_shared_expert(p, TB, sg_ap, su_ap, SGB, GSIL, UCB, WB, PS,
                       acts, rowbase, tag="s"):
    prev = None
    for gi, ocs in enumerate(_groups(44)):
        banks, lg = emit_proj_group(p, sg_ap, TB, ocs, D, WB, PS,
                                    tag=f"{tag}g{gi}",
                                    prev_evac=prev or p.steps[-1])

        def evg(s, banks=banks):
            last = None
            for bi, (oc, th, pv) in enumerate(banks):
                last = s.activation(GSIL[:, bi * 512:(bi + 1) * 512], pv, AF.Silu)
            return last
        sg_step = p.cstep("scalar", evg, deps=[lg] + ([prev] if prev else []))
        banks, lu = emit_proj_group(p, su_ap, TB, ocs, D, WB, PS,
                                    tag=f"{tag}u{gi}", prev_evac=sg_step)

        def evu(v, banks=banks):
            last = None
            for bi, (oc, th, pv) in enumerate(banks):
                last = v.tensor_tensor(UCB[:, bi * 512:(bi + 1) * 512], pv,
                                       SGB[:, th * 512:(th + 1) * 512], ALU.mult)
            return last
        evu_step = p.cstep("vector", evu, deps=[lu] + ([prev] if prev else []))

        def eva(v, banks=banks):
            last = None
            for bi in range(len(banks)):
                last = v.tensor_tensor(UCB[:, bi * 512:(bi + 1) * 512],
                                       UCB[:, bi * 512:(bi + 1) * 512],
                                       GSIL[:, bi * 512:(bi + 1) * 512], ALU.mult)
            return last
        eva_step = p.cstep("vector", eva, deps=[evu_step, sg_step])

        def st(ee, ocs=ocs):
            rows0 = (rowbase + ocs[0]) * 128
            return [ee.dma_start(
                out=acts.ap()[rows0:rows0 + len(ocs) * 128, :]
                .rearrange("(c p) t -> p c t", p=128),
                in_=UCB[:, 0:len(ocs) * TC].rearrange("p (c t) -> p c t", t=TC))]
        prev = p.dstep("sync", st, deps=[eva_step], label=f"{tag}act{gi}")


def emit_moe_down(p, sources, acts, ACT2A, ACT2B, WB, PS, down_evac, tag="d"):
    prev_evac = None
    total = sum(nkt for (_, nkt, _) in sources)
    for og2 in range(4):
        gk = 0
        bi = 0
        mm_by_gk = {}
        batch_last_mm = []
        anchor = prev_evac or p.steps[-1]
        for (wd_ap, nkt, rowbase) in sources:
            for b0 in range(0, nkt, 4):
                kts = list(range(b0, min(b0 + 4, nkt)))
                buf = WB[bi % 2]
                wdeps = [batch_last_mm[bi - 2]] if bi >= 2 else [anchor]
                wst = p.dstep("gpsimd", lambda g, wd_ap=wd_ap, kts=kts, og2=og2,
                              buf=buf: [g.dma_start(
                                  out=buf[:, 0:len(kts) * 512]
                                  .rearrange("p (c n) -> p c n", n=512),
                                  in_=wd_ap[kts[0] * 128:(kts[-1] + 1) * 128,
                                            og2 * 512:(og2 + 1) * 512]
                                  .rearrange("(c p) n -> p c n", p=128))],
                              deps=wdeps, label=f"{tag}w")
                for i, kt in enumerate(kts):
                    abuf = ACT2A if gk % 2 == 0 else ACT2B
                    adeps = [mm_by_gk[gk - 2]] if gk - 2 in mm_by_gk else [anchor]
                    ast_ = p.dstep("sync", lambda e, rowbase=rowbase, kt=kt,
                                   abuf=abuf: [e.dma_start(
                                       out=abuf[:, :],
                                       in_=acts.ap()[(rowbase + kt) * 128:
                                                     (rowbase + kt + 1) * 128, :])],
                                   deps=adeps, label=f"{tag}a")
                    first = gk == 0
                    last_ = gk == total - 1

                    def mm(t, first=first, last_=last_, abuf=abuf, buf=buf, i=i):
                        r = None
                        for j in range(4):
                            for th in range(2):
                                r = t.matmul(
                                    PS[:, (j * 2 + th) * 512:(j * 2 + th) * 512 + 512],
                                    buf[:, i * 512 + j * 128:i * 512 + (j + 1) * 128],
                                    abuf[:, th * 512:th * 512 + 512],
                                    start=first, stop=last_)
                        return r
                    mdeps = [wst, ast_] + ([prev_evac] if (gk == 0 and prev_evac)
                                           else [])
                    mm_by_gk[gk] = p.cstep("tensor", mm, deps=mdeps,
                                           label=f"{tag}m")
                    gk += 1
                batch_last_mm.append(mm_by_gk[gk - 1])
                bi += 1
        banks = [(og2 * 4 + j, th,
                  PS[:, (j * 2 + th) * 512:(j * 2 + th) * 512 + 512])
                 for j in range(4) for th in range(2)]
        prev_evac = down_evac(og2, banks, [mm_by_gk[total - 1]])


def emit_idf_ohall(p, IDF, OHALL, IDI):
    p.cstep("gpsimd", lambda g: g.iota(
        IDI[:, :], pattern=[[1, 128]], base=0, channel_multiplier=-1,
        allow_small_or_imprecise_dtypes=True))
    p.cstep("vector", lambda v: v.tensor_scalar(
        IDF[:, :], IDI[:, :], 0.0, None, ALU.is_equal))
    p.cstep("gpsimd", lambda g: g.iota(
        OHALL[0:8, :].rearrange("p (e m) -> p e m", m=128),
        pattern=[[-1, 8], [0, 128]], base=0, channel_multiplier=1,
        allow_small_or_imprecise_dtypes=True))
    p.cstep("vector", lambda v: v.tensor_scalar(
        OHALL[0:8, :], OHALL[0:8, :], 0.0, None, ALU.is_equal))


def _run(nc, in_maps):
    res = run_bass_kernel_spmd(nc, in_maps, core_ids=list(range(NCORES)))
    return res.results


def build_fused():
    p = P()
    x_in = p.din("x", [D, TC])
    out_out = p.dout("out", [D, TC])
    igw_ap = p.din("igw", [16, 128, 128])
    rgw_ap = p.din("rgw", [16, 128, 128])
    igb_ap = p.din("igb", [128, 16])
    rgb_ap = p.din("rgb", [128, 16])
    rp_ap = p.din("rp", [128, 16])
    sel_ap = p.din("sel", [128, 8])
    selp_ap = p.din("selp", [128, 8])
    seln_ap = p.din("seln", [128, 8])
    fm_ap = p.din("first_mask", [128, 1])
    wq_ap = p.din("wq", [D, D])
    wk_ap = p.din("wk", [D, 512])
    wv_ap = p.din("wv", [D, 512])
    wo_ap = p.din("wo", [D, D])
    ln1_ap = p.din("ln1", [128, 16])
    ln2_ap = p.din("ln2", [128, 16])
    qnw_ap = p.din("qnw", [128, 16])
    knw_ap = p.din("knw", [128, 4])
    cos_ap = p.din("cosr", [64, TC])
    sin_ap = p.din("sinsgn", [64, TC])
    swap_ap = p.din("swapm", [64, 64], BF16)
    qpos_ap = p.din("qpos", [1, TC])
    rw0_ap = p.din("rw0", [D, 8])
    rbt_ap = p.din("rb0t", [128, 64])
    wg0_ap = p.din("wg0", [8, D, 1024])
    wu0_ap = p.din("wu0", [8, D, 1024])
    wd0_ap = p.din("wd0", [8, 1024, D])
    cw_ap = p.din("cw", [3, 128, 16])
    cb_ap = p.din("cb", [128, 16])
    WuT_ap = p.din("WuT", [D, D])
    WvT_ap = p.din("WvT", [D, D])
    Wg1T_ap = p.din("Wg1T", [D, 512])
    Wg2T_ap = p.din("Wg2T", [512, D])
    bu_ap = p.din("bu", [128, 16])
    bv_ap = p.din("bv", [128, 16])
    bg1_ap = p.din("bg1", [128, 4])
    bg2_ap = p.din("bg2", [128, 16])
    wr1_ap = p.din("Wr1", [D, 8])
    wg1_ap = p.din("wg1", [8, D, 1408])
    wu1_ap = p.din("wu1", [8, D, 1408])
    wd1_ap = p.din("wd1", [8, 1408, D])
    sg_ap = p.din("Sg", [D, 5632])
    su_ap = p.din("Su", [D, 5632])
    sd_ap = p.din("Sd", [5632, D])
    wsg_ap = p.din("Wsg", [128, 16])

    agg_in = p.dint("agg_in", [128, 32])
    agg_out = p.dint("agg_out", [128 * NCORES, 32])
    kag_in = p.dint("kag_in", [512, TC], BF16)
    kag_out = p.dint("kag_out", [512 * NCORES, TC], BF16)
    vag_in = p.dint("vag_in", [TC, 512], BF16)
    vag_out = p.dint("vag_out", [TC * NCORES, 512], BF16)
    s0_d = p.dint("s0_d", [D, TC], BF16)
    o_d = p.dint("o_d", [D, TC], BF16)
    g0_d = p.dint("g0_d", [D, TC], BF16)
    xl1_d = p.dint("xl1_d", [D, TC])
    tout_d = p.dint("tout_d", [D, TC], BF16)
    s1_d = p.dint("s1_d", [D, TC], BF16)
    hal_in = p.dint("hal_in", [128, 32])
    hal_out = p.dint("hal_out", [128 * NCORES, 32])
    acts0 = p.dint("acts0", [64 * 128, TC], BF16)
    acts1 = p.dint("acts1", [132 * 128, TC], BF16)

    F32A = p.sb("F32A", [128, NCH * TC])
    BF1 = p.sb("BF1", [128, NCH * TC], BF16)
    BF2 = p.sb("BF2", [128, NCH * TC], BF16)
    BF3 = p.sb("BF3", [128, NCH * TC], BF16)
    SMF = p.sb("SMF", [128, 4608])
    SMB = p.sb("SMB", [128, 8448], BF16)
    WBA = p.sb("WBA", [128, 2048], BF16)
    WBB = p.sb("WBB", [128, 2048], BF16)
    WB = (WBA, WBB)
    IDI = p.sb("IDI", [128, 128], mybir.dt.int32)
    TAB = p.sb("TAB", [128, 64])
    ONE1 = p.sb("ONE1", [1, 128])
    ONEBF = p.sb("ONEBF", [128, 1], BF16)
    NEG30 = p.sb("NEG30", [128, 1])
    SEL = p.sb("SEL", [128, 8])
    SELP = p.sb("SELP", [128, 8])
    SELN = p.sb("SELN", [128, 8])
    FMASK = p.sb("FMASK", [128, 1])
    AGG = p.sb("AGG", [128, 32])
    CAR = p.sb("CAR", [128, 16])
    SST = p.sb("SST", [128, 16])
    QNWS = p.sb("QNWS", [128, 16])
    PS = p.ps("PS", [128, 4096])
    MB = SMF[:, 3584:4096]
    RB = SMF[:, 4096:4608]

    X, XNB, ALOC, YLOC = F32A, BF1, BF2, BF3
    SCR = SMF[:, 0:4096]
    ZEROSA = SMB[:, 0:1024]
    AGALL = SMF[:, 0:256]
    p.dstep("sync", lambda e: [
        feat_load(p, e, X, x_in),
        e.dma_start(out=TAB[:, 0:16], in_=igb_ap),
        e.dma_start(out=TAB[:, 16:32], in_=rgb_ap),
        e.dma_start(out=TAB[:, 32:48], in_=rp_ap),
        e.dma_start(out=SEL[:, :], in_=sel_ap),
        e.dma_start(out=SELP[:, :], in_=selp_ap),
        e.dma_start(out=SELN[:, :], in_=seln_ap),
        e.dma_start(out=FMASK[:, :], in_=fm_ap),
        e.dma_start(out=QNWS[:, 0:16], in_=qnw_ap),
    ], label="loadA", n=9)

    def init(v):
        v.memset(ZEROSA[:, :], 0.0)
        v.memset(ONEBF[:, :], 1.0)
        v.memset(NEG30[:, :], -30.0)
        return v.memset(ONE1[:, :], 1.0)
    p.cstep("vector", init)
    emit_softplus8(p, TAB, SCR)
    emit_ln(p, X, XNB, ALOC[:, 0:NCH * 512], YLOC[:, 0:NCH * 512],
            MB, RB, PS[0:1, 3584:4096], PS[0:1, 3072:3584], PS[:, 2048:2560],
            ONEBF, ONE1)
    emit_scan_branch(p, X, XNB, ALOC, YLOC, SCR, ZEROSA, TAB, SEL, FMASK,
                     WBA[:, 0:128], AGG, AGALL, CAR, SST,
                     PS[:, 0:512], PS[:, 512:1024],
                     igw_ap, rgw_ap, agg_in, agg_out)
    p.dstep("sync", lambda e: [feat_store(p, e, s0_d.ap(), YLOC)], label="s0st")

    XN, HB, QB = F32A, BF1, BF2
    SQB = BF3[:, 0:NCH * 512]
    KL = SMB[:, 0:4096]
    VT = SMB[:, 4096:8192]
    SWAPM = SMB[0:64, 8192:8256]
    COSR = SMF[0:64, 0:1024]
    SINSGN = SMF[0:64, 1024:2048]
    RROW = SMF[0:1, 3584:4096]
    p.dstep("sync", lambda e: [
        e.dma_start(out=TAB[:, 0:16], in_=ln1_ap),
        e.dma_start(out=TAB[:, 32:36], in_=knw_ap),
        e.dma_start(out=COSR, in_=cos_ap),
        e.dma_start(out=SINSGN, in_=sin_ap),
        e.dma_start(out=SWAPM, in_=swap_ap),
    ], label="loadB1", n=5)
    p.cstep("vector", lambda v: v.tensor_scalar(
        QNWS[:, :], QNWS[:, :], float(1.0 / np.sqrt(128.0)), None, ALU.mult))
    emit_ln(p, XN, HB, SQB, None, None, RB,
            PS[0:1, 3584:4096], None, PS[:, 3584:4096], ONEBF, ONE1,
            eps=1e-6, gain=TAB[:, 0:16])

    def ev_copy(dst, col_of):
        def f(og, banks, deps):
            def e(s, banks=banks):
                last = None
                for (oc, th, pv) in banks:
                    last = s.copy(dst[:, col_of(oc, th):col_of(oc, th) + 512], pv)
                return last
            return p.cstep("scalar", e, deps=deps, label="pev")
        return f
    emit_proj(p, wq_ap, HB, D, D, WB, PS,
              ev_copy(QB, lambda oc, th: oc * TC + th * 512))
    emit_proj(p, wk_ap, HB, D, 512, WB, PS,
              ev_copy(KL, lambda oc, th: oc * TC + th * 512))
    emit_rownorm_scale(p, QB, QNWS[:, 0:16], 16, PS, SQB, RROW, RB,
                       ONEBF, ONE1, 1e-6, "qn")
    emit_rownorm_scale(p, KL, TAB[:, 32:36], 4, PS, SQB[:, 0:4 * 512], RROW,
                       RB, ONEBF, ONE1, 1e-6, "kn")
    emit_rope(p, QB, 16, COSR, SINSGN, SWAPM, PS, SMF[:, 2048:4096])
    emit_rope(p, KL, 4, COSR, SINSGN, SWAPM, PS, SMF[:, 2048:4096])
    for kt in range(NCH):
        p.dstep("gpsimd", lambda gg, kt=kt: [gg.dma_start(
            out=WBA[:, 0:512], in_=wv_ap[kt * 128:(kt + 1) * 128, :])],
            label=f"vw{kt}")

        def mmv(t, kt=kt):
            last = None
            for tt in range(8):
                last = t.matmul(PS[:, tt * 512:(tt + 1) * 512],
                                HB[:, kt * TC + tt * 128:kt * TC + tt * 128 + 128],
                                WBA[:, 0:512], start=(kt == 0), stop=(kt == NCH - 1))
            return last
        p.cstep("tensor", mmv, label="vmm")

    def vev(s):
        last = None
        for tt in range(8):
            last = s.copy(VT[:, tt * 512:(tt + 1) * 512],
                          PS[:, tt * 512:(tt + 1) * 512])
        return last
    p.cstep("scalar", vev)
    p.dstep("sync", lambda e: [
        e.dma_start(out=kag_in.ap().rearrange("(c p) t -> p c t", p=128),
                    in_=KL[:, :].rearrange("p (c t) -> p c t", t=TC)),
        e.dma_start(out=vag_in.ap().rearrange("(tt p) o -> p tt o", p=128),
                    in_=VT[:, :].rearrange("p (tt o) -> p tt o", o=512)),
    ], label="kv_out", n=2)
    p.xstep("gpsimd", lambda gg: [gg.collective_compute(
        "AllGather", ALU.bypass, replica_groups=[list(range(NCORES))],
        ins=[kag_in.ap().opt()], outs=[kag_out.ap().opt()])], label="ag_k")
    p.xstep("gpsimd", lambda gg: [gg.collective_compute(
        "AllGather", ALU.bypass, replica_groups=[list(range(NCORES))],
        ins=[vag_in.ap().opt()], outs=[vag_out.ap().opt()])], label="ag_v")

    KF2, VF2 = BF1, BF3
    PTT = (SMB[:, 0:1024], SMB[:, 1024:2048])
    MTT = (SMB[:, 2048:3072], SMB[:, 3072:4096])
    QPOSB = SMF[:, 0:1024]
    ZB = SMF[:, 1024:1536]
    KIDXC = SMF[:, 1536:1600]
    QROW = SMF[0:1, 1600:2624]
    ZROW = SMF[0:1, 2624:3136]
    SLABS = (PS[:, 0:1024], PS[:, 1024:2048])
    OP = PS[:, 2048:2560]
    ZP = PS[0:1, 2560:3072]
    PBC = PS[:, 3072:3584]
    p.dstep("sync", lambda e: [e.dma_start(out=QROW, in_=qpos_ap)],
            label="loadqp")
    p.cstep("gpsimd", lambda gg: gg.iota(
        KIDXC, pattern=[[128, 64]], base=0, channel_multiplier=1,
        allow_small_or_imprecise_dtypes=True))
    for th in range(2):
        p.cstep("tensor", lambda t, th=th: t.matmul(
            PBC, ONE1[0:1, :], QROW[0:1, th * 512:(th + 1) * 512],
            start=True, stop=True))
        p.cstep("vector", lambda v, th=th: v.tensor_copy(
            QPOSB[:, th * 512:(th + 1) * 512], PBC))

    for grp in range(2):
        def ldk(e, grp=grp):
            ins = []
            for chl in range(2):
                ch = 2 * grp + chl
                for r in range(NCORES):
                    ins.append(e.dma_start(
                        out=KF2[:, chl * 8192 + r * TC: chl * 8192 + (r + 1) * TC],
                        in_=kag_out.ap()[r * 512 + ch * 128: r * 512 + (ch + 1) * 128, :]))
            return ins
        p.dstep("sync", ldk, label=f"ldk{grp}", n=16)
        p.dstep("sync", lambda e, grp=grp: [e.dma_start(
            out=VF2[:, 0:64 * 256].rearrange("p (kt o) -> p kt o", o=256),
            in_=vag_out.ap()[:, grp * 256:(grp + 1) * 256]
            .rearrange("(kt p) o -> p kt o", p=128))], label=f"ldv{grp}")
        prev_tail = p.steps[-1]
        for h8 in range(8):
            h = grp * 8 + h8
            chl = h8 // 4
            for qtg in range(2):
                last_kt = 59 + 4 * qtg
                nkt = last_kt + 1
                qc0 = qtg * 512
                nktg = (nkt + 1) // 2
                exp_steps, zav_steps, map_steps = {}, {}, {}
                for ktg in range(nktg):
                    kts = [kt for kt in range(ktg * 2, min(ktg * 2 + 2, nkt))]
                    par = ktg % 2
                    slab, PT, MT = SLABS[par], PTT[par], MTT[par]

                    def mmsc(t, kts=kts, h=h, chl=chl, qtg=qtg, slab=slab):
                        last = None
                        for j, kt in enumerate(kts):
                            last = t.matmul(
                                slab[:, j * 512:(j + 1) * 512],
                                KF2[:, chl * 8192 + kt * 128: chl * 8192 + kt * 128 + 128],
                                QB[:, h * TC + qtg * 512: h * TC + qtg * 512 + 512],
                                start=True, stop=True)
                        return last
                    d = [exp_steps[ktg - 2]] if ktg >= 2 else [prev_tail]
                    sc_step = p.cstep("tensor", mmsc, deps=d, label="sc")

                    def expv(s, n=len(kts), slab=slab, PT=PT):
                        return s.activation(PT[:, 0:n * 512], slab[:, 0:n * 512],
                                            AF.Exp, bias=NEG30[:, 0:1])
                    d = [sc_step] + ([zav_steps[ktg - 2]] if ktg >= 2 else [prev_tail])
                    exp_steps[ktg] = p.cstep("scalar", expv, deps=d, label="exp")

                    mkts = [kt for kt in kts if kt >= 4 * qtg]
                    if mkts:
                        def mbuild(v, kts=kts, mkts=mkts, qc0=qc0, MT=MT):
                            last = None
                            for j, kt in enumerate(kts):
                                if kt in mkts:
                                    last = v.tensor_scalar(
                                        MT[:, j * 512:(j + 1) * 512],
                                        QPOSB[:, qc0:qc0 + 512],
                                        KIDXC[:, kt:kt + 1], None, ALU.is_ge)
                            return last
                        d = [map_steps[ktg - 2]] if ktg - 2 in map_steps else [prev_tail]
                        mb_step = p.cstep("vector", mbuild, deps=d, label="mb")

                        def mapply(g, kts=kts, mkts=mkts, PT=PT, MT=MT):
                            last = None
                            for j, kt in enumerate(kts):
                                if kt in mkts:
                                    last = g.tensor_tensor(
                                        PT[:, j * 512:(j + 1) * 512],
                                        PT[:, j * 512:(j + 1) * 512],
                                        MT[:, j * 512:(j + 1) * 512], ALU.mult)
                            return last
                        map_steps[ktg] = p.cstep("gpsimd", mapply,
                                                 deps=[exp_steps[ktg], mb_step],
                                                 label="ma")

                    def zav(t, kts=kts, chl=chl, last_kt=last_kt, PT=PT):
                        last = None
                        for j, kt in enumerate(kts):
                            last = t.matmul(ZP, ONEBF[:, 0:1],
                                            PT[:, j * 512:(j + 1) * 512],
                                            start=(kt == 0), stop=(kt == last_kt))
                        for j, kt in enumerate(kts):
                            last = t.matmul(
                                OP, VF2[:, kt * 256 + chl * 128: kt * 256 + chl * 128 + 128],
                                PT[:, j * 512:(j + 1) * 512],
                                start=(kt == 0), stop=(kt == last_kt))
                        return last
                    d = ([map_steps[ktg]] if ktg in map_steps else [exp_steps[ktg]])
                    if ktg == 0:
                        d = d + [prev_tail]
                    zav_steps[ktg] = p.cstep("tensor", zav, deps=d, label="zav")
                rc = p.cstep("vector", lambda v: v.reciprocal(ZROW, ZP),
                             deps=[zav_steps[nktg - 1]])
                bc = p.cstep("tensor", lambda t: t.matmul(
                    PBC, ONE1[0:1, :], ZROW, start=True, stop=True), deps=[rc])
                cz = p.cstep("vector", lambda v: v.tensor_copy(ZB, PBC), deps=[bc])
                oe = p.cstep("vector", lambda v, par=(nktg - 1) % 2: v.tensor_tensor(
                    PTT[par][:, 0:512], OP, ZB, ALU.mult), deps=[cz])
                prev_tail = p.dstep("sync", lambda e, h=h, qc0=qc0,
                                    par=(nktg - 1) % 2: [e.dma_start(
                                        out=o_d.ap()[h * 128:(h + 1) * 128, qc0:qc0 + 512],
                                        in_=PTT[par][:, 0:512])],
                                    deps=[oe], label="ost")

    OB = BF1
    p.dstep("sync", lambda e: [feat_load(p, e, OB, o_d.ap())], label="ldo")

    def ev_add_xn(og, banks, deps):
        def f(v, banks=banks):
            last = None
            for (oc, th, pv) in banks:
                seg = XN[:, oc * TC + th * 512: oc * TC + th * 512 + 512]
                last = v.tensor_tensor(seg, seg, pv, ALU.add)
            return last
        return p.cstep("vector", f, deps=deps, label="ev_add")
    emit_proj(p, wo_ap, OB, D, D, WB, PS, ev_add_xn)

    X1, TB = F32A, BF1
    GSIL = BF2
    UCB = BF3
    RWB = SMB[:, 0:128]
    A2A = SMB[:, 128:1152]
    A2B = SMB[:, 1152:2176]
    COMB = SMF[:, 0:64]
    RLOG = SMF[:, 64:128]
    RSIG = SMF[:, 128:192]
    RB0B = SMF[:, 192:256]
    M1 = SMF[:, 256:280]
    IDF = SMF[:, 280:408]
    OHALL = SMF[0:8, 408:1432]
    CT = SMF[0:8, 1432:2456]
    CB = SMF[:, 2456:3480]
    p.dstep("sync", lambda e: [e.dma_start(out=TAB[:, 0:16], in_=ln2_ap)],
            label="loadC")
    emit_idf_ohall(p, IDF, OHALL, IDI)
    emit_ln(p, X1, TB, UCB[:, 0:NCH * 512], None, None, RB,
            PS[0:1, 3584:4096], None, PS[:, 3584:4096], ONEBF, ONE1,
            eps=1e-6, gain=TAB[:, 0:16])
    emit_router(p, TB, rw0_ap, RWB, COMB, CT, RLOG, RSIG, RB0B, IDF, OHALL,
                PS, M1, "l0", bias_tiled_ap=rbt_ap)
    emit_moe_experts(p, TB, wg0_ap, wu0_ap, 8, CT, OHALL, CB,
                     GSIL[:, 0:4096], UCB[:, 0:4096], WB, PS, acts0,
                     lambda e: e * 8, tag="c")

    def devacC(og2, banks, deps):
        def f1(v, banks=banks):
            last = None
            for (oc, th, pv) in banks:
                seg = X1[:, oc * TC + th * 512: oc * TC + th * 512 + 512]
                last = v.tensor_tensor(seg, seg, pv, ALU.add)
            return last
        s1_ = p.cstep("vector", f1, deps=deps)

        def f2(s, banks=banks):
            last = None
            for (oc, th, pv) in banks:
                seg = X1[:, oc * TC + th * 512: oc * TC + th * 512 + 512]
                last = s.activation(TB[:, oc * TC + th * 512:
                                       oc * TC + th * 512 + 512], seg, AF.Silu)
            return last
        return p.cstep("scalar", f2, deps=[s1_])
    emit_moe_down(p, [(wd0_ap[e], 8, e * 8) for e in range(8)], acts0,
                  A2A, A2B, WB, PS, devacC, tag="cd")
    p.dstep("sync", lambda e: [feat_store(p, e, g0_d.ap(), TB)], label="g0st")

    X = F32A
    S0, G0, XN1B = BF1, BF2, BF3
    SCRD = SMF[:, 0:1024]
    HAL = SMF[:, 1024:1056]
    HALL = SMF[:, 1056:1312]
    LCOL = SMF[:, 1312:1328]
    RCOL = SMF[:, 1328:1344]
    TCV = SMB[:, 0:1024]
    p.dstep("sync", lambda e: [
        feat_load(p, e, X, x_in),
        feat_load(p, e, S0, s0_d.ap()),
        feat_load(p, e, G0, g0_d.ap()),
        e.dma_start(out=TAB[:, 0:48].rearrange("p (k c) -> p k c", c=16),
                    in_=cw_ap[:, :, :].rearrange("k p c -> p k c")),
        e.dma_start(out=TAB[:, 48:64], in_=cb_ap),
    ], label="loadD", n=5)
    for c in range(NCH):
        p.cstep("vector", lambda v, c=c: v.tensor_tensor(
            SCRD[:, :], S0[:, c * TC:(c + 1) * TC], G0[:, c * TC:(c + 1) * TC],
            ALU.mult))
        p.cstep("vector", lambda v, c=c: v.tensor_tensor(
            X[:, c * TC:(c + 1) * TC], X[:, c * TC:(c + 1) * TC], SCRD[:, :],
            ALU.add))
    p.dstep("sync", lambda e: [feat_store(p, e, xl1_d.ap(), X)], label="xl1st")
    emit_ln(p, X, XN1B, S0[:, 0:NCH * 512], S0[:, NCH * 512:NCH * 1024],
            MB, RB, PS[0:1, 3584:4096], PS[0:1, 3072:3584], PS[:, 2048:2560],
            ONEBF, ONE1, eps=1e-5)

    def pack(v):
        last = None
        for c in range(NCH):
            v.tensor_copy(HAL[:, c:c + 1], X[:, c * TC:c * TC + 1])
            last = v.tensor_copy(HAL[:, 16 + c:17 + c],
                                 X[:, c * TC + TC - 1:c * TC + TC])
        return last
    p.cstep("vector", pack)
    p.dstep("gpsimd", lambda g: [g.dma_start(out=hal_in[:, :], in_=HAL)])
    p.xstep("gpsimd", lambda g: [g.collective_compute(
        "AllGather", ALU.bypass, replica_groups=[list(range(NCORES))],
        ins=[hal_in.ap().opt()], outs=[hal_out.ap().opt()])], label="ag_hal")
    p.dstep("gpsimd", lambda g: [g.dma_start(
        out=HALL.rearrange("p (r c) -> p r c", c=32),
        in_=hal_out.ap().rearrange("(r p) c -> p r c", p=128))])

    def zc(v):
        v.memset(LCOL[:, :], 0.0)
        return v.memset(RCOL[:, :], 0.0)
    p.cstep("vector", zc)
    for j in range(NCORES):
        p.cstep("vector", lambda v, j=j: v.scalar_tensor_tensor(
            LCOL[:, :], HALL[:, j * 32 + 16:j * 32 + 32], SELP[:, j:j + 1],
            LCOL[:, :], ALU.mult, ALU.add))
        p.cstep("vector", lambda v, j=j: v.scalar_tensor_tensor(
            RCOL[:, :], HALL[:, j * 32:j * 32 + 16], SELN[:, j:j + 1],
            RCOL[:, :], ALU.mult, ALU.add))
    for c in range(NCH):
        xc = X[:, c * TC:(c + 1) * TC]
        p.cstep("vector", lambda v, c=c, xc=xc: v.tensor_scalar(
            SCRD[:, :], xc, TAB[:, 16 + c:17 + c], TAB[:, 48 + c:49 + c],
            ALU.mult, ALU.add))
        p.cstep("vector", lambda v, c=c, xc=xc: v.scalar_tensor_tensor(
            SCRD[:, 1:TC], xc[:, 0:TC - 1], TAB[:, c:c + 1], SCRD[:, 1:TC],
            ALU.mult, ALU.add))
        p.cstep("vector", lambda v, c=c: v.scalar_tensor_tensor(
            SCRD[:, 0:1], LCOL[:, c:c + 1], TAB[:, c:c + 1], SCRD[:, 0:1],
            ALU.mult, ALU.add))
        p.cstep("vector", lambda v, c=c, xc=xc: v.scalar_tensor_tensor(
            SCRD[:, 0:TC - 1], xc[:, 1:TC], TAB[:, 32 + c:33 + c],
            SCRD[:, 0:TC - 1], ALU.mult, ALU.add))
        p.cstep("vector", lambda v, c=c: v.scalar_tensor_tensor(
            SCRD[:, TC - 1:TC], RCOL[:, c:c + 1], TAB[:, 32 + c:33 + c],
            SCRD[:, TC - 1:TC], ALU.mult, ALU.add))
        p.cstep("vector", lambda v, c=c: v.tensor_copy(TCV, SCRD[:, :]))
        p.dstep("sync", lambda e, c=c: [e.dma_start(
            out=tout_d.ap()[c * 128:(c + 1) * 128, :], in_=TCV)],
            label=f"tst{c}")

    UB, GB = BF1, BF2
    H1B = SMB[:, 0:4096]
    TOUT2 = SMB[:, 4096:8192]
    ZEROSE = SMB[:, 4096:5120]
    AGALLE = SMF[:, 0:256]
    p.dstep("sync", lambda e: [
        e.dma_start(out=TAB[:, 0:16], in_=bu_ap),
        e.dma_start(out=TAB[:, 16:32], in_=bv_ap),
        e.dma_start(out=TAB[:, 32:36], in_=bg1_ap),
        e.dma_start(out=TAB[:, 36:52], in_=bg2_ap),
    ], label="loadE", n=4)
    p.cstep("vector", lambda v: v.memset(ZEROSE[:, :], 0.0))

    def ev_biasE(dst, bias_col_of, act=None):
        def f(og, banks, deps):
            def e(eng, banks=banks):
                last = None
                for (oc, th, pv) in banks:
                    seg = dst[:, oc * TC + th * 512: oc * TC + th * 512 + 512]
                    if act is None:
                        last = eng.tensor_scalar(seg, pv, bias_col_of(oc), None,
                                                 ALU.add)
                    else:
                        last = eng.activation(seg, pv, act, bias=bias_col_of(oc))
                return last
            return p.cstep("vector" if act is None else "scalar", e, deps=deps,
                           label="evb")
        return f
    emit_proj(p, WuT_ap, XN1B, D, D, WB, PS,
              ev_biasE(UB, lambda oc: TAB[:, oc:oc + 1]))
    emit_proj(p, Wg1T_ap, XN1B, D, 512, WB, PS,
              ev_biasE(H1B, lambda oc: TAB[:, 32 + oc:33 + oc], act=AF.Relu))
    emit_proj(p, Wg2T_ap, H1B, 512, D, WB, PS,
              ev_biasE(GB, lambda oc: TAB[:, 36 + oc:37 + oc], act=AF.Sigmoid))

    def ug(v):
        last = None
        for c in range(NCH):
            last = v.tensor_tensor(UB[:, c * TC:(c + 1) * TC],
                                   UB[:, c * TC:(c + 1) * TC],
                                   GB[:, c * TC:(c + 1) * TC], ALU.mult)
        return last
    p.cstep("vector", ug)

    def cum(v):
        last = None
        for c in range(NCH):
            last = v.tensor_tensor_scan(
                UB[:, c * TC:(c + 1) * TC], ZEROSE[:, :],
                UB[:, c * TC:(c + 1) * TC], 0.0, ALU.add, ALU.add)
        return last
    p.cstep("vector", cum)

    def agcp(v):
        last = None
        for c in range(NCH):
            last = v.tensor_copy(AGG[:, c:c + 1],
                                 UB[:, c * TC + TC - 1:c * TC + TC])
        return last
    p.cstep("vector", agcp)
    p.dstep("gpsimd", lambda g: [g.dma_start(out=agg_in.ap()[:, 0:16],
                                             in_=AGG[:, 0:16])])
    p.xstep("gpsimd", lambda g: [g.collective_compute(
        "AllGather", ALU.bypass, replica_groups=[list(range(NCORES))],
        ins=[agg_in.ap().opt()], outs=[agg_out.ap().opt()])], label="ag_cum")
    p.dstep("gpsimd", lambda g: [g.dma_start(
        out=AGALLE.rearrange("p (r c) -> p r c", c=32),
        in_=agg_out.ap().rearrange("(r p) c -> p r c", p=128))])

    def cm(v):
        v.memset(CAR[:, :], 0.0)
        return v.memset(SST[:, :], 0.0)
    p.cstep("vector", cm)
    for j in range(NCORES):
        p.cstep("vector", lambda v, j=j: v.scalar_tensor_tensor(
            CAR[:, :], SST[:, :], SEL[:, j:j + 1], CAR[:, :], ALU.mult, ALU.add))
        p.cstep("vector", lambda v, j=j: v.tensor_tensor(
            SST[:, :], SST[:, :], AGALLE[:, j * 32:j * 32 + 16], ALU.add))

    def ev_vg(og, banks, deps):
        def f(v, banks=banks):
            last = None
            for (oc, th, pv) in banks:
                seg = GB[:, oc * TC + th * 512: oc * TC + th * 512 + 512]
                last = v.scalar_tensor_tensor(seg, pv, TAB[:, 16 + oc:17 + oc],
                                              seg, ALU.add, ALU.mult)
            return last
        return p.cstep("vector", f, deps=deps, label="ev_vg")
    emit_proj(p, WvT_ap, XN1B, D, D, WB, PS, ev_vg)

    def go(v):
        last = None
        for c in range(NCH):
            last = v.scalar_tensor_tensor(
                GB[:, c * TC:(c + 1) * TC], UB[:, c * TC:(c + 1) * TC],
                CAR[:, c:c + 1], GB[:, c * TC:(c + 1) * TC], ALU.add, ALU.mult)
        return last
    p.cstep("vector", go)

    def ev_uv(og, banks, deps):
        ocs = sorted({oc for (oc, th, pv) in banks})
        ld = p.dstep("sync", lambda e, ocs=ocs: [e.dma_start(
            out=TOUT2[:, 0:len(ocs) * TC].rearrange("p (c t) -> p c t", t=TC),
            in_=tout_d.ap()[ocs[0] * 128:(ocs[0] + len(ocs)) * 128, :]
            .rearrange("(c p) t -> p c t", p=128))], deps=deps, label="t2ld")

        def f(v, banks=banks, ocs=ocs):
            last = None
            for (oc, th, pv) in banks:
                lc = oc - ocs[0]
                seg = TOUT2[:, lc * TC + th * 512: lc * TC + th * 512 + 512]
                last = v.tensor_tensor(seg, seg, pv, ALU.add)
            return last
        fs = p.cstep("vector", f, deps=[ld] + deps, label="uv_add")
        return p.dstep("sync", lambda e, ocs=ocs: [e.dma_start(
            out=s1_d.ap()[ocs[0] * 128:(ocs[0] + len(ocs)) * 128, :]
            .rearrange("(c p) t -> p c t", p=128),
            in_=TOUT2[:, 0:len(ocs) * TC].rearrange("p (c t) -> p c t", t=TC))],
            deps=[fs], label="s1st")
    emit_proj(p, WuT_ap, GB, D, D, WB, PS, ev_uv)

    XL1 = F32A
    S1 = BF1
    GSILF = BF2[:, 0:4096]
    UCBF = BF2[:, 4096:8192]
    SGB = SMF[:, 3480:4504]
    SROW = SMF[0:1, 2456:3480]
    WSGB = SMB[:, 2176:2192]
    p.dstep("sync", lambda e: [
        feat_load(p, e, XL1, xl1_d.ap()),
        feat_load(p, e, S1, s1_d.ap()),
    ], label="loadF", n=2)
    p.dstep("gpsimd", lambda g: [g.dma_start(out=WSGB, in_=wsg_ap)],
            label="wsg_dma")
    emit_idf_ohall(p, IDF, OHALL, IDI)
    emit_router(p, XN1B, wr1_ap, RWB, COMB, CT, RLOG, RSIG, RB0B, IDF, OHALL,
                PS, M1, "l1")

    def mmsg(t):
        last = None
        for th in range(2):
            for kt in range(NCH):
                last = t.matmul(PS[0:1, (7 - th) * 512:(7 - th) * 512 + 512],
                                WSGB[:, kt:kt + 1],
                                XN1B[:, kt * TC + th * 512:kt * TC + th * 512 + 512],
                                start=(kt == 0), stop=(kt == NCH - 1))
        return last
    p.cstep("tensor", mmsg)
    p.cstep("scalar", lambda s: s.activation(SROW[0:1, 0:512],
                                             PS[0:1, 3584:4096], AF.Sigmoid))
    p.cstep("scalar", lambda s: s.activation(SROW[0:1, 512:1024],
                                             PS[0:1, 3072:3584], AF.Sigmoid))
    for th in range(2):
        p.cstep("tensor", lambda t, th=th: t.matmul(
            PS[:, 2048:2560], ONE1[0:1, :], SROW[0:1, th * 512:(th + 1) * 512],
            start=True, stop=True))
        p.cstep("vector", lambda v, th=th: v.tensor_copy(
            SGB[:, th * 512:(th + 1) * 512], PS[:, 2048:2560]))
    emit_moe_experts(p, XN1B, wg1_ap, wu1_ap, 11, CT, OHALL, CB,
                     GSILF, UCBF, WB, PS, acts1, lambda e: e * 11, tag="f")
    emit_shared_expert(p, XN1B, sg_ap, su_ap, SGB, GSILF, UCBF, WB, PS,
                       acts1, 88, tag="fs")

    def devacF(og2, banks, deps):
        def f1(s, banks=banks):
            last = None
            for bi, (oc, th, pv) in enumerate(banks):
                last = s.activation(GSILF[:, bi * 512:(bi + 1) * 512], pv, AF.Silu)
            return last
        s1_ = p.cstep("scalar", f1, deps=deps)

        def f2(v, banks=banks):
            last = None
            for bi, (oc, th, pv) in enumerate(banks):
                last = v.tensor_tensor(
                    UCBF[:, bi * 512:(bi + 1) * 512],
                    GSILF[:, bi * 512:(bi + 1) * 512],
                    S1[:, oc * TC + th * 512: oc * TC + th * 512 + 512], ALU.mult)
            return last
        s2_ = p.cstep("vector", f2, deps=[s1_])

        def f3(v, banks=banks):
            last = None
            for bi, (oc, th, pv) in enumerate(banks):
                seg = XL1[:, oc * TC + th * 512: oc * TC + th * 512 + 512]
                last = v.tensor_tensor(seg, seg, UCBF[:, bi * 512:(bi + 1) * 512],
                                       ALU.add)
            return last
        return p.cstep("vector", f3, deps=[s2_])
    emit_moe_down(p, [(wd1_ap[e], 11, e * 11) for e in range(8)]
                  + [(sd_ap, 44, 88)], acts1, A2A, A2B, WB, PS, devacF,
                  tag="fd")

    p.dstep("sync", lambda e: [feat_store(p, e, out_out, XL1)], label="outst")
    nc = p.build()
    return p, nc


def kernel(**inputs):
    g, pc = _prep_host(inputs)
    p, nc = build_fused()
    in_maps = []
    for i in range(NCORES):
        m = dict(x=pc[i]["x"], igw=g["igw"], rgw=g["rgw"], igb=g["igb"],
                 rgb=g["rgb"], rp=g["rp"], sel=pc[i]["sel"], selp=pc[i]["selp"],
                 seln=pc[i]["seln"], first_mask=pc[i]["first_mask"],
                 wq=g["wq"], wk=g["wk"], wv=g["wv"], wo=g["wo"],
                 ln1=g["ln1"], ln2=g["ln2"], qnw=g["qnw"], knw=g["knw"],
                 cosr=pc[i]["cosr"], sinsgn=pc[i]["sinsgn"], swapm=g["swapm"],
                 qpos=pc[i]["qpos"], rw0=g["rw0"], rb0t=g["rb0t"],
                 wg0=g["wg0"], wu0=g["wu0"], wd0=g["wd0"],
                 cw=g["cw"], cb=g["cb"], WuT=g["WuT"], WvT=g["WvT"],
                 Wg1T=g["Wg1T"], Wg2T=g["Wg2T"], bu=g["bu"], bv=g["bv"],
                 bg1=g["bg1"], bg2=g["bg2"], Wr1=g["Wr1"],
                 wg1=g["wg1"], wu1=g["wu1"], wd1=g["wd1"],
                 Sg=g["Sg"], Su=g["Su"], Sd=g["Sd"], Wsg=g["Wsg"])
        in_maps.append(m)
    res = run_bass_kernel_spmd(nc, in_maps, core_ids=list(range(NCORES)))
    p.close()
    out = np.concatenate([np.asarray(res.results[i]["out"], np.float32)
                          for i in range(NCORES)], axis=1)
    return np.ascontiguousarray(out.T)[None]


# version 04
# speedup vs baseline: 2.0321x; 1.1737x over previous
import numpy as np
import ml_dtypes
from contextlib import ExitStack

import concourse.bass as bass
import concourse.mybir as mybir
from concourse.bass_utils import run_bass_kernel_spmd

F32 = mybir.dt.float32
BF16 = mybir.dt.bfloat16
AF = mybir.ActivationFunctionType
ALU = mybir.AluOpType
AX = mybir.AxisListType

NCORES = 8
TC = 1024
D = 2048
NCH = 16
BFNP = ml_dtypes.bfloat16
ENGINES = ("sync", "gpsimd", "vector", "scalar", "tensor")


class Step:
    __slots__ = ("idx", "engine", "fn", "deps", "kind", "label", "_ndma")

    def __init__(self, idx, engine, fn, deps, kind, label):
        self.idx, self.engine, self.fn = idx, engine, fn
        self.deps, self.kind, self.label = deps, kind, label
        self._ndma = 0


class P:

    def __init__(self):
        self.nc = bass.Bass("TRN2", target_bir_lowering=False, debug=True)
        self.es = ExitStack()
        self.steps = []
        self._names = set()

    def din(self, name, shape, dtype=F32):
        return self.nc.dram_tensor(name, list(shape), dtype, kind="ExternalInput").ap()

    def dout(self, name, shape, dtype=F32):
        return self.nc.dram_tensor(name, list(shape), dtype, kind="ExternalOutput").ap()

    def dint(self, name, shape, dtype=F32):
        return self.nc.dram_tensor(name, list(shape), dtype)

    def sb(self, name, shape, dtype=F32):
        assert name not in self._names, name
        self._names.add(name)
        return self.es.enter_context(self.nc.sbuf_tensor(name, list(shape), dtype))

    def ps(self, name, shape, dtype=F32):
        assert name not in self._names, name
        self._names.add(name)
        return self.es.enter_context(self.nc.psum_tensor(name, list(shape), dtype))

    def cstep(self, engine, fn, deps="all", label=""):
        s = Step(len(self.steps), engine, fn, deps, "c", label)
        self.steps.append(s)
        return s

    def dstep(self, engine, fn, deps="all", label="", n=None):
        s = Step(len(self.steps), engine, fn, deps, "d", label)
        s._ndma = 1 if n is None else n
        self.steps.append(s)
        return s

    def xstep(self, engine, fn, deps="all", label="", n=1):
        s = Step(len(self.steps), engine, fn, deps, "x", label)
        s._ndma = n
        self.steps.append(s)
        return s

    def build(self):
        used = {(st.engine, st.kind) for st in self.steps if st.kind != "x"}
        chans = [(e, k) for e in ENGINES for k in ("c", "d")
                 if (e, k) in used]
        for st in self.steps:
            if st.kind == "x":
                chans.append((st.engine, f"x{st.idx}"))
        sems = {ch: self.es.enter_context(
            self.nc.semaphore(f"sem_{ch[0]}_{ch[1]}")) for ch in chans}
        blk = self.es.enter_context(self.nc.Block())

        counts = {ch: 0 for ch in chans}
        after = []
        waits = []
        for st in self.steps:
            mych = (st.engine, st.kind if st.kind != "x" else f"x{st.idx}")
            if st.deps == "all":
                tg = {ch: counts[ch] for ch in chans if counts[ch] > 0}
            else:
                tg = {}
                for d in st.deps:
                    for ch in chans:
                        v = after[d.idx][ch]
                        if v > tg.get(ch, 0):
                            tg[ch] = v
                for ch in chans:
                    if ch[0] == st.engine and counts.get(ch, 0) > tg.get(ch, 0):
                        tg[ch] = counts[ch]
            waits.append(tg)
            if st.kind == "c":
                counts[mych] += 1
            elif st.kind == "x":
                counts[mych] += 1
            else:
                assert st._ndma is not None, f"step {st.idx} {st.label}: dstep needs n="
                counts[mych] += 16 * st._ndma
            after.append(dict(counts))

        by_eng = {e: [st for st in self.steps if st.engine == e] for e in ENGINES}
        for e in ENGINES:
            if not by_eng[e] and e != "sync":
                continue

            def section(eng, e=e):
                for st in by_eng[e]:
                    for ch, v in waits[st.idx].items():
                        eng.wait_ge(sems[ch], v)
                    if st.kind == "c":
                        last = st.fn(eng)
                        assert last is not None, (
                            f"step {st.idx} {st.label}: fn must return last instr")
                        last.then_inc(sems[(e, "c")], 1)
                    elif st.kind == "x":
                        instrs = st.fn(eng)
                        assert len(instrs) == 1, "one collective per xstep"
                        instrs[0].then_inc(sems[(e, f"x{st.idx}")])
                    else:
                        instrs = st.fn(eng)
                        assert len(instrs) == st._ndma, (
                            f"step {st.idx} {st.label}: declared n={st._ndma} "
                            f"!= emitted {len(instrs)}")
                        for ins in instrs:
                            ins.then_inc(sems[(e, "d")], 16)
                if e == "sync":
                    for ch in chans:
                        if counts[ch] > 0:
                            eng.wait_ge(sems[ch], counts[ch])
            getattr(blk, e)(section)

        return self.nc

    def close(self):
        self.es.close()


def chunkc(ap, c, width=TC):
    return ap[:, c * width:(c + 1) * width]




def _prep_host(inputs):
    g = {}
    pc = [dict() for _ in range(NCORES)]

    x = np.ascontiguousarray(np.asarray(inputs["L0_attn_x"], np.float32)[0].T)
    for i in range(NCORES):
        pc[i]["x"] = np.ascontiguousarray(x[:, i * TC:(i + 1) * TC])

    A = lambda k: np.ascontiguousarray(np.asarray(inputs[k], np.float32))
    g["igw"] = A("L0_attn_input_gate_weight")
    g["rgw"] = A("L0_attn_recurrent_gate_weight")
    g["igb"] = np.ascontiguousarray(A("L0_attn_input_gate_bias").T)
    g["rgb"] = np.ascontiguousarray(A("L0_attn_recurrent_gate_bias").T)
    g["rp"] = np.ascontiguousarray(A("L0_attn_recurrent_param").reshape(16, 128).T)

    g["ln1"] = np.ascontiguousarray(A("L0_ffn_ln1_w").reshape(16, 128).T)
    g["ln2"] = np.ascontiguousarray(A("L0_ffn_ln2_w").reshape(16, 128).T)
    g["qnw"] = np.ascontiguousarray(A("L0_ffn_qnorm_w").reshape(16, 128).T)
    g["knw"] = np.ascontiguousarray(A("L0_ffn_knorm_w").reshape(4, 128).T)
    g["wq"] = A("L0_ffn_wq")
    g["wk"] = A("L0_ffn_wk")
    g["wv"] = A("L0_ffn_wv")
    g["wo"] = A("L0_ffn_wo")
    g["rw0"] = A("L0_ffn_router_w")
    g["rb0t"] = np.ascontiguousarray(
        np.tile(A("L0_ffn_router_bias").reshape(1, 8), (128, 8)))
    g["wg0"] = A("L0_ffn_wg")
    g["wu0"] = A("L0_ffn_wu")
    g["wd0"] = A("L0_ffn_wd")

    g["WuT"] = np.ascontiguousarray(A("L1_attn_Wu").T)
    g["WvT"] = np.ascontiguousarray(A("L1_attn_Wv").T)
    g["Wg1T"] = np.ascontiguousarray(A("L1_attn_Wg1").T)
    g["Wg2T"] = np.ascontiguousarray(A("L1_attn_Wg2").T)
    g["bu"] = np.ascontiguousarray(A("L1_attn_bu").reshape(16, 128).T)
    g["bv"] = np.ascontiguousarray(A("L1_attn_bv").reshape(16, 128).T)
    g["bg1"] = np.ascontiguousarray(A("L1_attn_bg1").reshape(4, 128).T)
    g["bg2"] = np.ascontiguousarray(A("L1_attn_bg2").reshape(16, 128).T)
    g["cw"] = np.ascontiguousarray(
        A("L1_attn_conv_w")[:, 0, :].reshape(16, 128, 3).transpose(2, 1, 0))
    g["cb"] = np.ascontiguousarray(A("L1_attn_conv_b").reshape(16, 128).T)

    g["Wr1"] = A("L1_ffn_Wr")
    g["wg1"] = A("L1_ffn_Wg")
    g["wu1"] = A("L1_ffn_Wu")
    g["wd1"] = A("L1_ffn_Wd")
    g["Sg"] = A("L1_ffn_Sg")
    g["Su"] = A("L1_ffn_Su")
    g["Sd"] = A("L1_ffn_Sd")
    g["Wsg"] = np.ascontiguousarray(
        A("L1_ffn_Wsg").reshape(16, 128).T)

    swapm = np.zeros((64, 64), np.float32)
    for m in range(64):
        swapm[(m + 32) % 64, m] = 1.0
    g["swapm"] = swapm.astype(BFNP)

    theta = 1e6
    inv = 1.0 / theta ** (np.arange(0, 64, 2, dtype=np.float32) / 64)
    for i in range(NCORES):
        pos = (i * TC + np.arange(TC)).astype(np.float32)
        ang = pos[:, None] * inv[None, :]
        emb = np.concatenate([ang, ang], 1)
        pc[i]["cosr"] = np.ascontiguousarray(np.cos(emb).T.astype(np.float32))
        sint = np.sin(emb).T.astype(np.float32)
        sinsgn = sint.copy()
        sinsgn[0:32] = -sint[0:32]
        pc[i]["sinsgn"] = np.ascontiguousarray(sinsgn)
        pc[i]["qpos"] = pos.reshape(1, TC).copy()
        sel = np.zeros((128, 8), np.float32)
        sel[:, i] = 1.0
        pc[i]["sel"] = sel
        selp = np.zeros((128, 8), np.float32)
        if i > 0:
            selp[:, i - 1] = 1.0
        pc[i]["selp"] = selp
        seln = np.zeros((128, 8), np.float32)
        if i < NCORES - 1:
            seln[:, i + 1] = 1.0
        pc[i]["seln"] = seln
        fm = np.zeros((128, 1), np.float32)
        if i == 0:
            fm[:, 0] = 1.0
        pc[i]["first_mask"] = fm
    return g, pc




def feat_store(p, eng, dram_ap, sbuf, dtype_elems=TC):
    return eng.dma_start(
        out=dram_ap.rearrange("(c p) t -> p c t", p=128),
        in_=sbuf[:, :].rearrange("p (c t) -> p c t", t=dtype_elems))


def feat_load(p, eng, sbuf, dram_ap, dtype_elems=TC):
    return eng.dma_start(
        out=sbuf[:, :].rearrange("p (c t) -> p c t", t=dtype_elems),
        in_=dram_ap.rearrange("(c p) t -> p c t", p=128))


def emit_ln(p, X, XNB, SQB, XB, MB, RB, PSQ, PSS, PSB, ONEBF, ONE1,
            eps=1e-5, gain=None, out_f32_inplace=True, nch=NCH, width=TC):
    dim = 128 * nch
    for th in range(2):
        def sq(s, th=th):
            last = None
            for c in range(nch):
                last = s.activation(SQB[:, c * 512:(c + 1) * 512],
                                    X[:, c * width + th * 512: c * width + th * 512 + 512],
                                    AF.Square)
            return last
        p.cstep("scalar", sq, label="ln_sq")
        if gain is None:
            def cp(v, th=th):
                last = None
                for c in range(nch):
                    last = v.tensor_copy(XB[:, c * 512:(c + 1) * 512],
                                         X[:, c * width + th * 512: c * width + th * 512 + 512])
                return last
            p.cstep("vector", cp, label="ln_cp")

        def mm(t, th=th):
            last = None
            for c in range(nch):
                last = t.matmul(PSQ[:], ONEBF[:, 0:1], SQB[:, c * 512:(c + 1) * 512],
                                start=(c == 0), stop=(c == nch - 1))
            if gain is None:
                for c in range(nch):
                    last = t.matmul(PSS[:], ONEBF[:, 0:1], XB[:, c * 512:(c + 1) * 512],
                                    start=(c == 0), stop=(c == nch - 1))
            return last
        p.cstep("tensor", mm, label="ln_mm")

        if gain is None:
            def st1(v, th=th):
                return v.tensor_scalar(MB[0:1, 0:512], PSS[:], 1.0 / dim, None, ALU.mult)
            p.cstep("vector", st1, label="ln_mean")

            def st2(s, th=th):
                return s.activation(RB[0:1, 0:512], MB[0:1, 0:512], AF.Square)
            p.cstep("scalar", st2, label="ln_m2")

            def st3(v, th=th):
                return v.scalar_tensor_tensor(RB[0:1, 0:512], PSQ[:], 1.0 / dim,
                                              RB[0:1, 0:512], ALU.mult, ALU.subtract)
            p.cstep("vector", st3, label="ln_var")
            p.cstep("vector", lambda v: v.tensor_scalar(
                RB[0:1, 0:512], RB[0:1, 0:512], eps, None, ALU.add), label="ln_eps")
        else:
            def st3(v, th=th):
                return v.tensor_scalar(RB[0:1, 0:512], PSQ[:], 1.0 / dim, eps,
                                       ALU.mult, ALU.add)
            p.cstep("vector", st3, label="ln_var")

        def st4(s, th=th):
            return s.activation(RB[0:1, 0:512], RB[0:1, 0:512], AF.Sqrt)
        p.cstep("scalar", st4, label="ln_sqrt")

        def st5(v, th=th):
            return v.reciprocal(RB[0:1, 0:512], RB[0:1, 0:512])
        p.cstep("vector", st5, label="ln_recip")

        def bc1(t, th=th):
            return t.matmul(PSB[:], ONE1[0:1, :], RB[0:1, 0:512], start=True, stop=True)
        p.cstep("tensor", bc1, label="ln_bc_r")

        def cpr(v, th=th):
            return v.tensor_copy(RB[:, 0:512], PSB[:])
        p.cstep("vector", cpr, label="ln_cp_r")

        if gain is None:
            def bc2(t, th=th):
                return t.matmul(PSB[:], ONE1[0:1, :], MB[0:1, 0:512], start=True, stop=True)
            p.cstep("tensor", bc2, label="ln_bc_m")

            def cpm(v, th=th):
                return v.tensor_copy(MB[:, 0:512], PSB[:])
            p.cstep("vector", cpm, label="ln_cp_m")

        def segf(c, th):
            return X[:, c * width + th * 512: c * width + th * 512 + 512]

        def segb(c, th):
            return XNB[:, c * width + th * 512: c * width + th * 512 + 512]
        if gain is None:
            def n1(v, th=th):
                last = None
                for c in range(nch):
                    last = v.scalar_tensor_tensor(segf(c, th), segf(c, th), 1.0,
                                                  MB[:, 0:512], ALU.mult, ALU.subtract)
                return last
            p.cstep("vector", n1, label="ln_n1")

            def n2(v, th=th):
                last = None
                for c in range(nch):
                    last = v.tensor_tensor(segf(c, th), segf(c, th), RB[:, 0:512], ALU.mult)
                return last
            p.cstep("vector", n2, label="ln_n2")

            def n3(v, th=th):
                last = None
                for c in range(nch):
                    last = v.tensor_copy(segb(c, th), segf(c, th))
                return last
            p.cstep("vector", n3, label="ln_n3")
        else:
            def n1(v, th=th):
                last = None
                for c in range(nch):
                    last = v.scalar_tensor_tensor(segb(c, th), segf(c, th),
                                                  gain[:, c:c + 1], RB[:, 0:512],
                                                  ALU.mult, ALU.mult)
                return last
            p.cstep("vector", n1, label="ln_rms")


def emit_softplus8(p, TAB, SCR):
    p.cstep("scalar", lambda s: (s.activation(TAB[:, 48:64], TAB[:, 32:48], AF.Abs),
                                 s.activation(SCR[:, 32:48], TAB[:, 32:48], AF.Relu))[-1])
    p.cstep("scalar", lambda s: s.activation(SCR[:, 0:16], TAB[:, 48:64], AF.Exp, scale=-1.0))
    p.cstep("scalar", lambda s: s.activation(SCR[:, 16:32], SCR[:, 0:16], AF.Ln, bias=1.0))
    p.cstep("vector", lambda v: v.scalar_tensor_tensor(
        TAB[:, 48:64], SCR[:, 16:32], 1.0, SCR[:, 32:48], ALU.mult, ALU.add))
    p.cstep("vector", lambda v: v.tensor_scalar(
        TAB[:, 48:64], TAB[:, 48:64], -8.0, None, ALU.mult))


def emit_scan_branch(p, X, XNB, ALOC, YLOC, SCR, ZEROS, TAB, SEL, FMASK,
                     WGT, AGG, AGALL, CAR, SST, PG0, PG1,
                     igw_ap, rgw_ap, agg_in, agg_out):
    s_gx, s_ga = 0, 1024
    s_a, s_m, s_b = 2048, 3072, 1024
    for h in range(16):
        p.dstep("gpsimd", lambda g, h=h: [
            g.dma_start(out=WGT[:, :], in_=igw_ap[h])], label=f"gw{h}")

        def mm1(t, h=h):
            t.matmul(PG0[:], WGT[:, :], XNB[:, h * TC:h * TC + 512],
                     start=True, stop=True)
            return t.matmul(PG1[:], WGT[:, :], XNB[:, h * TC + 512:h * TC + 1024],
                            start=True, stop=True)
        p.cstep("tensor", mm1)

        def ev1(s, h=h):
            s.activation(SCR[:, s_gx:s_gx + 512], PG0[:], AF.Sigmoid,
                         bias=TAB[:, h:h + 1])
            return s.activation(SCR[:, s_gx + 512:s_gx + 1024], PG1[:], AF.Sigmoid,
                                bias=TAB[:, h:h + 1])
        p.cstep("scalar", ev1)

        p.dstep("gpsimd", lambda g, h=h: [
            g.dma_start(out=WGT[:, :], in_=rgw_ap[h])], label=f"rw{h}")

        def mm2(t, h=h):
            t.matmul(PG0[:], WGT[:, :], XNB[:, h * TC:h * TC + 512],
                     start=True, stop=True)
            return t.matmul(PG1[:], WGT[:, :], XNB[:, h * TC + 512:h * TC + 1024],
                            start=True, stop=True)
        p.cstep("tensor", mm2)

        def ev2(s, h=h):
            s.activation(SCR[:, s_ga:s_ga + 512], PG0[:], AF.Sigmoid,
                         bias=TAB[:, 16 + h:17 + h])
            return s.activation(SCR[:, s_ga + 512:s_ga + 1024], PG1[:], AF.Sigmoid,
                                bias=TAB[:, 16 + h:17 + h])
        p.cstep("scalar", ev2)

        def e1(v, h=h):
            return v.tensor_scalar(SCR[:, s_a:s_a + 1024], SCR[:, s_ga:s_ga + 1024],
                                   TAB[:, 48 + h:49 + h], None, ALU.mult)
        p.cstep("vector", e1)

        def e2(s, h=h):
            return s.activation(SCR[:, s_a:s_a + 1024], SCR[:, s_a:s_a + 1024], AF.Exp)
        p.cstep("scalar", e2)

        def e3(s, h=h):
            return s.activation(SCR[:, s_m:s_m + 1024], SCR[:, s_a:s_a + 1024], AF.Square)
        p.cstep("scalar", e3)

        def e4(v, h=h):
            return v.tensor_scalar(SCR[:, s_m:s_m + 1024], SCR[:, s_m:s_m + 1024],
                                   -1.0, 1.0, ALU.mult, ALU.add)
        p.cstep("vector", e4)

        def e5(s, h=h):
            return s.activation(SCR[:, s_m:s_m + 1024], SCR[:, s_m:s_m + 1024], AF.Sqrt)
        p.cstep("scalar", e5)

        p.cstep("vector", lambda v, h=h: v.tensor_scalar(
            SCR[:, s_b:s_b + 1], FMASK[:, 0:1], -1.0, 1.0, ALU.mult, ALU.add))
        p.cstep("vector", lambda v, h=h: v.scalar_tensor_tensor(
            SCR[:, s_m:s_m + 1], SCR[:, s_m:s_m + 1], SCR[:, s_b:s_b + 1],
            FMASK[:, 0:1], ALU.mult, ALU.add))
        p.cstep("vector", lambda v, h=h: v.tensor_tensor(
            SCR[:, s_b:s_b + 1024], X[:, h * TC:(h + 1) * TC],
            SCR[:, s_gx:s_gx + 1024], ALU.mult))
        p.cstep("vector", lambda v, h=h: v.tensor_tensor(
            SCR[:, s_b:s_b + 1024], SCR[:, s_b:s_b + 1024],
            SCR[:, s_m:s_m + 1024], ALU.mult))

        def e7(v, h=h):
            v.tensor_tensor_scan(ALOC[:, h * TC:(h + 1) * TC],
                                 SCR[:, s_a:s_a + 1024], ZEROS[:, 0:1024],
                                 1.0, ALU.mult, ALU.add)
            return v.tensor_tensor_scan(
                YLOC[:, h * TC:(h + 1) * TC], SCR[:, s_a:s_a + 1024],
                SCR[:, s_b:s_b + 1024], 0.0, ALU.mult, ALU.add)
        p.cstep("vector", e7)

        def e8(v, h=h):
            v.tensor_copy(AGG[:, h:h + 1], ALOC[:, h * TC + 1023:h * TC + 1024])
            return v.tensor_copy(AGG[:, 16 + h:17 + h],
                                 YLOC[:, h * TC + 1023:h * TC + 1024])
        p.cstep("vector", e8)

    p.dstep("gpsimd", lambda g: [g.dma_start(out=agg_in[:, :], in_=AGG[:, :])],
            label="agg_out_dma")
    import os
    if os.environ.get("NO_CC"):
        p.dstep("gpsimd", lambda g: [g.dma_start(
            out=agg_out[0:128, :], in_=agg_in[:, :])], label="agg_fake")
    else:
        p.xstep("gpsimd", lambda g: [g.collective_compute(
            "AllGather", ALU.bypass, replica_groups=[list(range(NCORES))],
            ins=[agg_in[:, :]], outs=[agg_out[:, :]])], label="agg_ag")
    p.dstep("gpsimd", lambda g: [g.dma_start(
        out=AGALL[:, :].rearrange("p (r c) -> p r c", c=32),
        in_=agg_out.ap().rearrange("(r p) c -> p r c", p=128))],
        label="agg_in_dma")

    def cm(v):
        v.memset(CAR[:, :], 0.0)
        return v.memset(SST[:, :], 0.0)
    p.cstep("vector", cm)
    for j in range(NCORES):
        p.cstep("vector", lambda v, j=j: v.scalar_tensor_tensor(
            CAR[:, :], SST[:, :], SEL[:, j:j + 1], CAR[:, :], ALU.mult, ALU.add))
        p.cstep("vector", lambda v, j=j: v.tensor_tensor(
            SST[:, :], SST[:, :], AGALL[:, j * 32:j * 32 + 16], ALU.mult))
        p.cstep("vector", lambda v, j=j: v.tensor_tensor(
            SST[:, :], SST[:, :], AGALL[:, j * 32 + 16:j * 32 + 32], ALU.add))

    def corr(v):
        last = None
        for h in range(16):
            last = v.scalar_tensor_tensor(
                YLOC[:, h * TC:(h + 1) * TC], ALOC[:, h * TC:(h + 1) * TC],
                CAR[:, h:h + 1], YLOC[:, h * TC:(h + 1) * TC], ALU.mult, ALU.add)
        return last
    p.cstep("vector", corr, label="carry_corr")


def build_launch_A(dbg=False):
    p = P()
    x_in = p.din("x", [D, TC])
    igw_ap = p.din("igw", [16, 128, 128])
    rgw_ap = p.din("rgw", [16, 128, 128])
    igb_ap = p.din("igb", [128, 16])
    rgb_ap = p.din("rgb", [128, 16])
    rp_ap = p.din("rp", [128, 16])
    sel_ap = p.din("sel", [128, 8])
    fm_ap = p.din("first_mask", [128, 1])
    agg_in = p.dint("agg_in", [128, 32])
    agg_out = p.dint("agg_out", [128 * NCORES, 32])
    xn_out = p.dout("xn", [D, TC])
    s0_out = p.dout("s0", [D, TC], BF16)

    X = p.sb("X", [128, NCH * TC])
    XNB = p.sb("XNB", [128, NCH * TC], BF16)
    ALOC = p.sb("ALOC", [128, NCH * TC], BF16)
    YLOC = p.sb("YLOC", [128, NCH * TC], BF16)
    SCR = p.sb("SCR", [128, 5 * 1024])
    ZEROS = p.sb("ZEROS", [128, 1024])
    MB = p.sb("MB", [128, 512])
    RB = p.sb("RB", [128, 512])
    TAB = p.sb("TAB", [128, 64])
    SEL = p.sb("SEL", [128, 8])
    FMASK = p.sb("FMASK", [128, 1])
    ONEBF = p.sb("ONEBF", [128, 1], BF16)
    ONE1 = p.sb("ONE1", [1, 128])
    WGT = p.sb("WGT", [128, 128], BF16)
    AGG = p.sb("AGG", [128, 32])
    AGALL = p.sb("AGALL", [128, 256])
    CAR = p.sb("CAR", [128, 16])
    SST = p.sb("SST", [128, 16])
    PSQ = p.ps("PSQ", [1, 512])
    PSS = p.ps("PSS", [1, 512])
    PSB = p.ps("PSB", [128, 512])
    PG0 = p.ps("PG0", [128, 512])
    PG1 = p.ps("PG1", [128, 512])

    p.dstep("sync", lambda e: [
        feat_load(p, e, X, x_in),
        e.dma_start(out=TAB[:, 0:16], in_=igb_ap),
        e.dma_start(out=TAB[:, 16:32], in_=rgb_ap),
        e.dma_start(out=TAB[:, 32:48], in_=rp_ap),
        e.dma_start(out=SEL[:, :], in_=sel_ap),
        e.dma_start(out=FMASK[:, :], in_=fm_ap),
    ], label="load", n=6)

    def init(v):
        v.memset(ZEROS[:, :], 0.0)
        v.memset(ONEBF[:, :], 1.0)
        return v.memset(ONE1[:, :], 1.0)
    p.cstep("vector", init, label="init")

    import os as _os
    emit_softplus8(p, TAB, SCR)
    if not _os.environ.get("NO_LN"):
        emit_ln(p, X, XNB, ALOC[:, 0:NCH * 512], YLOC[:, 0:NCH * 512],
                MB, RB, PSQ, PSS, PSB, ONEBF, ONE1)
    if _os.environ.get("NO_GATES"):
        def fake(v):
            return v.tensor_copy(AGG[:, 0:32], AGALL[:, 0:32])
        p.cstep("vector", fake)
        p.dstep("gpsimd", lambda g: [g.dma_start(out=agg_in[:, :], in_=AGG[:, :])])
        p.xstep("gpsimd", lambda g: [g.collective_compute(
            "AllGather", ALU.bypass, replica_groups=[list(range(NCORES))],
            ins=[agg_in.ap().opt()], outs=[agg_out.ap().opt()])], label="agg_ag")
        p.dstep("gpsimd", lambda g: [g.dma_start(
            out=AGALL[:, :].rearrange("p (r c) -> p r c", c=32),
            in_=agg_out.ap().rearrange("(r p) c -> p r c", p=128))])
    else:
        emit_scan_branch(p, X, XNB, ALOC, YLOC, SCR, ZEROS, TAB, SEL, FMASK,
                         WGT, AGG, AGALL, CAR, SST, PG0, PG1,
                         igw_ap, rgw_ap, agg_in, agg_out)

    import os as _os2
    if _os2.environ.get("NO_BF16STORE"):
        p.dstep("sync", lambda e: [feat_store(p, e, xn_out, X)], label="store")
        p.dstep("sync", lambda e: [e.dma_start(out=s0_out.rearrange(
            "(c p) t -> p c t", p=128)[:, 0:1, :],
            in_=YLOC[:, 0:TC].rearrange("p (c t) -> p c t", t=TC))], label="store2")
    else:
        p.dstep("sync", lambda e: [
            feat_store(p, e, xn_out, X),
            feat_store(p, e, s0_out, YLOC),
        ], label="store", n=2)

    nc = p.build()
    return p, nc




def emit_proj(p, w_ap, rhs_bf, Din, Dout, WB, PS, evac_fn):
    prev_evac = None
    for og in range(Dout // 512):
        ocs = list(range(og * 4, og * 4 + 4))
        banks, last_mm = emit_proj_group(p, w_ap, rhs_bf, ocs, Din, WB, PS,
                                         tag=f"p{og}", prev_evac=prev_evac)
        prev_evac = evac_fn(og, banks, [last_mm])


def emit_rownorm_scale(p, QB, gaintab, nch_q, PS, SQB, RROW, RBC, ONEBF, ONE1,
                       eps, tag, extra_scale=1.0):
    dim = nch_q * 128
    for th in range(2):
        def sq(s, th=th):
            last = None
            for c in range(nch_q):
                last = s.activation(SQB[:, c * 512:(c + 1) * 512],
                                    QB[:, c * TC + th * 512:c * TC + th * 512 + 512],
                                    AF.Square)
            return last
        p.cstep("scalar", sq, label=f"{tag}sq")

        def mmq(t, th=th):
            last = None
            for c in range(nch_q):
                last = t.matmul(PS[0:1, 3584:4096], ONEBF[:, 0:1],
                                SQB[:, c * 512:(c + 1) * 512],
                                start=(c == 0), stop=(c == nch_q - 1))
            return last
        p.cstep("tensor", mmq, label=f"{tag}mm")
        p.cstep("vector", lambda v: v.tensor_scalar(
            RROW[0:1, 0:512], PS[0:1, 3584:4096], 1.0 / dim, eps, ALU.mult, ALU.add))
        p.cstep("scalar", lambda s: s.activation(
            RROW[0:1, 0:512], RROW[0:1, 0:512], AF.Sqrt))
        p.cstep("vector", lambda v: v.reciprocal(RROW[0:1, 0:512], RROW[0:1, 0:512]))
        if extra_scale != 1.0:
            p.cstep("vector", lambda v: v.tensor_scalar(
                RROW[0:1, 0:512], RROW[0:1, 0:512], extra_scale, None, ALU.mult))
        p.cstep("tensor", lambda t: t.matmul(
            PS[:, 3584:4096], ONE1[0:1, :], RROW[0:1, 0:512], start=True, stop=True))
        p.cstep("vector", lambda v: v.tensor_copy(RBC[:, 0:512], PS[:, 3584:4096]))

        def scl(v, th=th):
            last = None
            for c in range(nch_q):
                seg = QB[:, c * TC + th * 512:c * TC + th * 512 + 512]
                last = v.scalar_tensor_tensor(seg, seg, gaintab[:, c:c + 1],
                                              RBC[:, 0:512], ALU.mult, ALU.mult)
            return last
        p.cstep("vector", scl, label=f"{tag}scl")


def emit_rope(p, QB, nch_q, COSR, SINSGN, SWAPM, PS, SCR):
    for h in range(nch_q):
        def mmswap(t, h=h):
            t.matmul(PS[0:64, 0:512], SWAPM[:, :],
                     QB[0:64, h * TC:h * TC + 512], start=True, stop=True)
            return t.matmul(PS[0:64, 512:1024], SWAPM[:, :],
                            QB[0:64, h * TC + 512:h * TC + 1024],
                            start=True, stop=True)
        p.cstep("tensor", mmswap)
        p.cstep("vector", lambda v, h=h: v.tensor_copy(
            SCR[0:64, 0:1024], PS[0:64, 0:1024]))
        p.cstep("vector", lambda v, h=h: v.tensor_tensor(
            SCR[0:64, 1024:2048], QB[0:64, h * TC:(h) * TC + 1024],
            COSR[0:64, :], ALU.mult))
        p.cstep("vector", lambda v, h=h: v.tensor_tensor(
            SCR[0:64, 0:1024], SCR[0:64, 0:1024], SINSGN[0:64, :], ALU.mult))
        p.cstep("vector", lambda v, h=h: v.tensor_tensor(
            QB[0:64, h * TC:h * TC + 1024], SCR[0:64, 1024:2048],
            SCR[0:64, 0:1024], ALU.add))


def emit_router(p, TB, rw_slab_ap, RWB, COMB, CT, RLOG, RSIG, RB0B, IDF, OHALL,
                PS, M1, kind, bias_tiled_ap=None):
    p.dstep("gpsimd", lambda g: [g.dma_start(
        out=RWB[:, :].rearrange("p (k e) -> p k e", e=8),
        in_=rw_slab_ap.rearrange("(k p) e -> p k e", p=128))], label="rw_dma")
    if bias_tiled_ap is not None:
        p.dstep("sync", lambda e: [e.dma_start(out=RB0B[:, :], in_=bias_tiled_ap)],
                label="rb_dma")

    def mmr(t):
        last = None
        for tt in range(8):
            for kt in range(NCH):
                last = t.matmul(PS[:, tt * 8: tt * 8 + 8],
                                TB[:, kt * TC + tt * 128: kt * TC + tt * 128 + 128],
                                RWB[:, kt * 8:(kt + 1) * 8],
                                start=(kt == 0), stop=(kt == NCH - 1))
        return last
    p.cstep("tensor", mmr, label="r_mm")
    p.cstep("vector", lambda v: v.tensor_copy(RLOG[:, :], PS[:, 0:64]))

    if kind == "l0":
        p.cstep("scalar", lambda s: s.activation(RSIG[:, :], RLOG[:, :], AF.Sigmoid))
        p.cstep("vector", lambda v: v.tensor_tensor(RLOG[:, :], RSIG[:, :],
                                                    RB0B[:, :], ALU.add))
        SELV = RLOG
        W = RSIG
    else:
        p.cstep("vector", lambda v: v.tensor_reduce(
            M1[:, 0:8], RLOG[:, :].rearrange("p (t e) -> p t e", e=8),
            AX.X, ALU.max))
        def sub(v):
            last = None
            for tt in range(8):
                last = v.tensor_scalar(RLOG[:, tt * 8:(tt + 1) * 8],
                                       RLOG[:, tt * 8:(tt + 1) * 8],
                                       M1[:, tt:tt + 1], None, ALU.subtract)
            return last
        p.cstep("vector", sub)
        p.cstep("scalar", lambda s: s.activation(RSIG[:, :], RLOG[:, :], AF.Exp))
        p.cstep("vector", lambda v: v.tensor_reduce(
            M1[:, 8:16], RSIG[:, :].rearrange("p (t e) -> p t e", e=8),
            AX.X, ALU.add))
        p.cstep("vector", lambda v: v.reciprocal(M1[:, 8:16], M1[:, 8:16]))
        def nrm(v):
            last = None
            for tt in range(8):
                last = v.tensor_scalar(RSIG[:, tt * 8:(tt + 1) * 8],
                                       RSIG[:, tt * 8:(tt + 1) * 8],
                                       M1[:, 8 + tt:9 + tt], None, ALU.mult)
            return last
        p.cstep("vector", nrm)
        SELV = RSIG
        W = RSIG

    p.cstep("vector", lambda v: v.tensor_reduce(
        M1[:, 0:8], SELV[:, :].rearrange("p (t e) -> p t e", e=8), AX.X, ALU.max))
    for tt in range(8):
        sl = slice(tt * 8, (tt + 1) * 8)
        p.cstep("vector", lambda v, tt=tt, sl=sl: v.tensor_scalar(
            COMB[:, sl], SELV[:, sl], M1[:, tt:tt + 1], None, ALU.is_ge))
        p.cstep("vector", lambda v, tt=tt, sl=sl: v.scalar_tensor_tensor(
            M1[:, 16 + tt * 8:16 + tt * 8 + 8]
            if False else RLOG[:, sl], COMB[:, sl], -40000.0, SELV[:, sl],
            ALU.mult, ALU.add))
    p.cstep("vector", lambda v: v.tensor_reduce(
        M1[:, 8:16], RLOG[:, :].rearrange("p (t e) -> p t e", e=8), AX.X, ALU.max))
    for tt in range(8):
        sl = slice(tt * 8, (tt + 1) * 8)
        p.cstep("vector", lambda v, tt=tt, sl=sl: v.tensor_scalar(
            RLOG[:, sl], RLOG[:, sl], M1[:, 8 + tt:9 + tt], None, ALU.is_ge))
        p.cstep("vector", lambda v, tt=tt, sl=sl: v.tensor_tensor(
            COMB[:, sl], COMB[:, sl], RLOG[:, sl], ALU.add))
        p.cstep("vector", lambda v, tt=tt, sl=sl: v.tensor_tensor(
            COMB[:, sl], COMB[:, sl], W[:, sl], ALU.mult))
    if kind == "l0":
        p.cstep("vector", lambda v: v.tensor_reduce(
            M1[:, 0:8], COMB[:, :].rearrange("p (t e) -> p t e", e=8),
            AX.X, ALU.add))
        p.cstep("vector", lambda v: v.reciprocal(M1[:, 0:8], M1[:, 0:8]))
        def nm(v):
            last = None
            for tt in range(8):
                last = v.tensor_scalar(COMB[:, tt * 8:(tt + 1) * 8],
                                       COMB[:, tt * 8:(tt + 1) * 8],
                                       M1[:, tt:tt + 1], None, ALU.mult)
            return last
        p.cstep("vector", nm)

    for tt in range(8):
        p.cstep("tensor", lambda t, tt=tt: t.transpose(
            PS[0:8, tt * 128:(tt + 1) * 128], COMB[:, tt * 8:(tt + 1) * 8],
            IDF[:, :]))
    p.cstep("vector", lambda v: v.tensor_copy(CT[0:8, :], PS[0:8, 0:1024]))


def emit_proj_group(p, w_ap, rhs_bf, ocs, Din, WB, PS, tag="", prev_evac=None):
    nkt = Din // 128
    now = len(ocs) * 128
    o0 = ocs[0] * 128
    nb = (nkt + 3) // 4
    mm_steps = []
    for b in range(nb):
        kts = list(range(b * 4, min(b * 4 + 4, nkt)))
        deps = []
        if b >= 2:
            deps.append(mm_steps[b - 2])
        elif prev_evac is not None:
            deps.append(prev_evac)
        if not deps:
            deps = [p.steps[-1]]
        buf = WB[b % 2]
        p.dstep("gpsimd", lambda g, kts=kts, buf=buf: [g.dma_start(
            out=buf[:, 0:len(kts) * now].rearrange("p (c n) -> p c n", n=now),
            in_=w_ap[kts[0] * 128:(kts[-1] + 1) * 128, o0:o0 + now]
            .rearrange("(c p) n -> p c n", p=128))],
            deps=deps, label=f"pg{tag}")

        def mm(t, kts=kts, buf=buf, b=b):
            last = None
            for i, kt in enumerate(kts):
                for j in range(len(ocs)):
                    for th in range(2):
                        last = t.matmul(
                            PS[:, (j * 2 + th) * 512:(j * 2 + th) * 512 + 512],
                            buf[:, i * now + j * 128:i * now + (j + 1) * 128],
                            rhs_bf[:, kt * TC + th * 512:kt * TC + th * 512 + 512],
                            start=(kt == 0), stop=(kt == nkt - 1))
            return last
        mdeps = [p.steps[-1]]
        if b == 0 and prev_evac is not None:
            mdeps.append(prev_evac)
        mm_steps.append(p.cstep("tensor", mm, deps=mdeps, label=f"pgm{tag}"))
    banks = [(oc, th, PS[:, (j * 2 + th) * 512:(j * 2 + th) * 512 + 512])
             for j, oc in enumerate(ocs) for th in range(2)]
    return banks, mm_steps[-1]


def _groups(nch):
    ocs, out = list(range(nch)), []
    while ocs:
        out.append(ocs[:4])
        ocs = ocs[4:]
    return out


def emit_moe_experts(p, TB, wg_ap, wu_ap, Ich, CT, OHALL, CB, GSIL, UCB,
                     WB, PS, acts, rowbase_of, tag="m"):
    prev = None
    for e in range(8):
        for th in range(2):
            p.cstep("tensor", lambda t, e=e, th=th: t.matmul(
                PS[:, 3584:4096], OHALL[0:8, e * 128:(e + 1) * 128],
                CT[0:8, th * 512:(th + 1) * 512], start=True, stop=True))
            p.cstep("vector", lambda v, th=th: v.tensor_copy(
                CB[:, th * 512:(th + 1) * 512], PS[:, 3584:4096]))
        cb_anchor = p.steps[-1]
        for gi, ocs in enumerate(_groups(Ich)):
            banks, lg = emit_proj_group(p, wg_ap[e], TB, ocs, D, WB, PS,
                                        tag=f"{tag}g{e}_{gi}",
                                        prev_evac=prev or cb_anchor)

            def evg(s, banks=banks):
                last = None
                for bi, (oc, th, pv) in enumerate(banks):
                    last = s.activation(GSIL[:, bi * 512:(bi + 1) * 512], pv,
                                        AF.Silu)
                return last
            dg = [lg] + ([prev] if prev else [])
            sg_step = p.cstep("scalar", evg, deps=dg)
            banks, lu = emit_proj_group(p, wu_ap[e], TB, ocs, D, WB, PS,
                                        tag=f"{tag}u{e}_{gi}", prev_evac=sg_step)

            def evu(v, banks=banks):
                last = None
                for bi, (oc, th, pv) in enumerate(banks):
                    last = v.tensor_tensor(UCB[:, bi * 512:(bi + 1) * 512], pv,
                                           CB[:, th * 512:(th + 1) * 512],
                                           ALU.mult)
                return last
            du = [lu] + ([prev] if prev else [])
            evu_step = p.cstep("vector", evu, deps=du)

            def eva(v, banks=banks):
                last = None
                for bi in range(len(banks)):
                    last = v.tensor_tensor(UCB[:, bi * 512:(bi + 1) * 512],
                                           UCB[:, bi * 512:(bi + 1) * 512],
                                           GSIL[:, bi * 512:(bi + 1) * 512],
                                           ALU.mult)
                return last
            eva_step = p.cstep("vector", eva, deps=[evu_step, sg_step])

            def st(ee, ocs=ocs, e=e):
                rows0 = (rowbase_of(e) + ocs[0]) * 128
                return [ee.dma_start(
                    out=acts.ap()[rows0:rows0 + len(ocs) * 128, :]
                    .rearrange("(c p) t -> p c t", p=128),
                    in_=UCB[:, 0:len(ocs) * TC]
                    .rearrange("p (c t) -> p c t", t=TC))]
            prev = p.dstep("sync", st, deps=[eva_step], label=f"{tag}act{e}")


def emit_shared_expert(p, TB, sg_ap, su_ap, SGB, GSIL, UCB, WB, PS,
                       acts, rowbase, tag="s"):
    prev = None
    for gi, ocs in enumerate(_groups(44)):
        banks, lg = emit_proj_group(p, sg_ap, TB, ocs, D, WB, PS,
                                    tag=f"{tag}g{gi}",
                                    prev_evac=prev or p.steps[-1])

        def evg(s, banks=banks):
            last = None
            for bi, (oc, th, pv) in enumerate(banks):
                last = s.activation(GSIL[:, bi * 512:(bi + 1) * 512], pv, AF.Silu)
            return last
        sg_step = p.cstep("scalar", evg, deps=[lg] + ([prev] if prev else []))
        banks, lu = emit_proj_group(p, su_ap, TB, ocs, D, WB, PS,
                                    tag=f"{tag}u{gi}", prev_evac=sg_step)

        def evu(v, banks=banks):
            last = None
            for bi, (oc, th, pv) in enumerate(banks):
                last = v.tensor_tensor(UCB[:, bi * 512:(bi + 1) * 512], pv,
                                       SGB[:, th * 512:(th + 1) * 512], ALU.mult)
            return last
        evu_step = p.cstep("vector", evu, deps=[lu] + ([prev] if prev else []))

        def eva(v, banks=banks):
            last = None
            for bi in range(len(banks)):
                last = v.tensor_tensor(UCB[:, bi * 512:(bi + 1) * 512],
                                       UCB[:, bi * 512:(bi + 1) * 512],
                                       GSIL[:, bi * 512:(bi + 1) * 512], ALU.mult)
            return last
        eva_step = p.cstep("vector", eva, deps=[evu_step, sg_step])

        def st(ee, ocs=ocs):
            rows0 = (rowbase + ocs[0]) * 128
            return [ee.dma_start(
                out=acts.ap()[rows0:rows0 + len(ocs) * 128, :]
                .rearrange("(c p) t -> p c t", p=128),
                in_=UCB[:, 0:len(ocs) * TC].rearrange("p (c t) -> p c t", t=TC))]
        prev = p.dstep("sync", st, deps=[eva_step], label=f"{tag}act{gi}")


def emit_moe_down(p, sources, acts, ACT2A, ACT2B, WB, PS, down_evac, tag="d"):
    prev_evac = None
    total = sum(nkt for (_, nkt, _) in sources)
    for og2 in range(4):
        gk = 0
        bi = 0
        mm_by_gk = {}
        batch_last_mm = []
        anchor = prev_evac or p.steps[-1]
        for (wd_ap, nkt, rowbase) in sources:
            for b0 in range(0, nkt, 4):
                kts = list(range(b0, min(b0 + 4, nkt)))
                buf = WB[bi % 2]
                wdeps = [batch_last_mm[bi - 2]] if bi >= 2 else [anchor]
                wst = p.dstep("gpsimd", lambda g, wd_ap=wd_ap, kts=kts, og2=og2,
                              buf=buf: [g.dma_start(
                                  out=buf[:, 0:len(kts) * 512]
                                  .rearrange("p (c n) -> p c n", n=512),
                                  in_=wd_ap[kts[0] * 128:(kts[-1] + 1) * 128,
                                            og2 * 512:(og2 + 1) * 512]
                                  .rearrange("(c p) n -> p c n", p=128))],
                              deps=wdeps, label=f"{tag}w")
                for i, kt in enumerate(kts):
                    abuf = ACT2A if gk % 2 == 0 else ACT2B
                    adeps = [mm_by_gk[gk - 2]] if gk - 2 in mm_by_gk else [anchor]
                    ast_ = p.dstep("sync", lambda e, rowbase=rowbase, kt=kt,
                                   abuf=abuf: [e.dma_start(
                                       out=abuf[:, :],
                                       in_=acts.ap()[(rowbase + kt) * 128:
                                                     (rowbase + kt + 1) * 128, :])],
                                   deps=adeps, label=f"{tag}a")
                    first = gk == 0
                    last_ = gk == total - 1

                    def mm(t, first=first, last_=last_, abuf=abuf, buf=buf, i=i):
                        r = None
                        for j in range(4):
                            for th in range(2):
                                r = t.matmul(
                                    PS[:, (j * 2 + th) * 512:(j * 2 + th) * 512 + 512],
                                    buf[:, i * 512 + j * 128:i * 512 + (j + 1) * 128],
                                    abuf[:, th * 512:th * 512 + 512],
                                    start=first, stop=last_)
                        return r
                    mdeps = [wst, ast_] + ([prev_evac] if (gk == 0 and prev_evac)
                                           else [])
                    mm_by_gk[gk] = p.cstep("tensor", mm, deps=mdeps,
                                           label=f"{tag}m")
                    gk += 1
                batch_last_mm.append(mm_by_gk[gk - 1])
                bi += 1
        banks = [(og2 * 4 + j, th,
                  PS[:, (j * 2 + th) * 512:(j * 2 + th) * 512 + 512])
                 for j in range(4) for th in range(2)]
        prev_evac = down_evac(og2, banks, [mm_by_gk[total - 1]])


def emit_idf_ohall(p, IDF, OHALL, IDI):
    p.cstep("gpsimd", lambda g: g.iota(
        IDI[:, :], pattern=[[1, 128]], base=0, channel_multiplier=-1,
        allow_small_or_imprecise_dtypes=True))
    p.cstep("vector", lambda v: v.tensor_scalar(
        IDF[:, :], IDI[:, :], 0.0, None, ALU.is_equal))
    p.cstep("gpsimd", lambda g: g.iota(
        OHALL[0:8, :].rearrange("p (e m) -> p e m", m=128),
        pattern=[[-1, 8], [0, 128]], base=0, channel_multiplier=1,
        allow_small_or_imprecise_dtypes=True))
    p.cstep("vector", lambda v: v.tensor_scalar(
        OHALL[0:8, :], OHALL[0:8, :], 0.0, None, ALU.is_equal))


def _run(nc, in_maps):
    res = run_bass_kernel_spmd(nc, in_maps, core_ids=list(range(NCORES)))
    return res.results


def build_fused():
    p = P()
    x_in = p.din("x", [D, TC])
    out_out = p.dout("out", [D, TC])
    igw_ap = p.din("igw", [16, 128, 128])
    rgw_ap = p.din("rgw", [16, 128, 128])
    igb_ap = p.din("igb", [128, 16])
    rgb_ap = p.din("rgb", [128, 16])
    rp_ap = p.din("rp", [128, 16])
    sel_ap = p.din("sel", [128, 8])
    selp_ap = p.din("selp", [128, 8])
    seln_ap = p.din("seln", [128, 8])
    fm_ap = p.din("first_mask", [128, 1])
    wq_ap = p.din("wq", [D, D])
    wk_ap = p.din("wk", [D, 512])
    wv_ap = p.din("wv", [D, 512])
    wo_ap = p.din("wo", [D, D])
    ln1_ap = p.din("ln1", [128, 16])
    ln2_ap = p.din("ln2", [128, 16])
    qnw_ap = p.din("qnw", [128, 16])
    knw_ap = p.din("knw", [128, 4])
    cos_ap = p.din("cosr", [64, TC])
    sin_ap = p.din("sinsgn", [64, TC])
    swap_ap = p.din("swapm", [64, 64], BF16)
    qpos_ap = p.din("qpos", [1, TC])
    rw0_ap = p.din("rw0", [D, 8])
    rbt_ap = p.din("rb0t", [128, 64])
    wg0_ap = p.din("wg0", [8, D, 1024])
    wu0_ap = p.din("wu0", [8, D, 1024])
    wd0_ap = p.din("wd0", [8, 1024, D])
    cw_ap = p.din("cw", [3, 128, 16])
    cb_ap = p.din("cb", [128, 16])
    WuT_ap = p.din("WuT", [D, D])
    WvT_ap = p.din("WvT", [D, D])
    Wg1T_ap = p.din("Wg1T", [D, 512])
    Wg2T_ap = p.din("Wg2T", [512, D])
    bu_ap = p.din("bu", [128, 16])
    bv_ap = p.din("bv", [128, 16])
    bg1_ap = p.din("bg1", [128, 4])
    bg2_ap = p.din("bg2", [128, 16])
    wr1_ap = p.din("Wr1", [D, 8])
    wg1_ap = p.din("wg1", [8, D, 1408])
    wu1_ap = p.din("wu1", [8, D, 1408])
    wd1_ap = p.din("wd1", [8, 1408, D])
    sg_ap = p.din("Sg", [D, 5632])
    su_ap = p.din("Su", [D, 5632])
    sd_ap = p.din("Sd", [5632, D])
    wsg_ap = p.din("Wsg", [128, 16])

    agg_in = p.dint("agg_in", [128, 32])
    agg_out = p.dint("agg_out", [128 * NCORES, 32])
    kag_in = p.dint("kag_in", [512, TC], BF16)
    kag_out = p.dint("kag_out", [512 * NCORES, TC], BF16)
    vag_in = p.dint("vag_in", [TC, 512], BF16)
    vag_out = p.dint("vag_out", [TC * NCORES, 512], BF16)
    s0_d = p.dint("s0_d", [D, TC], BF16)
    o_d = p.dint("o_d", [D, TC], BF16)
    g0_d = p.dint("g0_d", [D, TC], BF16)
    xl1_d = p.dint("xl1_d", [D, TC])
    tout_d = p.dint("tout_d", [D, TC], BF16)
    s1_d = p.dint("s1_d", [D, TC], BF16)
    hal_in = p.dint("hal_in", [128, 32])
    hal_out = p.dint("hal_out", [128 * NCORES, 32])
    acts0 = p.dint("acts0", [64 * 128, TC], BF16)
    acts1 = p.dint("acts1", [132 * 128, TC], BF16)

    F32A = p.sb("F32A", [128, NCH * TC])
    BF1 = p.sb("BF1", [128, NCH * TC], BF16)
    BF2 = p.sb("BF2", [128, NCH * TC], BF16)
    BF3 = p.sb("BF3", [128, NCH * TC], BF16)
    SMF = p.sb("SMF", [128, 4608])
    SMB = p.sb("SMB", [128, 8448], BF16)
    WBA = p.sb("WBA", [128, 2048], BF16)
    WBB = p.sb("WBB", [128, 2048], BF16)
    WB = (WBA, WBB)
    IDI = p.sb("IDI", [128, 128], mybir.dt.int32)
    TAB = p.sb("TAB", [128, 64])
    ONE1 = p.sb("ONE1", [1, 128])
    ONEBF = p.sb("ONEBF", [128, 1], BF16)
    NEG30 = p.sb("NEG30", [128, 1])
    SEL = p.sb("SEL", [128, 8])
    SELP = p.sb("SELP", [128, 8])
    SELN = p.sb("SELN", [128, 8])
    FMASK = p.sb("FMASK", [128, 1])
    AGG = p.sb("AGG", [128, 32])
    CAR = p.sb("CAR", [128, 16])
    SST = p.sb("SST", [128, 16])
    QNWS = p.sb("QNWS", [128, 16])
    PS = p.ps("PS", [128, 4096])
    MB = SMF[:, 3584:4096]
    RB = SMF[:, 4096:4608]

    X, XNB, ALOC, YLOC = F32A, BF1, BF2, BF3
    SCR = SMF[:, 0:4096]
    ZEROSA = SMB[:, 0:1024]
    AGALL = SMF[:, 0:256]
    p.dstep("sync", lambda e: [
        feat_load(p, e, X, x_in),
        e.dma_start(out=TAB[:, 0:16], in_=igb_ap),
        e.dma_start(out=TAB[:, 16:32], in_=rgb_ap),
        e.dma_start(out=TAB[:, 32:48], in_=rp_ap),
        e.dma_start(out=SEL[:, :], in_=sel_ap),
        e.dma_start(out=SELP[:, :], in_=selp_ap),
        e.dma_start(out=SELN[:, :], in_=seln_ap),
        e.dma_start(out=FMASK[:, :], in_=fm_ap),
        e.dma_start(out=QNWS[:, 0:16], in_=qnw_ap),
    ], label="loadA", n=9)

    def init(v):
        v.memset(ZEROSA[:, :], 0.0)
        v.memset(ONEBF[:, :], 1.0)
        v.memset(NEG30[:, :], -30.0)
        return v.memset(ONE1[:, :], 1.0)
    p.cstep("vector", init)
    emit_softplus8(p, TAB, SCR)
    emit_ln(p, X, XNB, ALOC[:, 0:NCH * 512], YLOC[:, 0:NCH * 512],
            MB, RB, PS[0:1, 3584:4096], PS[0:1, 3072:3584], PS[:, 2048:2560],
            ONEBF, ONE1)
    emit_scan_branch(p, X, XNB, ALOC, YLOC, SCR, ZEROSA, TAB, SEL, FMASK,
                     WBA[:, 0:128], AGG, AGALL, CAR, SST,
                     PS[:, 0:512], PS[:, 512:1024],
                     igw_ap, rgw_ap, agg_in, agg_out)
    p.dstep("sync", lambda e: [feat_store(p, e, s0_d.ap(), YLOC)], label="s0st")

    XN, HB, QB = F32A, BF1, BF2
    SQB = BF3[:, 0:NCH * 512]
    KL = SMB[:, 0:4096]
    VT = SMB[:, 4096:8192]
    SWAPM = SMB[0:64, 8192:8256]
    COSR = SMF[0:64, 0:1024]
    SINSGN = SMF[0:64, 1024:2048]
    RROW = SMF[0:1, 3584:4096]
    p.dstep("sync", lambda e: [
        e.dma_start(out=TAB[:, 0:16], in_=ln1_ap),
        e.dma_start(out=TAB[:, 32:36], in_=knw_ap),
        e.dma_start(out=COSR, in_=cos_ap),
        e.dma_start(out=SINSGN, in_=sin_ap),
        e.dma_start(out=SWAPM, in_=swap_ap),
    ], label="loadB1", n=5)
    p.cstep("vector", lambda v: v.tensor_scalar(
        QNWS[:, :], QNWS[:, :], float(1.0 / np.sqrt(128.0)), None, ALU.mult))
    emit_ln(p, XN, HB, SQB, None, None, RB,
            PS[0:1, 3584:4096], None, PS[:, 3584:4096], ONEBF, ONE1,
            eps=1e-6, gain=TAB[:, 0:16])

    def ev_copy(dst, col_of):
        def f(og, banks, deps):
            def e(s, banks=banks):
                last = None
                for (oc, th, pv) in banks:
                    last = s.copy(dst[:, col_of(oc, th):col_of(oc, th) + 512], pv)
                return last
            return p.cstep("scalar", e, deps=deps, label="pev")
        return f
    emit_proj(p, wq_ap, HB, D, D, WB, PS,
              ev_copy(QB, lambda oc, th: oc * TC + th * 512))
    emit_proj(p, wk_ap, HB, D, 512, WB, PS,
              ev_copy(KL, lambda oc, th: oc * TC + th * 512))
    emit_rownorm_scale(p, QB, QNWS[:, 0:16], 16, PS, SQB, RROW, RB,
                       ONEBF, ONE1, 1e-6, "qn")
    emit_rownorm_scale(p, KL, TAB[:, 32:36], 4, PS, SQB[:, 0:4 * 512], RROW,
                       RB, ONEBF, ONE1, 1e-6, "kn")
    emit_rope(p, QB, 16, COSR, SINSGN, SWAPM, PS, SMF[:, 2048:4096])
    emit_rope(p, KL, 4, COSR, SINSGN, SWAPM, PS, SMF[:, 2048:4096])
    for kt in range(NCH):
        p.dstep("gpsimd", lambda gg, kt=kt: [gg.dma_start(
            out=WBA[:, 0:512], in_=wv_ap[kt * 128:(kt + 1) * 128, :])],
            label=f"vw{kt}")

        def mmv(t, kt=kt):
            last = None
            for tt in range(8):
                last = t.matmul(PS[:, tt * 512:(tt + 1) * 512],
                                HB[:, kt * TC + tt * 128:kt * TC + tt * 128 + 128],
                                WBA[:, 0:512], start=(kt == 0), stop=(kt == NCH - 1))
            return last
        p.cstep("tensor", mmv, label="vmm")

    def vev(s):
        last = None
        for tt in range(8):
            last = s.copy(VT[:, tt * 512:(tt + 1) * 512],
                          PS[:, tt * 512:(tt + 1) * 512])
        return last
    p.cstep("scalar", vev)
    p.dstep("sync", lambda e: [
        e.dma_start(out=kag_in.ap().rearrange("(c p) t -> p c t", p=128),
                    in_=KL[:, :].rearrange("p (c t) -> p c t", t=TC)),
        e.dma_start(out=vag_in.ap().rearrange("(tt p) o -> p tt o", p=128),
                    in_=VT[:, :].rearrange("p (tt o) -> p tt o", o=512)),
    ], label="kv_out", n=2)
    p.xstep("gpsimd", lambda gg: [gg.collective_compute(
        "AllGather", ALU.bypass, replica_groups=[list(range(NCORES))],
        ins=[kag_in.ap().opt()], outs=[kag_out.ap().opt()])], label="ag_k")
    p.xstep("gpsimd", lambda gg: [gg.collective_compute(
        "AllGather", ALU.bypass, replica_groups=[list(range(NCORES))],
        ins=[vag_in.ap().opt()], outs=[vag_out.ap().opt()])], label="ag_v")

    KF2, VF2 = BF1, BF3
    PTT = (SMB[:, 0:1024], SMB[:, 1024:2048])
    MTT = (SMB[:, 2048:3072], SMB[:, 3072:4096])
    QPOSB = SMF[:, 0:1024]
    ZB = SMF[:, 1024:1536]
    KIDXC = SMF[:, 1536:1600]
    QROW = SMF[0:1, 1600:2624]
    ZROW = SMF[0:1, 2624:3136]
    SLABS = (PS[:, 0:1024], PS[:, 1024:2048])
    OP = PS[:, 2048:2560]
    ZP = PS[0:1, 2560:3072]
    PBC = PS[:, 3072:3584]
    p.dstep("sync", lambda e: [e.dma_start(out=QROW, in_=qpos_ap)],
            label="loadqp")
    p.cstep("gpsimd", lambda gg: gg.iota(
        KIDXC, pattern=[[128, 64]], base=0, channel_multiplier=1,
        allow_small_or_imprecise_dtypes=True))
    for th in range(2):
        p.cstep("tensor", lambda t, th=th: t.matmul(
            PBC, ONE1[0:1, :], QROW[0:1, th * 512:(th + 1) * 512],
            start=True, stop=True))
        p.cstep("vector", lambda v, th=th: v.tensor_copy(
            QPOSB[:, th * 512:(th + 1) * 512], PBC))

    for grp in range(2):
        def ldk(e, grp=grp):
            ins = []
            for chl in range(2):
                ch = 2 * grp + chl
                for r in range(NCORES):
                    ins.append(e.dma_start(
                        out=KF2[:, chl * 8192 + r * TC: chl * 8192 + (r + 1) * TC],
                        in_=kag_out.ap()[r * 512 + ch * 128: r * 512 + (ch + 1) * 128, :]))
            return ins
        p.dstep("sync", ldk, label=f"ldk{grp}", n=16)
        p.dstep("sync", lambda e, grp=grp: [e.dma_start(
            out=VF2[:, 0:64 * 256].rearrange("p (kt o) -> p kt o", o=256),
            in_=vag_out.ap()[:, grp * 256:(grp + 1) * 256]
            .rearrange("(kt p) o -> p kt o", p=128))], label=f"ldv{grp}")
        prev_tail = p.steps[-1]
        prev_exp_par = {0: prev_tail, 1: prev_tail}
        prev_zav_par = {0: prev_tail, 1: prev_tail}
        prev_ma_par = {0: prev_tail, 1: prev_tail}
        prev_oe = prev_tail
        for h8 in range(8):
            h = grp * 8 + h8
            chl = h8 // 4
            for qtg in range(2):
                last_kt = 59 + 4 * qtg
                nkt = last_kt + 1
                qc0 = qtg * 512
                nktg = (nkt + 1) // 2
                exp_steps, zav_steps, map_steps = {}, {}, {}

                def emit_mm(ktg):
                    kts = [kt for kt in range(ktg * 2, min(ktg * 2 + 2, nkt))]
                    par = ktg % 2
                    slab = SLABS[par]

                    def mmsc(t, kts=kts, h=h, chl=chl, qtg=qtg, slab=slab):
                        last = None
                        for j, kt in enumerate(kts):
                            last = t.matmul(
                                slab[:, j * 512:(j + 1) * 512],
                                KF2[:, chl * 8192 + kt * 128: chl * 8192 + kt * 128 + 128],
                                QB[:, h * TC + qtg * 512: h * TC + qtg * 512 + 512],
                                start=True, stop=True)
                        return last
                    d = [exp_steps[ktg - 2]] if ktg >= 2 else [prev_exp_par[par]]
                    return p.cstep("tensor", mmsc, deps=d, label="sc")

                def emit_rest(ktg, sc_step):
                    kts = [kt for kt in range(ktg * 2, min(ktg * 2 + 2, nkt))]
                    par = ktg % 2
                    slab, PT, MT = SLABS[par], PTT[par], MTT[par]

                    def expv(s, n=len(kts), slab=slab, PT=PT):
                        return s.activation(PT[:, 0:n * 512], slab[:, 0:n * 512],
                                            AF.Exp, bias=NEG30[:, 0:1])
                    d = [sc_step] + ([zav_steps[ktg - 2]] if ktg >= 2
                                     else [prev_zav_par[par]])
                    exp_steps[ktg] = p.cstep("scalar", expv, deps=d, label="exp")

                    mkts = [kt for kt in kts if kt >= 4 * qtg]
                    if mkts:
                        def mbuild(v, kts=kts, mkts=mkts, qc0=qc0, MT=MT):
                            last = None
                            for j, kt in enumerate(kts):
                                if kt in mkts:
                                    last = v.tensor_scalar(
                                        MT[:, j * 512:(j + 1) * 512],
                                        QPOSB[:, qc0:qc0 + 512],
                                        KIDXC[:, kt:kt + 1], None, ALU.is_ge)
                            return last
                        d = ([map_steps[ktg - 2]] if ktg - 2 in map_steps
                             else [prev_ma_par[par]])
                        mb_step = p.cstep("vector", mbuild, deps=d, label="mb")

                        def mapply(v, kts=kts, mkts=mkts, PT=PT, MT=MT):
                            last = None
                            for j, kt in enumerate(kts):
                                if kt in mkts:
                                    last = v.tensor_tensor(
                                        PT[:, j * 512:(j + 1) * 512],
                                        PT[:, j * 512:(j + 1) * 512],
                                        MT[:, j * 512:(j + 1) * 512], ALU.mult)
                            return last
                        map_steps[ktg] = p.cstep("vector", mapply,
                                                 deps=[exp_steps[ktg], mb_step],
                                                 label="ma")

                    def zav(t, kts=kts, chl=chl, last_kt=last_kt, PT=PT):
                        last = None
                        for j, kt in enumerate(kts):
                            last = t.matmul(ZP, ONEBF[:, 0:1],
                                            PT[:, j * 512:(j + 1) * 512],
                                            start=(kt == 0), stop=(kt == last_kt))
                        for j, kt in enumerate(kts):
                            last = t.matmul(
                                OP, VF2[:, kt * 256 + chl * 128: kt * 256 + chl * 128 + 128],
                                PT[:, j * 512:(j + 1) * 512],
                                start=(kt == 0), stop=(kt == last_kt))
                        return last
                    d = ([map_steps[ktg]] if ktg in map_steps else [exp_steps[ktg]])
                    if ktg == 0:
                        d = d + [prev_oe]
                    zav_steps[ktg] = p.cstep("tensor", zav, deps=d, label="zav")

                pend_sc = {}
                for k in range(nktg + 1):
                    if k < nktg:
                        pend_sc[k] = emit_mm(k)
                    if k >= 1:
                        emit_rest(k - 1, pend_sc.pop(k - 1))

                rc = p.cstep("vector", lambda v: v.reciprocal(ZROW, ZP),
                             deps=[zav_steps[nktg - 1]])
                bc = p.cstep("tensor", lambda t: t.matmul(
                    PBC, ONE1[0:1, :], ZROW, start=True, stop=True), deps=[rc])
                cz = p.cstep("vector", lambda v: v.tensor_copy(ZB, PBC), deps=[bc])
                oe = p.cstep("vector", lambda v, par=(nktg - 1) % 2: v.tensor_tensor(
                    PTT[par][:, 0:512], OP, ZB, ALU.mult), deps=[cz])
                ost = p.dstep("sync", lambda e, h=h, qc0=qc0,
                              par=(nktg - 1) % 2: [e.dma_start(
                                  out=o_d.ap()[h * 128:(h + 1) * 128, qc0:qc0 + 512],
                                  in_=PTT[par][:, 0:512])],
                              deps=[oe], label="ost")
                lp = (nktg - 1) % 2
                prev_exp_par = {lp: ost,
                                1 - lp: exp_steps[nktg - 2]}
                prev_zav_par = {lp: ost,
                                1 - lp: zav_steps[nktg - 2]}
                for mk in sorted(map_steps)[-2:]:
                    prev_ma_par[mk % 2] = map_steps[mk]
                prev_oe = ost
                prev_tail = ost

    OB = BF1
    p.dstep("sync", lambda e: [feat_load(p, e, OB, o_d.ap())], label="ldo")

    def ev_add_xn(og, banks, deps):
        def f(v, banks=banks):
            last = None
            for (oc, th, pv) in banks:
                seg = XN[:, oc * TC + th * 512: oc * TC + th * 512 + 512]
                last = v.tensor_tensor(seg, seg, pv, ALU.add)
            return last
        return p.cstep("vector", f, deps=deps, label="ev_add")
    emit_proj(p, wo_ap, OB, D, D, WB, PS, ev_add_xn)

    X1, TB = F32A, BF1
    GSIL = BF2
    UCB = BF3
    RWB = SMB[:, 0:128]
    A2A = SMB[:, 128:1152]
    A2B = SMB[:, 1152:2176]
    COMB = SMF[:, 0:64]
    RLOG = SMF[:, 64:128]
    RSIG = SMF[:, 128:192]
    RB0B = SMF[:, 192:256]
    M1 = SMF[:, 256:280]
    IDF = SMF[:, 280:408]
    OHALL = SMF[0:8, 408:1432]
    CT = SMF[0:8, 1432:2456]
    CB = SMF[:, 2456:3480]
    p.dstep("sync", lambda e: [e.dma_start(out=TAB[:, 0:16], in_=ln2_ap)],
            label="loadC")
    emit_idf_ohall(p, IDF, OHALL, IDI)
    emit_ln(p, X1, TB, UCB[:, 0:NCH * 512], None, None, RB,
            PS[0:1, 3584:4096], None, PS[:, 3584:4096], ONEBF, ONE1,
            eps=1e-6, gain=TAB[:, 0:16])
    emit_router(p, TB, rw0_ap, RWB, COMB, CT, RLOG, RSIG, RB0B, IDF, OHALL,
                PS, M1, "l0", bias_tiled_ap=rbt_ap)
    emit_moe_experts(p, TB, wg0_ap, wu0_ap, 8, CT, OHALL, CB,
                     GSIL[:, 0:4096], UCB[:, 0:4096], WB, PS, acts0,
                     lambda e: e * 8, tag="c")

    def devacC(og2, banks, deps):
        def f1(v, banks=banks):
            last = None
            for (oc, th, pv) in banks:
                seg = X1[:, oc * TC + th * 512: oc * TC + th * 512 + 512]
                last = v.tensor_tensor(seg, seg, pv, ALU.add)
            return last
        s1_ = p.cstep("vector", f1, deps=deps)

        def f2(s, banks=banks):
            last = None
            for (oc, th, pv) in banks:
                seg = X1[:, oc * TC + th * 512: oc * TC + th * 512 + 512]
                last = s.activation(TB[:, oc * TC + th * 512:
                                       oc * TC + th * 512 + 512], seg, AF.Silu)
            return last
        return p.cstep("scalar", f2, deps=[s1_])
    emit_moe_down(p, [(wd0_ap[e], 8, e * 8) for e in range(8)], acts0,
                  A2A, A2B, WB, PS, devacC, tag="cd")
    p.dstep("sync", lambda e: [feat_store(p, e, g0_d.ap(), TB)], label="g0st")

    X = F32A
    S0, G0, XN1B = BF1, BF2, BF3
    SCRD = SMF[:, 0:1024]
    HAL = SMF[:, 1024:1056]
    HALL = SMF[:, 1056:1312]
    LCOL = SMF[:, 1312:1328]
    RCOL = SMF[:, 1328:1344]
    TCV = SMB[:, 0:1024]
    p.dstep("sync", lambda e: [
        feat_load(p, e, X, x_in),
        feat_load(p, e, S0, s0_d.ap()),
        feat_load(p, e, G0, g0_d.ap()),
        e.dma_start(out=TAB[:, 0:48].rearrange("p (k c) -> p k c", c=16),
                    in_=cw_ap[:, :, :].rearrange("k p c -> p k c")),
        e.dma_start(out=TAB[:, 48:64], in_=cb_ap),
    ], label="loadD", n=5)
    for c in range(NCH):
        p.cstep("vector", lambda v, c=c: v.tensor_tensor(
            SCRD[:, :], S0[:, c * TC:(c + 1) * TC], G0[:, c * TC:(c + 1) * TC],
            ALU.mult))
        p.cstep("vector", lambda v, c=c: v.tensor_tensor(
            X[:, c * TC:(c + 1) * TC], X[:, c * TC:(c + 1) * TC], SCRD[:, :],
            ALU.add))
    p.dstep("sync", lambda e: [feat_store(p, e, xl1_d.ap(), X)], label="xl1st")
    emit_ln(p, X, XN1B, S0[:, 0:NCH * 512], S0[:, NCH * 512:NCH * 1024],
            MB, RB, PS[0:1, 3584:4096], PS[0:1, 3072:3584], PS[:, 2048:2560],
            ONEBF, ONE1, eps=1e-5)

    def pack(v):
        last = None
        for c in range(NCH):
            v.tensor_copy(HAL[:, c:c + 1], X[:, c * TC:c * TC + 1])
            last = v.tensor_copy(HAL[:, 16 + c:17 + c],
                                 X[:, c * TC + TC - 1:c * TC + TC])
        return last
    p.cstep("vector", pack)
    p.dstep("gpsimd", lambda g: [g.dma_start(out=hal_in[:, :], in_=HAL)])
    p.xstep("gpsimd", lambda g: [g.collective_compute(
        "AllGather", ALU.bypass, replica_groups=[list(range(NCORES))],
        ins=[hal_in.ap().opt()], outs=[hal_out.ap().opt()])], label="ag_hal")
    p.dstep("gpsimd", lambda g: [g.dma_start(
        out=HALL.rearrange("p (r c) -> p r c", c=32),
        in_=hal_out.ap().rearrange("(r p) c -> p r c", p=128))])

    def zc(v):
        v.memset(LCOL[:, :], 0.0)
        return v.memset(RCOL[:, :], 0.0)
    p.cstep("vector", zc)
    for j in range(NCORES):
        p.cstep("vector", lambda v, j=j: v.scalar_tensor_tensor(
            LCOL[:, :], HALL[:, j * 32 + 16:j * 32 + 32], SELP[:, j:j + 1],
            LCOL[:, :], ALU.mult, ALU.add))
        p.cstep("vector", lambda v, j=j: v.scalar_tensor_tensor(
            RCOL[:, :], HALL[:, j * 32:j * 32 + 16], SELN[:, j:j + 1],
            RCOL[:, :], ALU.mult, ALU.add))
    for c in range(NCH):
        xc = X[:, c * TC:(c + 1) * TC]
        p.cstep("vector", lambda v, c=c, xc=xc: v.tensor_scalar(
            SCRD[:, :], xc, TAB[:, 16 + c:17 + c], TAB[:, 48 + c:49 + c],
            ALU.mult, ALU.add))
        p.cstep("vector", lambda v, c=c, xc=xc: v.scalar_tensor_tensor(
            SCRD[:, 1:TC], xc[:, 0:TC - 1], TAB[:, c:c + 1], SCRD[:, 1:TC],
            ALU.mult, ALU.add))
        p.cstep("vector", lambda v, c=c: v.scalar_tensor_tensor(
            SCRD[:, 0:1], LCOL[:, c:c + 1], TAB[:, c:c + 1], SCRD[:, 0:1],
            ALU.mult, ALU.add))
        p.cstep("vector", lambda v, c=c, xc=xc: v.scalar_tensor_tensor(
            SCRD[:, 0:TC - 1], xc[:, 1:TC], TAB[:, 32 + c:33 + c],
            SCRD[:, 0:TC - 1], ALU.mult, ALU.add))
        p.cstep("vector", lambda v, c=c: v.scalar_tensor_tensor(
            SCRD[:, TC - 1:TC], RCOL[:, c:c + 1], TAB[:, 32 + c:33 + c],
            SCRD[:, TC - 1:TC], ALU.mult, ALU.add))
        p.cstep("vector", lambda v, c=c: v.tensor_copy(TCV, SCRD[:, :]))
        p.dstep("sync", lambda e, c=c: [e.dma_start(
            out=tout_d.ap()[c * 128:(c + 1) * 128, :], in_=TCV)],
            label=f"tst{c}")

    UB, GB = BF1, BF2
    H1B = SMB[:, 0:4096]
    TOUT2 = SMB[:, 4096:8192]
    ZEROSE = SMB[:, 4096:5120]
    AGALLE = SMF[:, 0:256]
    p.dstep("sync", lambda e: [
        e.dma_start(out=TAB[:, 0:16], in_=bu_ap),
        e.dma_start(out=TAB[:, 16:32], in_=bv_ap),
        e.dma_start(out=TAB[:, 32:36], in_=bg1_ap),
        e.dma_start(out=TAB[:, 36:52], in_=bg2_ap),
    ], label="loadE", n=4)
    p.cstep("vector", lambda v: v.memset(ZEROSE[:, :], 0.0))

    def ev_biasE(dst, bias_col_of, act=None):
        def f(og, banks, deps):
            def e(eng, banks=banks):
                last = None
                for (oc, th, pv) in banks:
                    seg = dst[:, oc * TC + th * 512: oc * TC + th * 512 + 512]
                    if act is None:
                        last = eng.tensor_scalar(seg, pv, bias_col_of(oc), None,
                                                 ALU.add)
                    else:
                        last = eng.activation(seg, pv, act, bias=bias_col_of(oc))
                return last
            return p.cstep("vector" if act is None else "scalar", e, deps=deps,
                           label="evb")
        return f
    emit_proj(p, WuT_ap, XN1B, D, D, WB, PS,
              ev_biasE(UB, lambda oc: TAB[:, oc:oc + 1]))
    emit_proj(p, Wg1T_ap, XN1B, D, 512, WB, PS,
              ev_biasE(H1B, lambda oc: TAB[:, 32 + oc:33 + oc], act=AF.Relu))
    emit_proj(p, Wg2T_ap, H1B, 512, D, WB, PS,
              ev_biasE(GB, lambda oc: TAB[:, 36 + oc:37 + oc], act=AF.Sigmoid))

    def ug(v):
        last = None
        for c in range(NCH):
            last = v.tensor_tensor(UB[:, c * TC:(c + 1) * TC],
                                   UB[:, c * TC:(c + 1) * TC],
                                   GB[:, c * TC:(c + 1) * TC], ALU.mult)
        return last
    p.cstep("vector", ug)

    def cum(v):
        last = None
        for c in range(NCH):
            last = v.tensor_tensor_scan(
                UB[:, c * TC:(c + 1) * TC], ZEROSE[:, :],
                UB[:, c * TC:(c + 1) * TC], 0.0, ALU.add, ALU.add)
        return last
    p.cstep("vector", cum)

    def agcp(v):
        last = None
        for c in range(NCH):
            last = v.tensor_copy(AGG[:, c:c + 1],
                                 UB[:, c * TC + TC - 1:c * TC + TC])
        return last
    p.cstep("vector", agcp)
    p.dstep("gpsimd", lambda g: [g.dma_start(out=agg_in.ap()[:, 0:16],
                                             in_=AGG[:, 0:16])])
    p.xstep("gpsimd", lambda g: [g.collective_compute(
        "AllGather", ALU.bypass, replica_groups=[list(range(NCORES))],
        ins=[agg_in.ap().opt()], outs=[agg_out.ap().opt()])], label="ag_cum")
    p.dstep("gpsimd", lambda g: [g.dma_start(
        out=AGALLE.rearrange("p (r c) -> p r c", c=32),
        in_=agg_out.ap().rearrange("(r p) c -> p r c", p=128))])

    def cm(v):
        v.memset(CAR[:, :], 0.0)
        return v.memset(SST[:, :], 0.0)
    p.cstep("vector", cm)
    for j in range(NCORES):
        p.cstep("vector", lambda v, j=j: v.scalar_tensor_tensor(
            CAR[:, :], SST[:, :], SEL[:, j:j + 1], CAR[:, :], ALU.mult, ALU.add))
        p.cstep("vector", lambda v, j=j: v.tensor_tensor(
            SST[:, :], SST[:, :], AGALLE[:, j * 32:j * 32 + 16], ALU.add))

    def ev_vg(og, banks, deps):
        def f(v, banks=banks):
            last = None
            for (oc, th, pv) in banks:
                seg = GB[:, oc * TC + th * 512: oc * TC + th * 512 + 512]
                last = v.scalar_tensor_tensor(seg, pv, TAB[:, 16 + oc:17 + oc],
                                              seg, ALU.add, ALU.mult)
            return last
        return p.cstep("vector", f, deps=deps, label="ev_vg")
    emit_proj(p, WvT_ap, XN1B, D, D, WB, PS, ev_vg)

    def go(v):
        last = None
        for c in range(NCH):
            last = v.scalar_tensor_tensor(
                GB[:, c * TC:(c + 1) * TC], UB[:, c * TC:(c + 1) * TC],
                CAR[:, c:c + 1], GB[:, c * TC:(c + 1) * TC], ALU.add, ALU.mult)
        return last
    p.cstep("vector", go)

    def ev_uv(og, banks, deps):
        ocs = sorted({oc for (oc, th, pv) in banks})
        ld = p.dstep("sync", lambda e, ocs=ocs: [e.dma_start(
            out=TOUT2[:, 0:len(ocs) * TC].rearrange("p (c t) -> p c t", t=TC),
            in_=tout_d.ap()[ocs[0] * 128:(ocs[0] + len(ocs)) * 128, :]
            .rearrange("(c p) t -> p c t", p=128))], deps=deps, label="t2ld")

        def f(v, banks=banks, ocs=ocs):
            last = None
            for (oc, th, pv) in banks:
                lc = oc - ocs[0]
                seg = TOUT2[:, lc * TC + th * 512: lc * TC + th * 512 + 512]
                last = v.tensor_tensor(seg, seg, pv, ALU.add)
            return last
        fs = p.cstep("vector", f, deps=[ld] + deps, label="uv_add")
        return p.dstep("sync", lambda e, ocs=ocs: [e.dma_start(
            out=s1_d.ap()[ocs[0] * 128:(ocs[0] + len(ocs)) * 128, :]
            .rearrange("(c p) t -> p c t", p=128),
            in_=TOUT2[:, 0:len(ocs) * TC].rearrange("p (c t) -> p c t", t=TC))],
            deps=[fs], label="s1st")
    emit_proj(p, WuT_ap, GB, D, D, WB, PS, ev_uv)

    XL1 = F32A
    S1 = BF1
    GSILF = BF2[:, 0:4096]
    UCBF = BF2[:, 4096:8192]
    SGB = SMF[:, 3480:4504]
    SROW = SMF[0:1, 2456:3480]
    WSGB = SMB[:, 2176:2192]
    p.dstep("sync", lambda e: [
        feat_load(p, e, XL1, xl1_d.ap()),
        feat_load(p, e, S1, s1_d.ap()),
    ], label="loadF", n=2)
    p.dstep("gpsimd", lambda g: [g.dma_start(out=WSGB, in_=wsg_ap)],
            label="wsg_dma")
    emit_idf_ohall(p, IDF, OHALL, IDI)
    emit_router(p, XN1B, wr1_ap, RWB, COMB, CT, RLOG, RSIG, RB0B, IDF, OHALL,
                PS, M1, "l1")

    def mmsg(t):
        last = None
        for th in range(2):
            for kt in range(NCH):
                last = t.matmul(PS[0:1, (7 - th) * 512:(7 - th) * 512 + 512],
                                WSGB[:, kt:kt + 1],
                                XN1B[:, kt * TC + th * 512:kt * TC + th * 512 + 512],
                                start=(kt == 0), stop=(kt == NCH - 1))
        return last
    p.cstep("tensor", mmsg)
    p.cstep("scalar", lambda s: s.activation(SROW[0:1, 0:512],
                                             PS[0:1, 3584:4096], AF.Sigmoid))
    p.cstep("scalar", lambda s: s.activation(SROW[0:1, 512:1024],
                                             PS[0:1, 3072:3584], AF.Sigmoid))
    for th in range(2):
        p.cstep("tensor", lambda t, th=th: t.matmul(
            PS[:, 2048:2560], ONE1[0:1, :], SROW[0:1, th * 512:(th + 1) * 512],
            start=True, stop=True))
        p.cstep("vector", lambda v, th=th: v.tensor_copy(
            SGB[:, th * 512:(th + 1) * 512], PS[:, 2048:2560]))
    emit_moe_experts(p, XN1B, wg1_ap, wu1_ap, 11, CT, OHALL, CB,
                     GSILF, UCBF, WB, PS, acts1, lambda e: e * 11, tag="f")
    emit_shared_expert(p, XN1B, sg_ap, su_ap, SGB, GSILF, UCBF, WB, PS,
                       acts1, 88, tag="fs")

    def devacF(og2, banks, deps):
        def f1(s, banks=banks):
            last = None
            for bi, (oc, th, pv) in enumerate(banks):
                last = s.activation(GSILF[:, bi * 512:(bi + 1) * 512], pv, AF.Silu)
            return last
        s1_ = p.cstep("scalar", f1, deps=deps)

        def f2(v, banks=banks):
            last = None
            for bi, (oc, th, pv) in enumerate(banks):
                last = v.tensor_tensor(
                    UCBF[:, bi * 512:(bi + 1) * 512],
                    GSILF[:, bi * 512:(bi + 1) * 512],
                    S1[:, oc * TC + th * 512: oc * TC + th * 512 + 512], ALU.mult)
            return last
        s2_ = p.cstep("vector", f2, deps=[s1_])

        def f3(v, banks=banks):
            last = None
            for bi, (oc, th, pv) in enumerate(banks):
                seg = XL1[:, oc * TC + th * 512: oc * TC + th * 512 + 512]
                last = v.tensor_tensor(seg, seg, UCBF[:, bi * 512:(bi + 1) * 512],
                                       ALU.add)
            return last
        return p.cstep("vector", f3, deps=[s2_])
    emit_moe_down(p, [(wd1_ap[e], 11, e * 11) for e in range(8)]
                  + [(sd_ap, 44, 88)], acts1, A2A, A2B, WB, PS, devacF,
                  tag="fd")

    p.dstep("sync", lambda e: [feat_store(p, e, out_out, XL1)], label="outst")
    nc = p.build()
    return p, nc


def kernel(**inputs):
    g, pc = _prep_host(inputs)
    p, nc = build_fused()
    in_maps = []
    for i in range(NCORES):
        m = dict(x=pc[i]["x"], igw=g["igw"], rgw=g["rgw"], igb=g["igb"],
                 rgb=g["rgb"], rp=g["rp"], sel=pc[i]["sel"], selp=pc[i]["selp"],
                 seln=pc[i]["seln"], first_mask=pc[i]["first_mask"],
                 wq=g["wq"], wk=g["wk"], wv=g["wv"], wo=g["wo"],
                 ln1=g["ln1"], ln2=g["ln2"], qnw=g["qnw"], knw=g["knw"],
                 cosr=pc[i]["cosr"], sinsgn=pc[i]["sinsgn"], swapm=g["swapm"],
                 qpos=pc[i]["qpos"], rw0=g["rw0"], rb0t=g["rb0t"],
                 wg0=g["wg0"], wu0=g["wu0"], wd0=g["wd0"],
                 cw=g["cw"], cb=g["cb"], WuT=g["WuT"], WvT=g["WvT"],
                 Wg1T=g["Wg1T"], Wg2T=g["Wg2T"], bu=g["bu"], bv=g["bv"],
                 bg1=g["bg1"], bg2=g["bg2"], Wr1=g["Wr1"],
                 wg1=g["wg1"], wu1=g["wu1"], wd1=g["wd1"],
                 Sg=g["Sg"], Su=g["Su"], Sd=g["Sd"], Wsg=g["Wsg"])
        in_maps.append(m)
    res = run_bass_kernel_spmd(nc, in_maps, core_ids=list(range(NCORES)))
    p.close()
    out = np.concatenate([np.asarray(res.results[i]["out"], np.float32)
                          for i in range(NCORES)], axis=1)
    return np.ascontiguousarray(out.T)[None]


# version 06
# speedup vs baseline: 2.0358x; 1.0019x over previous
import numpy as np
import ml_dtypes
from contextlib import ExitStack

import concourse.bass as bass
import concourse.mybir as mybir
from concourse.bass_utils import run_bass_kernel_spmd

F32 = mybir.dt.float32
BF16 = mybir.dt.bfloat16
AF = mybir.ActivationFunctionType
ALU = mybir.AluOpType
AX = mybir.AxisListType

NCORES = 8
TC = 1024
D = 2048
NCH = 16
BFNP = ml_dtypes.bfloat16
ENGINES = ("sync", "gpsimd", "vector", "scalar", "tensor")


class Step:
    __slots__ = ("idx", "engine", "fn", "deps", "kind", "label", "_ndma")

    def __init__(self, idx, engine, fn, deps, kind, label):
        self.idx, self.engine, self.fn = idx, engine, fn
        self.deps, self.kind, self.label = deps, kind, label
        self._ndma = 0


class P:

    def __init__(self):
        self.nc = bass.Bass("TRN2", target_bir_lowering=False, debug=True)
        self.es = ExitStack()
        self.steps = []
        self._names = set()

    def din(self, name, shape, dtype=F32):
        return self.nc.dram_tensor(name, list(shape), dtype, kind="ExternalInput").ap()

    def dout(self, name, shape, dtype=F32):
        return self.nc.dram_tensor(name, list(shape), dtype, kind="ExternalOutput").ap()

    def dint(self, name, shape, dtype=F32):
        return self.nc.dram_tensor(name, list(shape), dtype)

    def sb(self, name, shape, dtype=F32):
        assert name not in self._names, name
        self._names.add(name)
        return self.es.enter_context(self.nc.sbuf_tensor(name, list(shape), dtype))

    def ps(self, name, shape, dtype=F32):
        assert name not in self._names, name
        self._names.add(name)
        return self.es.enter_context(self.nc.psum_tensor(name, list(shape), dtype))

    def cstep(self, engine, fn, deps="all", label=""):
        s = Step(len(self.steps), engine, fn, deps, "c", label)
        self.steps.append(s)
        return s

    def dstep(self, engine, fn, deps="all", label="", n=None):
        s = Step(len(self.steps), engine, fn, deps, "d", label)
        s._ndma = 1 if n is None else n
        self.steps.append(s)
        return s

    def xstep(self, engine, fn, deps="all", label="", n=1):
        s = Step(len(self.steps), engine, fn, deps, "x", label)
        s._ndma = n
        self.steps.append(s)
        return s

    def build(self):
        used = {(st.engine, st.kind) for st in self.steps if st.kind != "x"}
        chans = [(e, k) for e in ENGINES for k in ("c", "d")
                 if (e, k) in used]
        for st in self.steps:
            if st.kind == "x":
                chans.append((st.engine, f"x{st.idx}"))
        sems = {ch: self.es.enter_context(
            self.nc.semaphore(f"sem_{ch[0]}_{ch[1]}")) for ch in chans}
        blk = self.es.enter_context(self.nc.Block())

        counts = {ch: 0 for ch in chans}
        after = []
        waits = []
        for st in self.steps:
            mych = (st.engine, st.kind if st.kind != "x" else f"x{st.idx}")
            if st.deps == "all":
                tg = {ch: counts[ch] for ch in chans if counts[ch] > 0}
            else:
                tg = {}
                for d in st.deps:
                    for ch in chans:
                        v = after[d.idx][ch]
                        if v > tg.get(ch, 0):
                            tg[ch] = v
                for ch in chans:
                    if ch[0] == st.engine and counts.get(ch, 0) > tg.get(ch, 0):
                        tg[ch] = counts[ch]
            waits.append(tg)
            if st.kind == "c":
                counts[mych] += 1
            elif st.kind == "x":
                counts[mych] += 1
            else:
                assert st._ndma is not None, f"step {st.idx} {st.label}: dstep needs n="
                counts[mych] += 16 * st._ndma
            after.append(dict(counts))

        by_eng = {e: [st for st in self.steps if st.engine == e] for e in ENGINES}
        for e in ENGINES:
            if not by_eng[e] and e != "sync":
                continue

            def section(eng, e=e):
                for st in by_eng[e]:
                    for ch, v in waits[st.idx].items():
                        eng.wait_ge(sems[ch], v)
                    if st.kind == "c":
                        last = st.fn(eng)
                        assert last is not None, (
                            f"step {st.idx} {st.label}: fn must return last instr")
                        last.then_inc(sems[(e, "c")], 1)
                    elif st.kind == "x":
                        instrs = st.fn(eng)
                        assert len(instrs) == 1, "one collective per xstep"
                        instrs[0].then_inc(sems[(e, f"x{st.idx}")])
                    else:
                        instrs = st.fn(eng)
                        assert len(instrs) == st._ndma, (
                            f"step {st.idx} {st.label}: declared n={st._ndma} "
                            f"!= emitted {len(instrs)}")
                        for ins in instrs:
                            ins.then_inc(sems[(e, "d")], 16)
                if e == "sync":
                    for ch in chans:
                        if counts[ch] > 0:
                            eng.wait_ge(sems[ch], counts[ch])
            getattr(blk, e)(section)

        return self.nc

    def close(self):
        self.es.close()


def chunkc(ap, c, width=TC):
    return ap[:, c * width:(c + 1) * width]




def _prep_host(inputs):
    g = {}
    pc = [dict() for _ in range(NCORES)]

    x = np.ascontiguousarray(np.asarray(inputs["L0_attn_x"], np.float32)[0].T)
    for i in range(NCORES):
        pc[i]["x"] = np.ascontiguousarray(x[:, i * TC:(i + 1) * TC])

    A = lambda k: np.ascontiguousarray(np.asarray(inputs[k], np.float32))
    g["igw"] = A("L0_attn_input_gate_weight")
    g["rgw"] = A("L0_attn_recurrent_gate_weight")
    g["igb"] = np.ascontiguousarray(A("L0_attn_input_gate_bias").T)
    g["rgb"] = np.ascontiguousarray(A("L0_attn_recurrent_gate_bias").T)
    g["rp"] = np.ascontiguousarray(A("L0_attn_recurrent_param").reshape(16, 128).T)

    g["ln1"] = np.ascontiguousarray(A("L0_ffn_ln1_w").reshape(16, 128).T)
    g["ln2"] = np.ascontiguousarray(A("L0_ffn_ln2_w").reshape(16, 128).T)
    g["qnw"] = np.ascontiguousarray(A("L0_ffn_qnorm_w").reshape(16, 128).T)
    g["knw"] = np.ascontiguousarray(A("L0_ffn_knorm_w").reshape(4, 128).T)
    g["wq"] = A("L0_ffn_wq")
    g["wk"] = A("L0_ffn_wk")
    g["wv"] = A("L0_ffn_wv")
    g["wo"] = A("L0_ffn_wo")
    g["rw0"] = A("L0_ffn_router_w")
    g["rb0t"] = np.ascontiguousarray(
        np.tile(A("L0_ffn_router_bias").reshape(1, 8), (128, 8)))
    g["wg0"] = A("L0_ffn_wg")
    g["wu0"] = A("L0_ffn_wu")
    g["wd0"] = A("L0_ffn_wd")

    g["WuT"] = np.ascontiguousarray(A("L1_attn_Wu").T)
    g["WvT"] = np.ascontiguousarray(A("L1_attn_Wv").T)
    g["Wg1T"] = np.ascontiguousarray(A("L1_attn_Wg1").T)
    g["Wg2T"] = np.ascontiguousarray(A("L1_attn_Wg2").T)
    g["bu"] = np.ascontiguousarray(A("L1_attn_bu").reshape(16, 128).T)
    g["bv"] = np.ascontiguousarray(A("L1_attn_bv").reshape(16, 128).T)
    g["bg1"] = np.ascontiguousarray(A("L1_attn_bg1").reshape(4, 128).T)
    g["bg2"] = np.ascontiguousarray(A("L1_attn_bg2").reshape(16, 128).T)
    g["cw"] = np.ascontiguousarray(
        A("L1_attn_conv_w")[:, 0, :].reshape(16, 128, 3).transpose(2, 1, 0))
    g["cb"] = np.ascontiguousarray(A("L1_attn_conv_b").reshape(16, 128).T)

    g["Wr1"] = A("L1_ffn_Wr")
    g["wg1"] = A("L1_ffn_Wg")
    g["wu1"] = A("L1_ffn_Wu")
    g["wd1"] = A("L1_ffn_Wd")
    g["Sg"] = A("L1_ffn_Sg")
    g["Su"] = A("L1_ffn_Su")
    g["Sd"] = A("L1_ffn_Sd")
    g["Wsg"] = np.ascontiguousarray(
        A("L1_ffn_Wsg").reshape(16, 128).T)

    swapm = np.zeros((64, 64), np.float32)
    for m in range(64):
        swapm[(m + 32) % 64, m] = 1.0
    g["swapm"] = swapm.astype(BFNP)

    theta = 1e6
    inv = 1.0 / theta ** (np.arange(0, 64, 2, dtype=np.float32) / 64)
    for i in range(NCORES):
        pos = (i * TC + np.arange(TC)).astype(np.float32)
        ang = pos[:, None] * inv[None, :]
        emb = np.concatenate([ang, ang], 1)
        pc[i]["cosr"] = np.ascontiguousarray(np.cos(emb).T.astype(np.float32))
        sint = np.sin(emb).T.astype(np.float32)
        sinsgn = sint.copy()
        sinsgn[0:32] = -sint[0:32]
        pc[i]["sinsgn"] = np.ascontiguousarray(sinsgn)
        pc[i]["qpos"] = pos.reshape(1, TC).copy()
        sel = np.zeros((128, 8), np.float32)
        sel[:, i] = 1.0
        pc[i]["sel"] = sel
        selp = np.zeros((128, 8), np.float32)
        if i > 0:
            selp[:, i - 1] = 1.0
        pc[i]["selp"] = selp
        seln = np.zeros((128, 8), np.float32)
        if i < NCORES - 1:
            seln[:, i + 1] = 1.0
        pc[i]["seln"] = seln
        fm = np.zeros((128, 1), np.float32)
        if i == 0:
            fm[:, 0] = 1.0
        pc[i]["first_mask"] = fm
    return g, pc




def feat_store(p, eng, dram_ap, sbuf, dtype_elems=TC):
    return eng.dma_start(
        out=dram_ap.rearrange("(c p) t -> p c t", p=128),
        in_=sbuf[:, :].rearrange("p (c t) -> p c t", t=dtype_elems))


def feat_load(p, eng, sbuf, dram_ap, dtype_elems=TC):
    return eng.dma_start(
        out=sbuf[:, :].rearrange("p (c t) -> p c t", t=dtype_elems),
        in_=dram_ap.rearrange("(c p) t -> p c t", p=128))


def emit_ln(p, X, XNB, SQB, XB, MB, RB, PSQ, PSS, PSB, ONEBF, ONE1,
            eps=1e-5, gain=None, out_f32_inplace=True, nch=NCH, width=TC):
    dim = 128 * nch
    for th in range(2):
        def sq(s, th=th):
            last = None
            for c in range(nch):
                last = s.activation(SQB[:, c * 512:(c + 1) * 512],
                                    X[:, c * width + th * 512: c * width + th * 512 + 512],
                                    AF.Square)
            return last
        p.cstep("scalar", sq, label="ln_sq")
        if gain is None:
            def cp(v, th=th):
                last = None
                for c in range(nch):
                    last = v.tensor_copy(XB[:, c * 512:(c + 1) * 512],
                                         X[:, c * width + th * 512: c * width + th * 512 + 512])
                return last
            p.cstep("vector", cp, label="ln_cp")

        def mm(t, th=th):
            last = None
            for c in range(nch):
                last = t.matmul(PSQ[:], ONEBF[:, 0:1], SQB[:, c * 512:(c + 1) * 512],
                                start=(c == 0), stop=(c == nch - 1))
            if gain is None:
                for c in range(nch):
                    last = t.matmul(PSS[:], ONEBF[:, 0:1], XB[:, c * 512:(c + 1) * 512],
                                    start=(c == 0), stop=(c == nch - 1))
            return last
        p.cstep("tensor", mm, label="ln_mm")

        if gain is None:
            def st1(v, th=th):
                return v.tensor_scalar(MB[0:1, 0:512], PSS[:], 1.0 / dim, None, ALU.mult)
            p.cstep("vector", st1, label="ln_mean")

            def st2(s, th=th):
                return s.activation(RB[0:1, 0:512], MB[0:1, 0:512], AF.Square)
            p.cstep("scalar", st2, label="ln_m2")

            def st3(v, th=th):
                return v.scalar_tensor_tensor(RB[0:1, 0:512], PSQ[:], 1.0 / dim,
                                              RB[0:1, 0:512], ALU.mult, ALU.subtract)
            p.cstep("vector", st3, label="ln_var")
            p.cstep("vector", lambda v: v.tensor_scalar(
                RB[0:1, 0:512], RB[0:1, 0:512], eps, None, ALU.add), label="ln_eps")
        else:
            def st3(v, th=th):
                return v.tensor_scalar(RB[0:1, 0:512], PSQ[:], 1.0 / dim, eps,
                                       ALU.mult, ALU.add)
            p.cstep("vector", st3, label="ln_var")

        def st4(s, th=th):
            return s.activation(RB[0:1, 0:512], RB[0:1, 0:512], AF.Sqrt)
        p.cstep("scalar", st4, label="ln_sqrt")

        def st5(v, th=th):
            return v.reciprocal(RB[0:1, 0:512], RB[0:1, 0:512])
        p.cstep("vector", st5, label="ln_recip")

        def bc1(t, th=th):
            return t.matmul(PSB[:], ONE1[0:1, :], RB[0:1, 0:512], start=True, stop=True)
        p.cstep("tensor", bc1, label="ln_bc_r")

        def cpr(v, th=th):
            return v.tensor_copy(RB[:, 0:512], PSB[:])
        p.cstep("vector", cpr, label="ln_cp_r")

        if gain is None:
            def bc2(t, th=th):
                return t.matmul(PSB[:], ONE1[0:1, :], MB[0:1, 0:512], start=True, stop=True)
            p.cstep("tensor", bc2, label="ln_bc_m")

            def cpm(v, th=th):
                return v.tensor_copy(MB[:, 0:512], PSB[:])
            p.cstep("vector", cpm, label="ln_cp_m")

        def segf(c, th):
            return X[:, c * width + th * 512: c * width + th * 512 + 512]

        def segb(c, th):
            return XNB[:, c * width + th * 512: c * width + th * 512 + 512]
        if gain is None:
            def n1(v, th=th):
                last = None
                for c in range(nch):
                    last = v.scalar_tensor_tensor(segf(c, th), segf(c, th), 1.0,
                                                  MB[:, 0:512], ALU.mult, ALU.subtract)
                return last
            p.cstep("vector", n1, label="ln_n1")

            def n2(v, th=th):
                last = None
                for c in range(nch):
                    last = v.tensor_tensor(segf(c, th), segf(c, th), RB[:, 0:512], ALU.mult)
                return last
            p.cstep("vector", n2, label="ln_n2")

            def n3(v, th=th):
                last = None
                for c in range(nch):
                    last = v.tensor_copy(segb(c, th), segf(c, th))
                return last
            p.cstep("vector", n3, label="ln_n3")
        else:
            def n1(v, th=th):
                last = None
                for c in range(nch):
                    last = v.scalar_tensor_tensor(segb(c, th), segf(c, th),
                                                  gain[:, c:c + 1], RB[:, 0:512],
                                                  ALU.mult, ALU.mult)
                return last
            p.cstep("vector", n1, label="ln_rms")


def emit_softplus8(p, TAB, SCR):
    p.cstep("scalar", lambda s: (s.activation(TAB[:, 48:64], TAB[:, 32:48], AF.Abs),
                                 s.activation(SCR[:, 32:48], TAB[:, 32:48], AF.Relu))[-1])
    p.cstep("scalar", lambda s: s.activation(SCR[:, 0:16], TAB[:, 48:64], AF.Exp, scale=-1.0))
    p.cstep("scalar", lambda s: s.activation(SCR[:, 16:32], SCR[:, 0:16], AF.Ln, bias=1.0))
    p.cstep("vector", lambda v: v.scalar_tensor_tensor(
        TAB[:, 48:64], SCR[:, 16:32], 1.0, SCR[:, 32:48], ALU.mult, ALU.add))
    p.cstep("vector", lambda v: v.tensor_scalar(
        TAB[:, 48:64], TAB[:, 48:64], -8.0, None, ALU.mult))


def emit_scan_branch(p, X, XNB, ALOC, YLOC, SCR, ZEROS, TAB, SEL, FMASK,
                     WGT, AGG, AGALL, CAR, SST, PG0, PG1,
                     igw_ap, rgw_ap, agg_in, agg_out):
    s_gx, s_ga = 0, 1024
    s_a, s_m, s_b = 2048, 3072, 1024
    for h in range(16):
        p.dstep("gpsimd", lambda g, h=h: [
            g.dma_start(out=WGT[:, :], in_=igw_ap[h])], label=f"gw{h}")

        def mm1(t, h=h):
            t.matmul(PG0[:], WGT[:, :], XNB[:, h * TC:h * TC + 512],
                     start=True, stop=True)
            return t.matmul(PG1[:], WGT[:, :], XNB[:, h * TC + 512:h * TC + 1024],
                            start=True, stop=True)
        p.cstep("tensor", mm1)

        def ev1(s, h=h):
            s.activation(SCR[:, s_gx:s_gx + 512], PG0[:], AF.Sigmoid,
                         bias=TAB[:, h:h + 1])
            return s.activation(SCR[:, s_gx + 512:s_gx + 1024], PG1[:], AF.Sigmoid,
                                bias=TAB[:, h:h + 1])
        p.cstep("scalar", ev1)

        p.dstep("gpsimd", lambda g, h=h: [
            g.dma_start(out=WGT[:, :], in_=rgw_ap[h])], label=f"rw{h}")

        def mm2(t, h=h):
            t.matmul(PG0[:], WGT[:, :], XNB[:, h * TC:h * TC + 512],
                     start=True, stop=True)
            return t.matmul(PG1[:], WGT[:, :], XNB[:, h * TC + 512:h * TC + 1024],
                            start=True, stop=True)
        p.cstep("tensor", mm2)

        def ev2(s, h=h):
            s.activation(SCR[:, s_ga:s_ga + 512], PG0[:], AF.Sigmoid,
                         bias=TAB[:, 16 + h:17 + h])
            return s.activation(SCR[:, s_ga + 512:s_ga + 1024], PG1[:], AF.Sigmoid,
                                bias=TAB[:, 16 + h:17 + h])
        p.cstep("scalar", ev2)

        def e1(v, h=h):
            return v.tensor_scalar(SCR[:, s_a:s_a + 1024], SCR[:, s_ga:s_ga + 1024],
                                   TAB[:, 48 + h:49 + h], None, ALU.mult)
        p.cstep("vector", e1)

        def e2(s, h=h):
            return s.activation(SCR[:, s_a:s_a + 1024], SCR[:, s_a:s_a + 1024], AF.Exp)
        p.cstep("scalar", e2)

        def e3(s, h=h):
            return s.activation(SCR[:, s_m:s_m + 1024], SCR[:, s_a:s_a + 1024], AF.Square)
        p.cstep("scalar", e3)

        def e4(v, h=h):
            return v.tensor_scalar(SCR[:, s_m:s_m + 1024], SCR[:, s_m:s_m + 1024],
                                   -1.0, 1.0, ALU.mult, ALU.add)
        p.cstep("vector", e4)

        def e5(s, h=h):
            return s.activation(SCR[:, s_m:s_m + 1024], SCR[:, s_m:s_m + 1024], AF.Sqrt)
        p.cstep("scalar", e5)

        p.cstep("vector", lambda v, h=h: v.tensor_scalar(
            SCR[:, s_b:s_b + 1], FMASK[:, 0:1], -1.0, 1.0, ALU.mult, ALU.add))
        p.cstep("vector", lambda v, h=h: v.scalar_tensor_tensor(
            SCR[:, s_m:s_m + 1], SCR[:, s_m:s_m + 1], SCR[:, s_b:s_b + 1],
            FMASK[:, 0:1], ALU.mult, ALU.add))
        p.cstep("vector", lambda v, h=h: v.tensor_tensor(
            SCR[:, s_b:s_b + 1024], X[:, h * TC:(h + 1) * TC],
            SCR[:, s_gx:s_gx + 1024], ALU.mult))
        p.cstep("vector", lambda v, h=h: v.tensor_tensor(
            SCR[:, s_b:s_b + 1024], SCR[:, s_b:s_b + 1024],
            SCR[:, s_m:s_m + 1024], ALU.mult))

        def e7(v, h=h):
            v.tensor_tensor_scan(ALOC[:, h * TC:(h + 1) * TC],
                                 SCR[:, s_a:s_a + 1024], ZEROS[:, 0:1024],
                                 1.0, ALU.mult, ALU.add)
            return v.tensor_tensor_scan(
                YLOC[:, h * TC:(h + 1) * TC], SCR[:, s_a:s_a + 1024],
                SCR[:, s_b:s_b + 1024], 0.0, ALU.mult, ALU.add)
        p.cstep("vector", e7)

        def e8(v, h=h):
            v.tensor_copy(AGG[:, h:h + 1], ALOC[:, h * TC + 1023:h * TC + 1024])
            return v.tensor_copy(AGG[:, 16 + h:17 + h],
                                 YLOC[:, h * TC + 1023:h * TC + 1024])
        p.cstep("vector", e8)

    p.dstep("gpsimd", lambda g: [g.dma_start(out=agg_in[:, :], in_=AGG[:, :])],
            label="agg_out_dma")
    import os
    if os.environ.get("NO_CC"):
        p.dstep("gpsimd", lambda g: [g.dma_start(
            out=agg_out[0:128, :], in_=agg_in[:, :])], label="agg_fake")
    else:
        p.xstep("gpsimd", lambda g: [g.collective_compute(
            "AllGather", ALU.bypass, replica_groups=[list(range(NCORES))],
            ins=[agg_in[:, :]], outs=[agg_out[:, :]])], label="agg_ag")
    p.dstep("gpsimd", lambda g: [g.dma_start(
        out=AGALL[:, :].rearrange("p (r c) -> p r c", c=32),
        in_=agg_out.ap().rearrange("(r p) c -> p r c", p=128))],
        label="agg_in_dma")

    def cm(v):
        v.memset(CAR[:, :], 0.0)
        return v.memset(SST[:, :], 0.0)
    p.cstep("vector", cm)
    for j in range(NCORES):
        p.cstep("vector", lambda v, j=j: v.scalar_tensor_tensor(
            CAR[:, :], SST[:, :], SEL[:, j:j + 1], CAR[:, :], ALU.mult, ALU.add))
        p.cstep("vector", lambda v, j=j: v.tensor_tensor(
            SST[:, :], SST[:, :], AGALL[:, j * 32:j * 32 + 16], ALU.mult))
        p.cstep("vector", lambda v, j=j: v.tensor_tensor(
            SST[:, :], SST[:, :], AGALL[:, j * 32 + 16:j * 32 + 32], ALU.add))

    def corr(v):
        last = None
        for h in range(16):
            last = v.scalar_tensor_tensor(
                YLOC[:, h * TC:(h + 1) * TC], ALOC[:, h * TC:(h + 1) * TC],
                CAR[:, h:h + 1], YLOC[:, h * TC:(h + 1) * TC], ALU.mult, ALU.add)
        return last
    p.cstep("vector", corr, label="carry_corr")


def build_launch_A(dbg=False):
    p = P()
    x_in = p.din("x", [D, TC])
    igw_ap = p.din("igw", [16, 128, 128])
    rgw_ap = p.din("rgw", [16, 128, 128])
    igb_ap = p.din("igb", [128, 16])
    rgb_ap = p.din("rgb", [128, 16])
    rp_ap = p.din("rp", [128, 16])
    sel_ap = p.din("sel", [128, 8])
    fm_ap = p.din("first_mask", [128, 1])
    agg_in = p.dint("agg_in", [128, 32])
    agg_out = p.dint("agg_out", [128 * NCORES, 32])
    xn_out = p.dout("xn", [D, TC])
    s0_out = p.dout("s0", [D, TC], BF16)

    X = p.sb("X", [128, NCH * TC])
    XNB = p.sb("XNB", [128, NCH * TC], BF16)
    ALOC = p.sb("ALOC", [128, NCH * TC], BF16)
    YLOC = p.sb("YLOC", [128, NCH * TC], BF16)
    SCR = p.sb("SCR", [128, 5 * 1024])
    ZEROS = p.sb("ZEROS", [128, 1024])
    MB = p.sb("MB", [128, 512])
    RB = p.sb("RB", [128, 512])
    TAB = p.sb("TAB", [128, 64])
    SEL = p.sb("SEL", [128, 8])
    FMASK = p.sb("FMASK", [128, 1])
    ONEBF = p.sb("ONEBF", [128, 1], BF16)
    ONE1 = p.sb("ONE1", [1, 128])
    WGT = p.sb("WGT", [128, 128], BF16)
    AGG = p.sb("AGG", [128, 32])
    AGALL = p.sb("AGALL", [128, 256])
    CAR = p.sb("CAR", [128, 16])
    SST = p.sb("SST", [128, 16])
    PSQ = p.ps("PSQ", [1, 512])
    PSS = p.ps("PSS", [1, 512])
    PSB = p.ps("PSB", [128, 512])
    PG0 = p.ps("PG0", [128, 512])
    PG1 = p.ps("PG1", [128, 512])

    p.dstep("sync", lambda e: [
        feat_load(p, e, X, x_in),
        e.dma_start(out=TAB[:, 0:16], in_=igb_ap),
        e.dma_start(out=TAB[:, 16:32], in_=rgb_ap),
        e.dma_start(out=TAB[:, 32:48], in_=rp_ap),
        e.dma_start(out=SEL[:, :], in_=sel_ap),
        e.dma_start(out=FMASK[:, :], in_=fm_ap),
    ], label="load", n=6)

    def init(v):
        v.memset(ZEROS[:, :], 0.0)
        v.memset(ONEBF[:, :], 1.0)
        return v.memset(ONE1[:, :], 1.0)
    p.cstep("vector", init, label="init")

    import os as _os
    emit_softplus8(p, TAB, SCR)
    if not _os.environ.get("NO_LN"):
        emit_ln(p, X, XNB, ALOC[:, 0:NCH * 512], YLOC[:, 0:NCH * 512],
                MB, RB, PSQ, PSS, PSB, ONEBF, ONE1)
    if _os.environ.get("NO_GATES"):
        def fake(v):
            return v.tensor_copy(AGG[:, 0:32], AGALL[:, 0:32])
        p.cstep("vector", fake)
        p.dstep("gpsimd", lambda g: [g.dma_start(out=agg_in[:, :], in_=AGG[:, :])])
        p.xstep("gpsimd", lambda g: [g.collective_compute(
            "AllGather", ALU.bypass, replica_groups=[list(range(NCORES))],
            ins=[agg_in.ap().opt()], outs=[agg_out.ap().opt()])], label="agg_ag")
        p.dstep("gpsimd", lambda g: [g.dma_start(
            out=AGALL[:, :].rearrange("p (r c) -> p r c", c=32),
            in_=agg_out.ap().rearrange("(r p) c -> p r c", p=128))])
    else:
        emit_scan_branch(p, X, XNB, ALOC, YLOC, SCR, ZEROS, TAB, SEL, FMASK,
                         WGT, AGG, AGALL, CAR, SST, PG0, PG1,
                         igw_ap, rgw_ap, agg_in, agg_out)

    import os as _os2
    if _os2.environ.get("NO_BF16STORE"):
        p.dstep("sync", lambda e: [feat_store(p, e, xn_out, X)], label="store")
        p.dstep("sync", lambda e: [e.dma_start(out=s0_out.rearrange(
            "(c p) t -> p c t", p=128)[:, 0:1, :],
            in_=YLOC[:, 0:TC].rearrange("p (c t) -> p c t", t=TC))], label="store2")
    else:
        p.dstep("sync", lambda e: [
            feat_store(p, e, xn_out, X),
            feat_store(p, e, s0_out, YLOC),
        ], label="store", n=2)

    nc = p.build()
    return p, nc




def emit_proj(p, w_ap, rhs_bf, Din, Dout, WB, PS, evac_fn):
    evacs = {}
    guard = None
    for og in range(Dout // 256):
        ocs = list(range(og * 2, og * 2 + 2))
        prev = evacs.get(og - 2)
        banks, last_mm = emit_proj_group(p, w_ap, rhs_bf, ocs, Din, WB, PS,
                                         tag=f"p{og}", prev_evac=prev,
                                         bank0=(og % 2) * 4, wb_guard=guard)
        guard = last_mm
        evacs[og] = evac_fn(og, banks, [last_mm])


def emit_rownorm_scale(p, QB, gaintab, nch_q, PS, SQB, RROW, RBC, ONEBF, ONE1,
                       eps, tag, extra_scale=1.0):
    dim = nch_q * 128
    for th in range(2):
        def sq(s, th=th):
            last = None
            for c in range(nch_q):
                last = s.activation(SQB[:, c * 512:(c + 1) * 512],
                                    QB[:, c * TC + th * 512:c * TC + th * 512 + 512],
                                    AF.Square)
            return last
        p.cstep("scalar", sq, label=f"{tag}sq")

        def mmq(t, th=th):
            last = None
            for c in range(nch_q):
                last = t.matmul(PS[0:1, 3584:4096], ONEBF[:, 0:1],
                                SQB[:, c * 512:(c + 1) * 512],
                                start=(c == 0), stop=(c == nch_q - 1))
            return last
        p.cstep("tensor", mmq, label=f"{tag}mm")
        p.cstep("vector", lambda v: v.tensor_scalar(
            RROW[0:1, 0:512], PS[0:1, 3584:4096], 1.0 / dim, eps, ALU.mult, ALU.add))
        p.cstep("scalar", lambda s: s.activation(
            RROW[0:1, 0:512], RROW[0:1, 0:512], AF.Sqrt))
        p.cstep("vector", lambda v: v.reciprocal(RROW[0:1, 0:512], RROW[0:1, 0:512]))
        if extra_scale != 1.0:
            p.cstep("vector", lambda v: v.tensor_scalar(
                RROW[0:1, 0:512], RROW[0:1, 0:512], extra_scale, None, ALU.mult))
        p.cstep("tensor", lambda t: t.matmul(
            PS[:, 3584:4096], ONE1[0:1, :], RROW[0:1, 0:512], start=True, stop=True))
        p.cstep("vector", lambda v: v.tensor_copy(RBC[:, 0:512], PS[:, 3584:4096]))

        def scl(v, th=th):
            last = None
            for c in range(nch_q):
                seg = QB[:, c * TC + th * 512:c * TC + th * 512 + 512]
                last = v.scalar_tensor_tensor(seg, seg, gaintab[:, c:c + 1],
                                              RBC[:, 0:512], ALU.mult, ALU.mult)
            return last
        p.cstep("vector", scl, label=f"{tag}scl")


def emit_rope(p, QB, nch_q, COSR, SINSGN, SWAPM, PS, SCR):
    for h in range(nch_q):
        def mmswap(t, h=h):
            t.matmul(PS[0:64, 0:512], SWAPM[:, :],
                     QB[0:64, h * TC:h * TC + 512], start=True, stop=True)
            return t.matmul(PS[0:64, 512:1024], SWAPM[:, :],
                            QB[0:64, h * TC + 512:h * TC + 1024],
                            start=True, stop=True)
        p.cstep("tensor", mmswap)
        p.cstep("vector", lambda v, h=h: v.tensor_copy(
            SCR[0:64, 0:1024], PS[0:64, 0:1024]))
        p.cstep("vector", lambda v, h=h: v.tensor_tensor(
            SCR[0:64, 1024:2048], QB[0:64, h * TC:(h) * TC + 1024],
            COSR[0:64, :], ALU.mult))
        p.cstep("vector", lambda v, h=h: v.tensor_tensor(
            SCR[0:64, 0:1024], SCR[0:64, 0:1024], SINSGN[0:64, :], ALU.mult))
        p.cstep("vector", lambda v, h=h: v.tensor_tensor(
            QB[0:64, h * TC:h * TC + 1024], SCR[0:64, 1024:2048],
            SCR[0:64, 0:1024], ALU.add))


def emit_router(p, TB, rw_slab_ap, RWB, COMB, CT, RLOG, RSIG, RB0B, IDF, OHALL,
                PS, M1, kind, bias_tiled_ap=None):
    p.dstep("gpsimd", lambda g: [g.dma_start(
        out=RWB[:, :].rearrange("p (k e) -> p k e", e=8),
        in_=rw_slab_ap.rearrange("(k p) e -> p k e", p=128))], label="rw_dma")
    if bias_tiled_ap is not None:
        p.dstep("sync", lambda e: [e.dma_start(out=RB0B[:, :], in_=bias_tiled_ap)],
                label="rb_dma")

    def mmr(t):
        last = None
        for tt in range(8):
            for kt in range(NCH):
                last = t.matmul(PS[:, tt * 8: tt * 8 + 8],
                                TB[:, kt * TC + tt * 128: kt * TC + tt * 128 + 128],
                                RWB[:, kt * 8:(kt + 1) * 8],
                                start=(kt == 0), stop=(kt == NCH - 1))
        return last
    p.cstep("tensor", mmr, label="r_mm")
    p.cstep("vector", lambda v: v.tensor_copy(RLOG[:, :], PS[:, 0:64]))

    if kind == "l0":
        p.cstep("scalar", lambda s: s.activation(RSIG[:, :], RLOG[:, :], AF.Sigmoid))
        p.cstep("vector", lambda v: v.tensor_tensor(RLOG[:, :], RSIG[:, :],
                                                    RB0B[:, :], ALU.add))
        SELV = RLOG
        W = RSIG
    else:
        p.cstep("vector", lambda v: v.tensor_reduce(
            M1[:, 0:8], RLOG[:, :].rearrange("p (t e) -> p t e", e=8),
            AX.X, ALU.max))
        def sub(v):
            last = None
            for tt in range(8):
                last = v.tensor_scalar(RLOG[:, tt * 8:(tt + 1) * 8],
                                       RLOG[:, tt * 8:(tt + 1) * 8],
                                       M1[:, tt:tt + 1], None, ALU.subtract)
            return last
        p.cstep("vector", sub)
        p.cstep("scalar", lambda s: s.activation(RSIG[:, :], RLOG[:, :], AF.Exp))
        p.cstep("vector", lambda v: v.tensor_reduce(
            M1[:, 8:16], RSIG[:, :].rearrange("p (t e) -> p t e", e=8),
            AX.X, ALU.add))
        p.cstep("vector", lambda v: v.reciprocal(M1[:, 8:16], M1[:, 8:16]))
        def nrm(v):
            last = None
            for tt in range(8):
                last = v.tensor_scalar(RSIG[:, tt * 8:(tt + 1) * 8],
                                       RSIG[:, tt * 8:(tt + 1) * 8],
                                       M1[:, 8 + tt:9 + tt], None, ALU.mult)
            return last
        p.cstep("vector", nrm)
        SELV = RSIG
        W = RSIG

    p.cstep("vector", lambda v: v.tensor_reduce(
        M1[:, 0:8], SELV[:, :].rearrange("p (t e) -> p t e", e=8), AX.X, ALU.max))
    for tt in range(8):
        sl = slice(tt * 8, (tt + 1) * 8)
        p.cstep("vector", lambda v, tt=tt, sl=sl: v.tensor_scalar(
            COMB[:, sl], SELV[:, sl], M1[:, tt:tt + 1], None, ALU.is_ge))
        p.cstep("vector", lambda v, tt=tt, sl=sl: v.scalar_tensor_tensor(
            M1[:, 16 + tt * 8:16 + tt * 8 + 8]
            if False else RLOG[:, sl], COMB[:, sl], -40000.0, SELV[:, sl],
            ALU.mult, ALU.add))
    p.cstep("vector", lambda v: v.tensor_reduce(
        M1[:, 8:16], RLOG[:, :].rearrange("p (t e) -> p t e", e=8), AX.X, ALU.max))
    for tt in range(8):
        sl = slice(tt * 8, (tt + 1) * 8)
        p.cstep("vector", lambda v, tt=tt, sl=sl: v.tensor_scalar(
            RLOG[:, sl], RLOG[:, sl], M1[:, 8 + tt:9 + tt], None, ALU.is_ge))
        p.cstep("vector", lambda v, tt=tt, sl=sl: v.tensor_tensor(
            COMB[:, sl], COMB[:, sl], RLOG[:, sl], ALU.add))
        p.cstep("vector", lambda v, tt=tt, sl=sl: v.tensor_tensor(
            COMB[:, sl], COMB[:, sl], W[:, sl], ALU.mult))
    if kind == "l0":
        p.cstep("vector", lambda v: v.tensor_reduce(
            M1[:, 0:8], COMB[:, :].rearrange("p (t e) -> p t e", e=8),
            AX.X, ALU.add))
        p.cstep("vector", lambda v: v.reciprocal(M1[:, 0:8], M1[:, 0:8]))
        def nm(v):
            last = None
            for tt in range(8):
                last = v.tensor_scalar(COMB[:, tt * 8:(tt + 1) * 8],
                                       COMB[:, tt * 8:(tt + 1) * 8],
                                       M1[:, tt:tt + 1], None, ALU.mult)
            return last
        p.cstep("vector", nm)

    for tt in range(8):
        p.cstep("tensor", lambda t, tt=tt: t.transpose(
            PS[0:8, tt * 128:(tt + 1) * 128], COMB[:, tt * 8:(tt + 1) * 8],
            IDF[:, :]))
    p.cstep("vector", lambda v: v.tensor_copy(CT[0:8, :], PS[0:8, 0:1024]))


def emit_proj_group(p, w_ap, rhs_bf, ocs, Din, WB, PS, tag="",
                    prev_evac=None, bank0=0, wb_guard=None):
    nkt = Din // 128
    now = len(ocs) * 128
    o0 = ocs[0] * 128
    per = max(1, 2048 // now)
    nb = (nkt + per - 1) // per
    mm_steps = []
    for b in range(nb):
        kts = list(range(b * per, min(b * per + per, nkt)))
        deps = []
        if b >= 2:
            deps.append(mm_steps[b - 2])
        else:
            if prev_evac is not None:
                deps.extend(prev_evac if isinstance(prev_evac, list)
                            else [prev_evac])
            deps.append(wb_guard if wb_guard is not None else p.steps[-1])
        buf = WB[b % 2]
        p.dstep("gpsimd", lambda g, kts=kts, buf=buf: [g.dma_start(
            out=buf[:, 0:len(kts) * now].rearrange("p (c n) -> p c n", n=now),
            in_=w_ap[kts[0] * 128:(kts[-1] + 1) * 128, o0:o0 + now]
            .rearrange("(c p) n -> p c n", p=128))],
            deps=deps, label=f"pg{tag}")

        def mm(t, kts=kts, buf=buf, b=b):
            last = None
            for i, kt in enumerate(kts):
                for j in range(len(ocs)):
                    for th in range(2):
                        bk = bank0 + j * 2 + th
                        last = t.matmul(
                            PS[:, bk * 512:bk * 512 + 512],
                            buf[:, i * now + j * 128:i * now + (j + 1) * 128],
                            rhs_bf[:, kt * TC + th * 512:kt * TC + th * 512 + 512],
                            start=(kt == 0), stop=(kt == nkt - 1))
            return last
        mdeps = [p.steps[-1]]
        if b == 0 and prev_evac is not None:
            mdeps.extend(prev_evac if isinstance(prev_evac, list) else [prev_evac])
        mm_steps.append(p.cstep("tensor", mm, deps=mdeps, label=f"pgm{tag}"))
    banks = [(oc, th, PS[:, (bank0 + j * 2 + th) * 512:
                         (bank0 + j * 2 + th) * 512 + 512])
             for j, oc in enumerate(ocs) for th in range(2)]
    return banks, mm_steps[-1]


def _groups(nch):
    ocs, out = list(range(nch)), []
    while ocs:
        out.append(ocs[:2])
        ocs = ocs[2:]
    return out


def emit_moe_experts(p, TB, wg_ap, wu_ap, Ich, CT, OHALL, CB, GSIL, UCB,
                     WB, PS, acts, rowbase_of, tag="m"):
    evg_prev = evu_prev = eva_prev = st_prev = None
    lu_prev = None
    for e in range(8):
        cbc = []
        for th in range(2):
            cmdeps = ([cbc[-1]] if cbc else
                      ([evu_prev] if evu_prev else "all"))
            cm = p.cstep("tensor", lambda t, e=e, th=th: t.matmul(
                PS[:, 3584:4096], OHALL[0:8, e * 128:(e + 1) * 128],
                CT[0:8, th * 512:(th + 1) * 512], start=True, stop=True),
                deps=cmdeps)
            cbc.append(p.cstep("vector", lambda v, th=th: v.tensor_copy(
                CB[:, th * 512:(th + 1) * 512], PS[:, 3584:4096]), deps=[cm]))
        for gi, ocs in enumerate(_groups(Ich)):
            banks, lg = emit_proj_group(
                p, wg_ap[e], TB, ocs, D, WB, PS, tag=f"{tag}g{e}_{gi}",
                prev_evac=[evg_prev] if evg_prev else [p.steps[-1]], bank0=0,
                wb_guard=lu_prev)

            def evg(s, banks=banks):
                last = None
                for bi, (oc, th, pv) in enumerate(banks):
                    last = s.activation(GSIL[:, bi * 512:(bi + 1) * 512], pv,
                                        AF.Silu)
                return last
            dg = [lg] + ([eva_prev] if eva_prev else [])
            evg_step = p.cstep("scalar", evg, deps=dg)
            banks, lu = emit_proj_group(
                p, wu_ap[e], TB, ocs, D, WB, PS, tag=f"{tag}u{e}_{gi}",
                prev_evac=[x for x in ([evu_prev] + cbc) if x is not None],
                bank0=4, wb_guard=lg)
            cbc = []

            def evu(v, banks=banks):
                last = None
                for bi, (oc, th, pv) in enumerate(banks):
                    last = v.tensor_tensor(UCB[:, bi * 512:(bi + 1) * 512], pv,
                                           CB[:, th * 512:(th + 1) * 512],
                                           ALU.mult)
                return last
            du = [lu] + ([st_prev] if st_prev else [])
            evu_step = p.cstep("vector", evu, deps=du)

            def eva(v, banks=banks):
                last = None
                for bi in range(len(banks)):
                    last = v.tensor_tensor(UCB[:, bi * 512:(bi + 1) * 512],
                                           UCB[:, bi * 512:(bi + 1) * 512],
                                           GSIL[:, bi * 512:(bi + 1) * 512],
                                           ALU.mult)
                return last
            eva_step = p.cstep("vector", eva, deps=[evu_step, evg_step])

            def st(ee, ocs=ocs, e=e):
                rows0 = (rowbase_of(e) + ocs[0]) * 128
                return [ee.dma_start(
                    out=acts.ap()[rows0:rows0 + len(ocs) * 128, :]
                    .rearrange("(c p) t -> p c t", p=128),
                    in_=UCB[:, 0:len(ocs) * TC]
                    .rearrange("p (c t) -> p c t", t=TC))]
            st_prev = p.dstep("sync", st, deps=[eva_step], label=f"{tag}act{e}")
            evg_prev, evu_prev, eva_prev = evg_step, evu_step, eva_step
            lu_prev = lu


def emit_shared_expert(p, TB, sg_ap, su_ap, SGB, GSIL, UCB, WB, PS,
                       acts, rowbase, tag="s"):
    evg_prev = evu_prev = eva_prev = st_prev = None
    lu_prev = None
    for gi, ocs in enumerate(_groups(44)):
        banks, lg = emit_proj_group(
            p, sg_ap, TB, ocs, D, WB, PS, tag=f"{tag}g{gi}",
            prev_evac=[evg_prev] if evg_prev else [p.steps[-1]], bank0=0,
            wb_guard=lu_prev)

        def evg(s, banks=banks):
            last = None
            for bi, (oc, th, pv) in enumerate(banks):
                last = s.activation(GSIL[:, bi * 512:(bi + 1) * 512], pv, AF.Silu)
            return last
        evg_step = p.cstep("scalar", evg,
                           deps=[lg] + ([eva_prev] if eva_prev else []))
        banks, lu = emit_proj_group(
            p, su_ap, TB, ocs, D, WB, PS, tag=f"{tag}u{gi}",
            prev_evac=[evu_prev] if evu_prev else [p.steps[-1]], bank0=4,
            wb_guard=lg)

        def evu(v, banks=banks):
            last = None
            for bi, (oc, th, pv) in enumerate(banks):
                last = v.tensor_tensor(UCB[:, bi * 512:(bi + 1) * 512], pv,
                                       SGB[:, th * 512:(th + 1) * 512], ALU.mult)
            return last
        evu_step = p.cstep("vector", evu,
                           deps=[lu] + ([st_prev] if st_prev else []))

        def eva(v, banks=banks):
            last = None
            for bi in range(len(banks)):
                last = v.tensor_tensor(UCB[:, bi * 512:(bi + 1) * 512],
                                       UCB[:, bi * 512:(bi + 1) * 512],
                                       GSIL[:, bi * 512:(bi + 1) * 512], ALU.mult)
            return last
        eva_step = p.cstep("vector", eva, deps=[evu_step, evg_step])

        def st(ee, ocs=ocs):
            rows0 = (rowbase + ocs[0]) * 128
            return [ee.dma_start(
                out=acts.ap()[rows0:rows0 + len(ocs) * 128, :]
                .rearrange("(c p) t -> p c t", p=128),
                in_=UCB[:, 0:len(ocs) * TC].rearrange("p (c t) -> p c t", t=TC))]
        st_prev = p.dstep("sync", st, deps=[eva_step], label=f"{tag}act{gi}")
        evg_prev, evu_prev, eva_prev = evg_step, evu_step, eva_step
        lu_prev = lu


def emit_moe_down(p, sources, acts, ABUFS, WB, PS, down_evac, tag="d"):
    prev_evac = None
    total = sum(nkt for (_, nkt, _) in sources)
    for og2 in range(4):
        gk = 0
        bi = 0
        mm_by_gk = {}
        batch_last_mm = []
        anchor = prev_evac or p.steps[-1]
        for (wd_ap, nkt, rowbase) in sources:
            for b0 in range(0, nkt, 4):
                kts = list(range(b0, min(b0 + 4, nkt)))
                buf = WB[bi % 2]
                wdeps = [batch_last_mm[bi - 2]] if bi >= 2 else [anchor]
                wst = p.dstep("gpsimd", lambda g, wd_ap=wd_ap, kts=kts, og2=og2,
                              buf=buf: [g.dma_start(
                                  out=buf[:, 0:len(kts) * 512]
                                  .rearrange("p (c n) -> p c n", n=512),
                                  in_=wd_ap[kts[0] * 128:(kts[-1] + 1) * 128,
                                            og2 * 512:(og2 + 1) * 512]
                                  .rearrange("(c p) n -> p c n", p=128))],
                              deps=wdeps, label=f"{tag}w")
                for i, kt in enumerate(kts):
                    abuf = ABUFS[gk % len(ABUFS)]
                    adeps = ([mm_by_gk[gk - len(ABUFS)]]
                             if gk - len(ABUFS) in mm_by_gk else [anchor])
                    ast_ = p.dstep("sync", lambda e, rowbase=rowbase, kt=kt,
                                   abuf=abuf: [e.dma_start(
                                       out=abuf[:, :],
                                       in_=acts.ap()[(rowbase + kt) * 128:
                                                     (rowbase + kt + 1) * 128, :])],
                                   deps=adeps, label=f"{tag}a")
                    first = gk == 0
                    last_ = gk == total - 1

                    def mm(t, first=first, last_=last_, abuf=abuf, buf=buf, i=i):
                        r = None
                        for j in range(4):
                            for th in range(2):
                                r = t.matmul(
                                    PS[:, (j * 2 + th) * 512:(j * 2 + th) * 512 + 512],
                                    buf[:, i * 512 + j * 128:i * 512 + (j + 1) * 128],
                                    abuf[:, th * 512:th * 512 + 512],
                                    start=first, stop=last_)
                        return r
                    mdeps = [wst, ast_] + ([prev_evac] if (gk == 0 and prev_evac)
                                           else [])
                    mm_by_gk[gk] = p.cstep("tensor", mm, deps=mdeps,
                                           label=f"{tag}m")
                    gk += 1
                batch_last_mm.append(mm_by_gk[gk - 1])
                bi += 1
        banks = [(og2 * 4 + j, th,
                  PS[:, (j * 2 + th) * 512:(j * 2 + th) * 512 + 512])
                 for j in range(4) for th in range(2)]
        prev_evac = down_evac(og2, banks, [mm_by_gk[total - 1]])


def emit_idf_ohall(p, IDF, OHALL, IDI):
    p.cstep("gpsimd", lambda g: g.iota(
        IDI[:, :], pattern=[[1, 128]], base=0, channel_multiplier=-1,
        allow_small_or_imprecise_dtypes=True))
    p.cstep("vector", lambda v: v.tensor_scalar(
        IDF[:, :], IDI[:, :], 0.0, None, ALU.is_equal))
    p.cstep("gpsimd", lambda g: g.iota(
        OHALL[0:8, :].rearrange("p (e m) -> p e m", m=128),
        pattern=[[-1, 8], [0, 128]], base=0, channel_multiplier=1,
        allow_small_or_imprecise_dtypes=True))
    p.cstep("vector", lambda v: v.tensor_scalar(
        OHALL[0:8, :], OHALL[0:8, :], 0.0, None, ALU.is_equal))


def _run(nc, in_maps):
    res = run_bass_kernel_spmd(nc, in_maps, core_ids=list(range(NCORES)))
    return res.results


def build_fused():
    p = P()
    x_in = p.din("x", [D, TC])
    out_out = p.dout("out", [D, TC])
    igw_ap = p.din("igw", [16, 128, 128])
    rgw_ap = p.din("rgw", [16, 128, 128])
    igb_ap = p.din("igb", [128, 16])
    rgb_ap = p.din("rgb", [128, 16])
    rp_ap = p.din("rp", [128, 16])
    sel_ap = p.din("sel", [128, 8])
    selp_ap = p.din("selp", [128, 8])
    seln_ap = p.din("seln", [128, 8])
    fm_ap = p.din("first_mask", [128, 1])
    wq_ap = p.din("wq", [D, D])
    wk_ap = p.din("wk", [D, 512])
    wv_ap = p.din("wv", [D, 512])
    wo_ap = p.din("wo", [D, D])
    ln1_ap = p.din("ln1", [128, 16])
    ln2_ap = p.din("ln2", [128, 16])
    qnw_ap = p.din("qnw", [128, 16])
    knw_ap = p.din("knw", [128, 4])
    cos_ap = p.din("cosr", [64, TC])
    sin_ap = p.din("sinsgn", [64, TC])
    swap_ap = p.din("swapm", [64, 64], BF16)
    qpos_ap = p.din("qpos", [1, TC])
    rw0_ap = p.din("rw0", [D, 8])
    rbt_ap = p.din("rb0t", [128, 64])
    wg0_ap = p.din("wg0", [8, D, 1024])
    wu0_ap = p.din("wu0", [8, D, 1024])
    wd0_ap = p.din("wd0", [8, 1024, D])
    cw_ap = p.din("cw", [3, 128, 16])
    cb_ap = p.din("cb", [128, 16])
    WuT_ap = p.din("WuT", [D, D])
    WvT_ap = p.din("WvT", [D, D])
    Wg1T_ap = p.din("Wg1T", [D, 512])
    Wg2T_ap = p.din("Wg2T", [512, D])
    bu_ap = p.din("bu", [128, 16])
    bv_ap = p.din("bv", [128, 16])
    bg1_ap = p.din("bg1", [128, 4])
    bg2_ap = p.din("bg2", [128, 16])
    wr1_ap = p.din("Wr1", [D, 8])
    wg1_ap = p.din("wg1", [8, D, 1408])
    wu1_ap = p.din("wu1", [8, D, 1408])
    wd1_ap = p.din("wd1", [8, 1408, D])
    sg_ap = p.din("Sg", [D, 5632])
    su_ap = p.din("Su", [D, 5632])
    sd_ap = p.din("Sd", [5632, D])
    wsg_ap = p.din("Wsg", [128, 16])

    agg_in = p.dint("agg_in", [128, 32])
    agg_out = p.dint("agg_out", [128 * NCORES, 32])
    kag_in = p.dint("kag_in", [512, TC], BF16)
    kag_out = p.dint("kag_out", [512 * NCORES, TC], BF16)
    vag_in = p.dint("vag_in", [TC, 512], BF16)
    vag_out = p.dint("vag_out", [TC * NCORES, 512], BF16)
    s0_d = p.dint("s0_d", [D, TC], BF16)
    o_d = p.dint("o_d", [D, TC], BF16)
    g0_d = p.dint("g0_d", [D, TC], BF16)
    xl1_d = p.dint("xl1_d", [D, TC])
    tout_d = p.dint("tout_d", [D, TC], BF16)
    s1_d = p.dint("s1_d", [D, TC], BF16)
    hal_in = p.dint("hal_in", [128, 32])
    hal_out = p.dint("hal_out", [128 * NCORES, 32])
    acts0 = p.dint("acts0", [64 * 128, TC], BF16)
    acts1 = p.dint("acts1", [132 * 128, TC], BF16)

    F32A = p.sb("F32A", [128, NCH * TC])
    BF1 = p.sb("BF1", [128, NCH * TC], BF16)
    BF2 = p.sb("BF2", [128, NCH * TC], BF16)
    BF3 = p.sb("BF3", [128, NCH * TC], BF16)
    SMF = p.sb("SMF", [128, 4608])
    SMB = p.sb("SMB", [128, 8448], BF16)
    WBA = p.sb("WBA", [128, 2048], BF16)
    WBB = p.sb("WBB", [128, 2048], BF16)
    WB = (WBA, WBB)
    IDI = p.sb("IDI", [128, 128], mybir.dt.int32)
    TAB = p.sb("TAB", [128, 64])
    ONE1 = p.sb("ONE1", [1, 128])
    ONEBF = p.sb("ONEBF", [128, 1], BF16)
    NEG30 = p.sb("NEG30", [128, 1])
    SEL = p.sb("SEL", [128, 8])
    SELP = p.sb("SELP", [128, 8])
    SELN = p.sb("SELN", [128, 8])
    FMASK = p.sb("FMASK", [128, 1])
    AGG = p.sb("AGG", [128, 32])
    CAR = p.sb("CAR", [128, 16])
    SST = p.sb("SST", [128, 16])
    QNWS = p.sb("QNWS", [128, 16])
    PS = p.ps("PS", [128, 4096])
    MB = SMF[:, 3584:4096]
    RB = SMF[:, 4096:4608]

    X, XNB, ALOC, YLOC = F32A, BF1, BF2, BF3
    SCR = SMF[:, 0:4096]
    ZEROSA = SMB[:, 0:1024]
    AGALL = SMF[:, 0:256]
    p.dstep("sync", lambda e: [
        feat_load(p, e, X, x_in),
        e.dma_start(out=TAB[:, 0:16], in_=igb_ap),
        e.dma_start(out=TAB[:, 16:32], in_=rgb_ap),
        e.dma_start(out=TAB[:, 32:48], in_=rp_ap),
        e.dma_start(out=SEL[:, :], in_=sel_ap),
        e.dma_start(out=SELP[:, :], in_=selp_ap),
        e.dma_start(out=SELN[:, :], in_=seln_ap),
        e.dma_start(out=FMASK[:, :], in_=fm_ap),
        e.dma_start(out=QNWS[:, 0:16], in_=qnw_ap),
    ], label="loadA", n=9)

    def init(v):
        v.memset(ZEROSA[:, :], 0.0)
        v.memset(ONEBF[:, :], 1.0)
        v.memset(NEG30[:, :], -30.0)
        return v.memset(ONE1[:, :], 1.0)
    p.cstep("vector", init)
    emit_softplus8(p, TAB, SCR)
    emit_ln(p, X, XNB, ALOC[:, 0:NCH * 512], YLOC[:, 0:NCH * 512],
            MB, RB, PS[0:1, 3584:4096], PS[0:1, 3072:3584], PS[:, 2048:2560],
            ONEBF, ONE1)
    emit_scan_branch(p, X, XNB, ALOC, YLOC, SCR, ZEROSA, TAB, SEL, FMASK,
                     WBA[:, 0:128], AGG, AGALL, CAR, SST,
                     PS[:, 0:512], PS[:, 512:1024],
                     igw_ap, rgw_ap, agg_in, agg_out)
    p.dstep("sync", lambda e: [feat_store(p, e, s0_d.ap(), YLOC)], label="s0st")

    XN, HB, QB = F32A, BF1, BF2
    SQB = BF3[:, 0:NCH * 512]
    KL = SMB[:, 0:4096]
    VT = SMB[:, 4096:8192]
    SWAPM = SMB[0:64, 8192:8256]
    COSR = SMF[0:64, 0:1024]
    SINSGN = SMF[0:64, 1024:2048]
    RROW = SMF[0:1, 3584:4096]
    p.dstep("sync", lambda e: [
        e.dma_start(out=TAB[:, 0:16], in_=ln1_ap),
        e.dma_start(out=TAB[:, 32:36], in_=knw_ap),
        e.dma_start(out=COSR, in_=cos_ap),
        e.dma_start(out=SINSGN, in_=sin_ap),
        e.dma_start(out=SWAPM, in_=swap_ap),
    ], label="loadB1", n=5)
    p.cstep("vector", lambda v: v.tensor_scalar(
        QNWS[:, :], QNWS[:, :], float(1.0 / np.sqrt(128.0)), None, ALU.mult))
    emit_ln(p, XN, HB, SQB, None, None, RB,
            PS[0:1, 3584:4096], None, PS[:, 3584:4096], ONEBF, ONE1,
            eps=1e-6, gain=TAB[:, 0:16])

    def ev_copy(dst, col_of):
        def f(og, banks, deps):
            def e(s, banks=banks):
                last = None
                for (oc, th, pv) in banks:
                    last = s.copy(dst[:, col_of(oc, th):col_of(oc, th) + 512], pv)
                return last
            return p.cstep("scalar", e, deps=deps, label="pev")
        return f
    emit_proj(p, wq_ap, HB, D, D, WB, PS,
              ev_copy(QB, lambda oc, th: oc * TC + th * 512))
    emit_proj(p, wk_ap, HB, D, 512, WB, PS,
              ev_copy(KL, lambda oc, th: oc * TC + th * 512))
    emit_rownorm_scale(p, QB, QNWS[:, 0:16], 16, PS, SQB, RROW, RB,
                       ONEBF, ONE1, 1e-6, "qn")
    emit_rownorm_scale(p, KL, TAB[:, 32:36], 4, PS, SQB[:, 0:4 * 512], RROW,
                       RB, ONEBF, ONE1, 1e-6, "kn")
    emit_rope(p, QB, 16, COSR, SINSGN, SWAPM, PS, SMF[:, 2048:4096])
    emit_rope(p, KL, 4, COSR, SINSGN, SWAPM, PS, SMF[:, 2048:4096])
    for kt in range(NCH):
        p.dstep("gpsimd", lambda gg, kt=kt: [gg.dma_start(
            out=WBA[:, 0:512], in_=wv_ap[kt * 128:(kt + 1) * 128, :])],
            label=f"vw{kt}")

        def mmv(t, kt=kt):
            last = None
            for tt in range(8):
                last = t.matmul(PS[:, tt * 512:(tt + 1) * 512],
                                HB[:, kt * TC + tt * 128:kt * TC + tt * 128 + 128],
                                WBA[:, 0:512], start=(kt == 0), stop=(kt == NCH - 1))
            return last
        p.cstep("tensor", mmv, label="vmm")

    def vev(s):
        last = None
        for tt in range(8):
            last = s.copy(VT[:, tt * 512:(tt + 1) * 512],
                          PS[:, tt * 512:(tt + 1) * 512])
        return last
    p.cstep("scalar", vev)
    p.dstep("sync", lambda e: [
        e.dma_start(out=kag_in.ap().rearrange("(c p) t -> p c t", p=128),
                    in_=KL[:, :].rearrange("p (c t) -> p c t", t=TC)),
        e.dma_start(out=vag_in.ap().rearrange("(tt p) o -> p tt o", p=128),
                    in_=VT[:, :].rearrange("p (tt o) -> p tt o", o=512)),
    ], label="kv_out", n=2)
    p.xstep("gpsimd", lambda gg: [gg.collective_compute(
        "AllGather", ALU.bypass, replica_groups=[list(range(NCORES))],
        ins=[kag_in.ap().opt()], outs=[kag_out.ap().opt()])], label="ag_k")
    p.xstep("gpsimd", lambda gg: [gg.collective_compute(
        "AllGather", ALU.bypass, replica_groups=[list(range(NCORES))],
        ins=[vag_in.ap().opt()], outs=[vag_out.ap().opt()])], label="ag_v")

    KF2, VF2 = BF1, BF3
    PTT = (SMB[:, 0:1024], SMB[:, 1024:2048], SMB[:, 2048:3072],
           SMB[:, 3072:4096])
    MTT = (SMB[:, 4096:5120], SMB[:, 5120:6144], SMB[:, 6144:7168],
           SMB[:, 7168:8192])
    QPOSB = SMF[:, 0:1024]
    ZB = SMF[:, 1024:1536]
    ZBB = SMF[:, 3136:3648]
    KIDXC = SMF[:, 1536:1600]
    QROW = SMF[0:1, 1600:2624]
    ZROW = SMF[0:1, 2624:3136]
    ZROWB = SMF[0:1, 3648:4160]
    PBC = PS[:, 3072:3584]
    p.dstep("sync", lambda e: [e.dma_start(out=QROW, in_=qpos_ap)],
            label="loadqp")
    p.cstep("gpsimd", lambda gg: gg.iota(
        KIDXC, pattern=[[128, 64]], base=0, channel_multiplier=1,
        allow_small_or_imprecise_dtypes=True))
    for th in range(2):
        p.cstep("tensor", lambda t, th=th: t.matmul(
            PBC, ONE1[0:1, :], QROW[0:1, th * 512:(th + 1) * 512],
            start=True, stop=True))
        p.cstep("vector", lambda v, th=th: v.tensor_copy(
            QPOSB[:, th * 512:(th + 1) * 512], PBC))

    for grp in range(2):
        def ldk(e, grp=grp):
            ins = []
            for chl in range(2):
                ch = 2 * grp + chl
                for r in range(NCORES):
                    ins.append(e.dma_start(
                        out=KF2[:, chl * 8192 + r * TC: chl * 8192 + (r + 1) * TC],
                        in_=kag_out.ap()[r * 512 + ch * 128: r * 512 + (ch + 1) * 128, :]))
            return ins
        p.dstep("sync", ldk, label=f"ldk{grp}", n=16)
        p.dstep("sync", lambda e, grp=grp: [e.dma_start(
            out=VF2[:, 0:64 * 256].rearrange("p (kt o) -> p kt o", o=256),
            in_=vag_out.ap()[:, grp * 256:(grp + 1) * 256]
            .rearrange("(kt p) o -> p kt o", p=128))], label=f"ldv{grp}")
        prev_tail = p.steps[-1]

        class Stream:
            def __init__(s, sid):
                s.sid = sid
                s.slab = PS[:, sid * 1024:(sid + 1) * 1024]
                s.OP = PS[:, 2048 + sid * 512:2048 + (sid + 1) * 512]
                s.ZP = PS[0:1, 3072 + sid * 512:3072 + (sid + 1) * 512]
                s.PT = (SMB[:, sid * 2048:sid * 2048 + 1024],
                        SMB[:, sid * 2048 + 1024:sid * 2048 + 2048])
                s.MT = (SMB[:, 4096 + sid * 2048:4096 + sid * 2048 + 1024],
                        SMB[:, 4096 + sid * 2048 + 1024:4096 + sid * 2048 + 2048])
                s.ZBv = ZB[:, 0:512] if sid == 0 else ZBB[:, 0:512]
                s.ZRv = ZROW if sid == 0 else ZROWB
                s.prev_exp = {0: prev_tail, 1: prev_tail}
                s.prev_zav = {0: prev_tail, 1: prev_tail}
                s.prev_ma = {0: prev_tail, 1: prev_tail}
                s.prev_slabfree = prev_tail
                s.prev_oe = prev_tail

        streams = [Stream(0), Stream(1)]

        def unit_setup(s, h8, qtg):
            s.h = grp * 8 + h8
            s.chl = h8 // 4
            s.qtg = qtg
            s.last_kt = 59 + 4 * qtg
            s.nkt = s.last_kt + 1
            s.qc0 = qtg * 512
            s.nktg = (s.nkt + 1) // 2
            s.exp_steps, s.zav_steps, s.map_steps = {}, {}, {}
            s.pend = {}

        def emit_mm(s, ktg):
            kts = list(range(ktg * 2, min(ktg * 2 + 2, s.nkt)))

            def mmsc(t, kts=kts, slab=s.slab, chl=s.chl, h=s.h, qtg=s.qtg):
                last = None
                for j, kt in enumerate(kts):
                    last = t.matmul(
                        slab[:, j * 512:(j + 1) * 512],
                        KF2[:, chl * 8192 + kt * 128: chl * 8192 + kt * 128 + 128],
                        QB[:, h * TC + qtg * 512: h * TC + qtg * 512 + 512],
                        start=True, stop=True)
                return last
            d = [s.exp_steps[ktg - 1]] if ktg >= 1 else [s.prev_slabfree]
            s.pend[ktg] = p.cstep("tensor", mmsc, deps=d, label="sc")

        def emit_rest(s, ktg):
            sc_step = s.pend.pop(ktg)
            kts = list(range(ktg * 2, min(ktg * 2 + 2, s.nkt)))
            par = ktg % 2
            PT, MT = s.PT[par], s.MT[par]

            def expv(sc, n=len(kts), slab=s.slab, PT=PT):
                return sc.activation(PT[:, 0:n * 512], slab[:, 0:n * 512],
                                     AF.Exp, bias=NEG30[:, 0:1])
            d = [sc_step] + ([s.zav_steps[ktg - 2]] if ktg >= 2
                             else [s.prev_zav[par]])
            s.exp_steps[ktg] = p.cstep("scalar", expv, deps=d, label="exp")

            mkts = [kt for kt in kts if kt >= 4 * s.qtg]
            if mkts:
                def mbuild(v, kts=kts, mkts=mkts, qc0=s.qc0, MT=MT):
                    last = None
                    for j, kt in enumerate(kts):
                        if kt in mkts:
                            last = v.tensor_scalar(
                                MT[:, j * 512:(j + 1) * 512],
                                QPOSB[:, qc0:qc0 + 512],
                                KIDXC[:, kt:kt + 1], None, ALU.is_ge)
                    return last
                d = ([s.map_steps[ktg - 2]] if ktg - 2 in s.map_steps
                     else [s.prev_ma[par]])
                mb_step = p.cstep("vector", mbuild, deps=d, label="mb")

                def mapply(v, kts=kts, mkts=mkts, PT=PT, MT=MT):
                    last = None
                    for j, kt in enumerate(kts):
                        if kt in mkts:
                            last = v.tensor_tensor(
                                PT[:, j * 512:(j + 1) * 512],
                                PT[:, j * 512:(j + 1) * 512],
                                MT[:, j * 512:(j + 1) * 512], ALU.mult)
                    return last
                s.map_steps[ktg] = p.cstep("vector", mapply,
                                           deps=[s.exp_steps[ktg], mb_step],
                                           label="ma")

            def zav(t, kts=kts, ZPv=s.ZP, OPv=s.OP, chl=s.chl,
                    last_kt=s.last_kt, PT=PT):
                last = None
                for j, kt in enumerate(kts):
                    last = t.matmul(ZPv, ONEBF[:, 0:1],
                                    PT[:, j * 512:(j + 1) * 512],
                                    start=(kt == 0), stop=(kt == last_kt))
                for j, kt in enumerate(kts):
                    last = t.matmul(
                        OPv,
                        VF2[:, kt * 256 + chl * 128: kt * 256 + chl * 128 + 128],
                        PT[:, j * 512:(j + 1) * 512],
                        start=(kt == 0), stop=(kt == last_kt))
                return last
            d = ([s.map_steps[ktg]] if ktg in s.map_steps else [s.exp_steps[ktg]])
            if ktg == 0:
                d = d + [s.prev_oe]
            s.zav_steps[ktg] = p.cstep("tensor", zav, deps=d, label="zav")

        def emit_tail(s):
            n = s.nktg
            rc = p.cstep("vector", lambda v, ZRv=s.ZRv, ZPv=s.ZP:
                         v.reciprocal(ZRv, ZPv), deps=[s.zav_steps[n - 1]])
            bc = p.cstep("tensor", lambda t, slab=s.slab, ZRv=s.ZRv: t.matmul(
                slab[:, 0:512], ONE1[0:1, :], ZRv, start=True, stop=True),
                deps=[rc, s.exp_steps[n - 1]])
            cz = p.cstep("vector", lambda v, ZBv=s.ZBv, slab=s.slab:
                         v.tensor_copy(ZBv, slab[:, 0:512]), deps=[bc])
            lp = (n - 1) % 2
            oe = p.cstep("vector", lambda v, PTl=s.PT[lp], OPv=s.OP,
                         ZBv=s.ZBv: v.tensor_tensor(
                             PTl[:, 0:512], OPv, ZBv, ALU.mult), deps=[cz])
            ost = p.dstep("sync", lambda e, h=s.h, qc0=s.qc0,
                          PTl=s.PT[lp]: [e.dma_start(
                              out=o_d.ap()[h * 128:(h + 1) * 128, qc0:qc0 + 512],
                              in_=PTl[:, 0:512])], deps=[oe], label="ost")
            s.prev_exp = {0: prev_tail, 1: prev_tail}
            for k in range(max(0, n - 2), n):
                s.prev_exp[k % 2] = s.exp_steps[k]
            s.prev_zav = {0: prev_tail, 1: prev_tail}
            for k in range(max(0, n - 2), n):
                s.prev_zav[k % 2] = s.zav_steps[k]
            s.prev_zav[lp] = ost
            for mk in sorted(s.map_steps)[-2:]:
                s.prev_ma[mk % 2] = s.map_steps[mk]
            s.prev_slabfree = cz
            s.prev_oe = ost
            return ost

        units = [(h8, qtg) for h8 in range(8) for qtg in range(2)]
        last_osts = []
        for ui in range(0, 16, 2):
            sA, sB = streams
            unit_setup(sA, *units[ui])
            unit_setup(sB, *units[ui + 1])
            kmax = max(sA.nktg, sB.nktg)
            for k in range(kmax + 1):
                for s in (sA, sB):
                    if 1 <= k <= s.nktg:
                        emit_rest(s, k - 1)
                for s in (sA, sB):
                    if k < s.nktg:
                        emit_mm(s, k)
            last_osts = [emit_tail(sA), emit_tail(sB)]
        prev_tail = last_osts[-1]

    OB = BF1
    p.dstep("sync", lambda e: [feat_load(p, e, OB, o_d.ap())], label="ldo")

    def ev_add_xn(og, banks, deps):
        def f(v, banks=banks):
            last = None
            for (oc, th, pv) in banks:
                seg = XN[:, oc * TC + th * 512: oc * TC + th * 512 + 512]
                last = v.tensor_tensor(seg, seg, pv, ALU.add)
            return last
        return p.cstep("vector", f, deps=deps, label="ev_add")
    emit_proj(p, wo_ap, OB, D, D, WB, PS, ev_add_xn)

    X1, TB = F32A, BF1
    GSIL = BF2
    UCB = BF3
    RWB = SMB[:, 0:128]
    ABUFS = [SMB[:, 128 + i * 1024:128 + (i + 1) * 1024] for i in range(4)]
    COMB = SMF[:, 0:64]
    RLOG = SMF[:, 64:128]
    RSIG = SMF[:, 128:192]
    RB0B = SMF[:, 192:256]
    M1 = SMF[:, 256:280]
    IDF = SMF[:, 280:408]
    OHALL = SMF[0:8, 408:1432]
    CT = SMF[0:8, 1432:2456]
    CB = SMF[:, 2456:3480]
    p.dstep("sync", lambda e: [e.dma_start(out=TAB[:, 0:16], in_=ln2_ap)],
            label="loadC")
    emit_idf_ohall(p, IDF, OHALL, IDI)
    emit_ln(p, X1, TB, UCB[:, 0:NCH * 512], None, None, RB,
            PS[0:1, 3584:4096], None, PS[:, 3584:4096], ONEBF, ONE1,
            eps=1e-6, gain=TAB[:, 0:16])
    emit_router(p, TB, rw0_ap, RWB, COMB, CT, RLOG, RSIG, RB0B, IDF, OHALL,
                PS, M1, "l0", bias_tiled_ap=rbt_ap)
    emit_moe_experts(p, TB, wg0_ap, wu0_ap, 8, CT, OHALL, CB,
                     GSIL[:, 0:4096], UCB[:, 0:4096], WB, PS, acts0,
                     lambda e: e * 8, tag="c")

    def devacC(og2, banks, deps):
        def f1(v, banks=banks):
            last = None
            for (oc, th, pv) in banks:
                seg = X1[:, oc * TC + th * 512: oc * TC + th * 512 + 512]
                last = v.tensor_tensor(seg, seg, pv, ALU.add)
            return last
        s1_ = p.cstep("vector", f1, deps=deps)

        def f2(s, banks=banks):
            last = None
            for (oc, th, pv) in banks:
                seg = X1[:, oc * TC + th * 512: oc * TC + th * 512 + 512]
                last = s.activation(TB[:, oc * TC + th * 512:
                                       oc * TC + th * 512 + 512], seg, AF.Silu)
            return last
        return p.cstep("scalar", f2, deps=[s1_])
    emit_moe_down(p, [(wd0_ap[e], 8, e * 8) for e in range(8)], acts0,
                  ABUFS, WB, PS, devacC, tag="cd")
    p.dstep("sync", lambda e: [feat_store(p, e, g0_d.ap(), TB)], label="g0st")

    X = F32A
    S0, G0, XN1B = BF1, BF2, BF3
    SCRD = SMF[:, 0:1024]
    HAL = SMF[:, 1024:1056]
    HALL = SMF[:, 1056:1312]
    LCOL = SMF[:, 1312:1328]
    RCOL = SMF[:, 1328:1344]
    TCV = SMB[:, 0:1024]
    p.dstep("sync", lambda e: [
        feat_load(p, e, X, x_in),
        feat_load(p, e, S0, s0_d.ap()),
        feat_load(p, e, G0, g0_d.ap()),
        e.dma_start(out=TAB[:, 0:48].rearrange("p (k c) -> p k c", c=16),
                    in_=cw_ap[:, :, :].rearrange("k p c -> p k c")),
        e.dma_start(out=TAB[:, 48:64], in_=cb_ap),
    ], label="loadD", n=5)
    for c in range(NCH):
        p.cstep("vector", lambda v, c=c: v.tensor_tensor(
            SCRD[:, :], S0[:, c * TC:(c + 1) * TC], G0[:, c * TC:(c + 1) * TC],
            ALU.mult))
        p.cstep("vector", lambda v, c=c: v.tensor_tensor(
            X[:, c * TC:(c + 1) * TC], X[:, c * TC:(c + 1) * TC], SCRD[:, :],
            ALU.add))
    p.dstep("sync", lambda e: [feat_store(p, e, xl1_d.ap(), X)], label="xl1st")
    emit_ln(p, X, XN1B, S0[:, 0:NCH * 512], S0[:, NCH * 512:NCH * 1024],
            MB, RB, PS[0:1, 3584:4096], PS[0:1, 3072:3584], PS[:, 2048:2560],
            ONEBF, ONE1, eps=1e-5)

    def pack(v):
        last = None
        for c in range(NCH):
            v.tensor_copy(HAL[:, c:c + 1], X[:, c * TC:c * TC + 1])
            last = v.tensor_copy(HAL[:, 16 + c:17 + c],
                                 X[:, c * TC + TC - 1:c * TC + TC])
        return last
    p.cstep("vector", pack)
    p.dstep("gpsimd", lambda g: [g.dma_start(out=hal_in[:, :], in_=HAL)])
    p.xstep("gpsimd", lambda g: [g.collective_compute(
        "AllGather", ALU.bypass, replica_groups=[list(range(NCORES))],
        ins=[hal_in.ap().opt()], outs=[hal_out.ap().opt()])], label="ag_hal")
    p.dstep("gpsimd", lambda g: [g.dma_start(
        out=HALL.rearrange("p (r c) -> p r c", c=32),
        in_=hal_out.ap().rearrange("(r p) c -> p r c", p=128))])

    def zc(v):
        v.memset(LCOL[:, :], 0.0)
        return v.memset(RCOL[:, :], 0.0)
    p.cstep("vector", zc)
    for j in range(NCORES):
        p.cstep("vector", lambda v, j=j: v.scalar_tensor_tensor(
            LCOL[:, :], HALL[:, j * 32 + 16:j * 32 + 32], SELP[:, j:j + 1],
            LCOL[:, :], ALU.mult, ALU.add))
        p.cstep("vector", lambda v, j=j: v.scalar_tensor_tensor(
            RCOL[:, :], HALL[:, j * 32:j * 32 + 16], SELN[:, j:j + 1],
            RCOL[:, :], ALU.mult, ALU.add))
    for c in range(NCH):
        xc = X[:, c * TC:(c + 1) * TC]
        p.cstep("vector", lambda v, c=c, xc=xc: v.tensor_scalar(
            SCRD[:, :], xc, TAB[:, 16 + c:17 + c], TAB[:, 48 + c:49 + c],
            ALU.mult, ALU.add))
        p.cstep("vector", lambda v, c=c, xc=xc: v.scalar_tensor_tensor(
            SCRD[:, 1:TC], xc[:, 0:TC - 1], TAB[:, c:c + 1], SCRD[:, 1:TC],
            ALU.mult, ALU.add))
        p.cstep("vector", lambda v, c=c: v.scalar_tensor_tensor(
            SCRD[:, 0:1], LCOL[:, c:c + 1], TAB[:, c:c + 1], SCRD[:, 0:1],
            ALU.mult, ALU.add))
        p.cstep("vector", lambda v, c=c, xc=xc: v.scalar_tensor_tensor(
            SCRD[:, 0:TC - 1], xc[:, 1:TC], TAB[:, 32 + c:33 + c],
            SCRD[:, 0:TC - 1], ALU.mult, ALU.add))
        p.cstep("vector", lambda v, c=c: v.scalar_tensor_tensor(
            SCRD[:, TC - 1:TC], RCOL[:, c:c + 1], TAB[:, 32 + c:33 + c],
            SCRD[:, TC - 1:TC], ALU.mult, ALU.add))
        p.cstep("vector", lambda v, c=c: v.tensor_copy(TCV, SCRD[:, :]))
        p.dstep("sync", lambda e, c=c: [e.dma_start(
            out=tout_d.ap()[c * 128:(c + 1) * 128, :], in_=TCV)],
            label=f"tst{c}")

    UB, GB = BF1, BF2
    H1B = SMB[:, 0:4096]
    TOUT2 = SMB[:, 4096:8192]
    ZEROSE = SMB[:, 4096:5120]
    AGALLE = SMF[:, 0:256]
    p.dstep("sync", lambda e: [
        e.dma_start(out=TAB[:, 0:16], in_=bu_ap),
        e.dma_start(out=TAB[:, 16:32], in_=bv_ap),
        e.dma_start(out=TAB[:, 32:36], in_=bg1_ap),
        e.dma_start(out=TAB[:, 36:52], in_=bg2_ap),
    ], label="loadE", n=4)
    p.cstep("vector", lambda v: v.memset(ZEROSE[:, :], 0.0))

    def ev_biasE(dst, bias_col_of, act=None):
        def f(og, banks, deps):
            def e(eng, banks=banks):
                last = None
                for (oc, th, pv) in banks:
                    seg = dst[:, oc * TC + th * 512: oc * TC + th * 512 + 512]
                    if act is None:
                        last = eng.tensor_scalar(seg, pv, bias_col_of(oc), None,
                                                 ALU.add)
                    else:
                        last = eng.activation(seg, pv, act, bias=bias_col_of(oc))
                return last
            return p.cstep("vector" if act is None else "scalar", e, deps=deps,
                           label="evb")
        return f
    emit_proj(p, WuT_ap, XN1B, D, D, WB, PS,
              ev_biasE(UB, lambda oc: TAB[:, oc:oc + 1]))
    emit_proj(p, Wg1T_ap, XN1B, D, 512, WB, PS,
              ev_biasE(H1B, lambda oc: TAB[:, 32 + oc:33 + oc], act=AF.Relu))
    emit_proj(p, Wg2T_ap, H1B, 512, D, WB, PS,
              ev_biasE(GB, lambda oc: TAB[:, 36 + oc:37 + oc], act=AF.Sigmoid))

    def ug(v):
        last = None
        for c in range(NCH):
            last = v.tensor_tensor(UB[:, c * TC:(c + 1) * TC],
                                   UB[:, c * TC:(c + 1) * TC],
                                   GB[:, c * TC:(c + 1) * TC], ALU.mult)
        return last
    p.cstep("vector", ug)

    def cum(v):
        last = None
        for c in range(NCH):
            last = v.tensor_tensor_scan(
                UB[:, c * TC:(c + 1) * TC], ZEROSE[:, :],
                UB[:, c * TC:(c + 1) * TC], 0.0, ALU.add, ALU.add)
        return last
    p.cstep("vector", cum)

    def agcp(v):
        last = None
        for c in range(NCH):
            last = v.tensor_copy(AGG[:, c:c + 1],
                                 UB[:, c * TC + TC - 1:c * TC + TC])
        return last
    p.cstep("vector", agcp)
    p.dstep("gpsimd", lambda g: [g.dma_start(out=agg_in.ap()[:, 0:16],
                                             in_=AGG[:, 0:16])])
    p.xstep("gpsimd", lambda g: [g.collective_compute(
        "AllGather", ALU.bypass, replica_groups=[list(range(NCORES))],
        ins=[agg_in.ap().opt()], outs=[agg_out.ap().opt()])], label="ag_cum")
    p.dstep("gpsimd", lambda g: [g.dma_start(
        out=AGALLE.rearrange("p (r c) -> p r c", c=32),
        in_=agg_out.ap().rearrange("(r p) c -> p r c", p=128))])

    def cm(v):
        v.memset(CAR[:, :], 0.0)
        return v.memset(SST[:, :], 0.0)
    p.cstep("vector", cm)
    for j in range(NCORES):
        p.cstep("vector", lambda v, j=j: v.scalar_tensor_tensor(
            CAR[:, :], SST[:, :], SEL[:, j:j + 1], CAR[:, :], ALU.mult, ALU.add))
        p.cstep("vector", lambda v, j=j: v.tensor_tensor(
            SST[:, :], SST[:, :], AGALLE[:, j * 32:j * 32 + 16], ALU.add))

    def ev_vg(og, banks, deps):
        def f(v, banks=banks):
            last = None
            for (oc, th, pv) in banks:
                seg = GB[:, oc * TC + th * 512: oc * TC + th * 512 + 512]
                last = v.scalar_tensor_tensor(seg, pv, TAB[:, 16 + oc:17 + oc],
                                              seg, ALU.add, ALU.mult)
            return last
        return p.cstep("vector", f, deps=deps, label="ev_vg")
    emit_proj(p, WvT_ap, XN1B, D, D, WB, PS, ev_vg)

    def go(v):
        last = None
        for c in range(NCH):
            last = v.scalar_tensor_tensor(
                GB[:, c * TC:(c + 1) * TC], UB[:, c * TC:(c + 1) * TC],
                CAR[:, c:c + 1], GB[:, c * TC:(c + 1) * TC], ALU.add, ALU.mult)
        return last
    p.cstep("vector", go)

    def ev_uv(og, banks, deps):
        ocs = sorted({oc for (oc, th, pv) in banks})
        ld = p.dstep("sync", lambda e, ocs=ocs: [e.dma_start(
            out=TOUT2[:, 0:len(ocs) * TC].rearrange("p (c t) -> p c t", t=TC),
            in_=tout_d.ap()[ocs[0] * 128:(ocs[0] + len(ocs)) * 128, :]
            .rearrange("(c p) t -> p c t", p=128))], deps=deps, label="t2ld")

        def f(v, banks=banks, ocs=ocs):
            last = None
            for (oc, th, pv) in banks:
                lc = oc - ocs[0]
                seg = TOUT2[:, lc * TC + th * 512: lc * TC + th * 512 + 512]
                last = v.tensor_tensor(seg, seg, pv, ALU.add)
            return last
        fs = p.cstep("vector", f, deps=[ld] + deps, label="uv_add")
        return p.dstep("sync", lambda e, ocs=ocs: [e.dma_start(
            out=s1_d.ap()[ocs[0] * 128:(ocs[0] + len(ocs)) * 128, :]
            .rearrange("(c p) t -> p c t", p=128),
            in_=TOUT2[:, 0:len(ocs) * TC].rearrange("p (c t) -> p c t", t=TC))],
            deps=[fs], label="s1st")
    emit_proj(p, WuT_ap, GB, D, D, WB, PS, ev_uv)

    XL1 = F32A
    S1 = BF1
    GSILF = BF2[:, 0:4096]
    UCBF = BF2[:, 4096:8192]
    SGB = SMF[:, 3480:4504]
    SROW = SMF[0:1, 2456:3480]
    WSGB = SMB[:, 4224:4240]
    p.dstep("sync", lambda e: [
        feat_load(p, e, XL1, xl1_d.ap()),
        feat_load(p, e, S1, s1_d.ap()),
    ], label="loadF", n=2)
    p.dstep("gpsimd", lambda g: [g.dma_start(out=WSGB, in_=wsg_ap)],
            label="wsg_dma")
    emit_idf_ohall(p, IDF, OHALL, IDI)
    emit_router(p, XN1B, wr1_ap, RWB, COMB, CT, RLOG, RSIG, RB0B, IDF, OHALL,
                PS, M1, "l1")

    def mmsg(t):
        last = None
        for th in range(2):
            for kt in range(NCH):
                last = t.matmul(PS[0:1, (7 - th) * 512:(7 - th) * 512 + 512],
                                WSGB[:, kt:kt + 1],
                                XN1B[:, kt * TC + th * 512:kt * TC + th * 512 + 512],
                                start=(kt == 0), stop=(kt == NCH - 1))
        return last
    p.cstep("tensor", mmsg)
    p.cstep("scalar", lambda s: s.activation(SROW[0:1, 0:512],
                                             PS[0:1, 3584:4096], AF.Sigmoid))
    p.cstep("scalar", lambda s: s.activation(SROW[0:1, 512:1024],
                                             PS[0:1, 3072:3584], AF.Sigmoid))
    for th in range(2):
        p.cstep("tensor", lambda t, th=th: t.matmul(
            PS[:, 2048:2560], ONE1[0:1, :], SROW[0:1, th * 512:(th + 1) * 512],
            start=True, stop=True))
        p.cstep("vector", lambda v, th=th: v.tensor_copy(
            SGB[:, th * 512:(th + 1) * 512], PS[:, 2048:2560]))
    emit_moe_experts(p, XN1B, wg1_ap, wu1_ap, 11, CT, OHALL, CB,
                     GSILF, UCBF, WB, PS, acts1, lambda e: e * 11, tag="f")
    emit_shared_expert(p, XN1B, sg_ap, su_ap, SGB, GSILF, UCBF, WB, PS,
                       acts1, 88, tag="fs")

    def devacF(og2, banks, deps):
        def f1(s, banks=banks):
            last = None
            for bi, (oc, th, pv) in enumerate(banks):
                last = s.activation(GSILF[:, bi * 512:(bi + 1) * 512], pv, AF.Silu)
            return last
        s1_ = p.cstep("scalar", f1, deps=deps)

        def f2(v, banks=banks):
            last = None
            for bi, (oc, th, pv) in enumerate(banks):
                last = v.tensor_tensor(
                    UCBF[:, bi * 512:(bi + 1) * 512],
                    GSILF[:, bi * 512:(bi + 1) * 512],
                    S1[:, oc * TC + th * 512: oc * TC + th * 512 + 512], ALU.mult)
            return last
        s2_ = p.cstep("vector", f2, deps=[s1_])

        def f3(v, banks=banks):
            last = None
            for bi, (oc, th, pv) in enumerate(banks):
                seg = XL1[:, oc * TC + th * 512: oc * TC + th * 512 + 512]
                last = v.tensor_tensor(seg, seg, UCBF[:, bi * 512:(bi + 1) * 512],
                                       ALU.add)
            return last
        return p.cstep("vector", f3, deps=[s2_])
    emit_moe_down(p, [(wd1_ap[e], 11, e * 11) for e in range(8)]
                  + [(sd_ap, 44, 88)], acts1, ABUFS, WB, PS, devacF,
                  tag="fd")

    p.dstep("sync", lambda e: [feat_store(p, e, out_out, XL1)], label="outst")
    nc = p.build()
    return p, nc


def kernel(**inputs):
    g, pc = _prep_host(inputs)
    p, nc = build_fused()
    in_maps = []
    for i in range(NCORES):
        m = dict(x=pc[i]["x"], igw=g["igw"], rgw=g["rgw"], igb=g["igb"],
                 rgb=g["rgb"], rp=g["rp"], sel=pc[i]["sel"], selp=pc[i]["selp"],
                 seln=pc[i]["seln"], first_mask=pc[i]["first_mask"],
                 wq=g["wq"], wk=g["wk"], wv=g["wv"], wo=g["wo"],
                 ln1=g["ln1"], ln2=g["ln2"], qnw=g["qnw"], knw=g["knw"],
                 cosr=pc[i]["cosr"], sinsgn=pc[i]["sinsgn"], swapm=g["swapm"],
                 qpos=pc[i]["qpos"], rw0=g["rw0"], rb0t=g["rb0t"],
                 wg0=g["wg0"], wu0=g["wu0"], wd0=g["wd0"],
                 cw=g["cw"], cb=g["cb"], WuT=g["WuT"], WvT=g["WvT"],
                 Wg1T=g["Wg1T"], Wg2T=g["Wg2T"], bu=g["bu"], bv=g["bv"],
                 bg1=g["bg1"], bg2=g["bg2"], Wr1=g["Wr1"],
                 wg1=g["wg1"], wu1=g["wu1"], wd1=g["wd1"],
                 Sg=g["Sg"], Su=g["Su"], Sd=g["Sd"], Wsg=g["Wsg"])
        in_maps.append(m)
    res = run_bass_kernel_spmd(nc, in_maps, core_ids=list(range(NCORES)))
    p.close()
    out = np.concatenate([np.asarray(res.results[i]["out"], np.float32)
                          for i in range(NCORES)], axis=1)
    return np.ascontiguousarray(out.T)[None]
